# version 16
# baseline (speedup 1.0000x reference)
import numpy as np
from contextlib import ExitStack

import concourse.bass as bass
import concourse.mybir as mybir
from concourse.bass_utils import run_bass_kernel_spmd

F32 = mybir.dt.float32
BF16 = mybir.dt.bfloat16
I32 = mybir.dt.int32
AF = mybir.ActivationFunctionType
ALU = mybir.AluOpType
AX = mybir.AxisListType

T = 4096
D = 2048
NH = 16
FF = 8192
NT = T // 128
ND = D // 128
NI = FF // 128
EPS = 1e-5
QSCALE = float(1.0 / np.sqrt(128.0))

S_BANKS = 5
S_CAP = S_BANKS * 512


class SemC:

    def __init__(self, nc, ctx, name):
        self.h = ctx.enter_context(nc.semaphore(name))
        self.v = 0

    def tag(self, ins, n=1):
        ins.then_inc(self.h, n)
        self.v += n
        return self.v


def build(nphases=99, dbg=False):
    nc = bass.Bass("TRN2", target_bir_lowering=False)

    x_h = nc.dram_tensor("x", [T, D], F32, kind="ExternalInput").ap()
    wq_h = nc.dram_tensor("wq", [D, D], F32, kind="ExternalInput").ap()
    wk_h = nc.dram_tensor("wk", [D, D], F32, kind="ExternalInput").ap()
    wv_h = nc.dram_tensor("wv", [D, D], F32, kind="ExternalInput").ap()
    wo_h = nc.dram_tensor("wo", [D, D], F32, kind="ExternalInput").ap()
    bq_h = nc.dram_tensor("bq", [D], F32, kind="ExternalInput").ap()
    bv_h = nc.dram_tensor("bv", [D], F32, kind="ExternalInput").ap()
    bo_h = nc.dram_tensor("bo", [D], F32, kind="ExternalInput").ap()
    wgT_h = nc.dram_tensor("wgT", [D, FF], I32, kind="ExternalInput").ap()
    wuT_h = nc.dram_tensor("wuT", [D, FF], I32, kind="ExternalInput").ap()
    wdT_h = nc.dram_tensor("wdT", [FF, D], I32, kind="ExternalInput").ap()
    gs_h = nc.dram_tensor("gs", [FF], F32, kind="ExternalInput").ap()
    us_h = nc.dram_tensor("us", [FF], F32, kind="ExternalInput").ap()
    ds_h = nc.dram_tensor("ds", [D], F32, kind="ExternalInput").ap()

    out_h = nc.dram_tensor("out", [T, D], F32, kind="ExternalOutput").ap()

    sk = "ExternalOutput" if dbg else "Internal"
    q_h = nc.dram_tensor("q_s", [ND, 128, T], BF16, kind=sk).ap()
    k_h = nc.dram_tensor("k_s", [ND, 128, T], BF16, kind=sk).ap()
    v2_h = nc.dram_tensor("v_s", [NT, 128, D], BF16, kind=sk).ap()
    o_h = nc.dram_tensor("o_s", [ND, 128, T], BF16, kind=sk).ap()
    x1_h = nc.dram_tensor("x1_s", [NT, 128, D], F32, kind=sk).ap()
    h_h = nc.dram_tensor("h_s", [NI, 128, T], BF16, kind=sk).ap()
    wdb_h = nc.dram_tensor("wdb_s", [NI, 128, D], BF16, kind=sk).ap()
    xfm_h = (nc.dram_tensor("xfm_s", [ND, 128, T], BF16, kind="ExternalOutput").ap()
             if dbg else None)

    with ExitStack() as ctx:
        E = ctx.enter_context
        block = E(nc.Block())

        class _ScH:
            __slots__ = ("incs",)

            def __init__(self):
                self.incs = []

            def then_inc(self, sem, n):
                self.incs.append((sem, n))

        class _ScalarRec:
            def __init__(self):
                self.cmds = []

            def wait_ge(self, sem, v):
                self.cmds.append(("w", sem, v))

            def _op(self, name, a, k):
                h = _ScH()
                self.cmds.append(("o", name, a, k, h))
                return h

            def activation(self, *a, **k):
                return self._op("activation", a, k)

            def copy(self, *a, **k):
                return self._op("copy", a, k)

            def flush(self):
                if not self.cmds:
                    return
                cmds, self.cmds = self.cmds, []

                @block.scalar
                def _(s):
                    for c in cmds:
                        if c[0] == "w":
                            s.wait_ge(c[1], c[2])
                        else:
                            ins = getattr(s, c[1])(*c[2], **c[3])
                            for sem, n in c[4].incs:
                                ins.then_inc(sem, n)

        _scr = _ScalarRec()

        def scalar_section(fn):
            fn(_scr)
            if len(_scr.cmds) >= 100:
                _scr.flush()

        ident_b = E(nc.sbuf_tensor("ident_b", [128, 128], BF16))
        ident_f = E(nc.sbuf_tensor("ident_f", [128, 128], F32))
        mask_T = E(nc.sbuf_tensor("mask_T", [128, 128], F32))
        ones1 = E(nc.sbuf_tensor("ones1", [128, 128], F32))
        ones_bb = E(nc.sbuf_tensor("ones_bb", [128, 128], BF16))
        eps_t = E(nc.sbuf_tensor("eps_t", [128, 1], F32))
        zero_t = E(nc.sbuf_tensor("zero_t", [128, 1], F32))
        iota_i = E(nc.sbuf_tensor("iota_i", [128, 128], I32))
        iota_j = E(nc.sbuf_tensor("iota_j", [128, 128], I32))
        xfm_cm = nc.sbuf_tensor("xfm", [128, ND * T], BF16)
        xfm = xfm_cm.__enter__()

        sv = SemC(nc, ctx, "sv")
        ss = SemC(nc, ctx, "ss")
        stn = SemC(nc, ctx, "stn")
        sg = SemC(nc, ctx, "sg")
        sq = SemC(nc, ctx, "sq")

        @block.gpsimd
        def _(g):
            g.iota(iota_j[:], pattern=[[1, 128]], base=0, channel_multiplier=0)
            ins = g.iota(iota_i[:], pattern=[[0, 128]], base=0, channel_multiplier=1)
            sg.tag(ins)

        @block.vector
        def _(v):
            v.memset(eps_t[:], EPS)
            v.memset(zero_t[:], 0.0)
            v.memset(ones1[:], 1.0)
            v.memset(ones_bb[:], 1.0)
            v.wait_ge(sg.h, sg.v)
            v.tensor_tensor(out=ident_b[:], in0=iota_j[:], in1=iota_i[:], op=ALU.is_equal)
            v.tensor_tensor(out=ident_f[:], in0=iota_j[:], in1=iota_i[:], op=ALU.is_equal)
            ins = v.tensor_tensor(out=mask_T[:], in0=iota_i[:], in1=iota_j[:], op=ALU.is_gt)
            sv.tag(ins)
            v.wait_ge(sv.h, sv.v)
            ins = v.tensor_scalar(out=mask_T[:], in0=mask_T[:], scalar1=-1e10, scalar2=None,
                                  op0=ALU.mult, op1=ALU.bypass)
            sv.tag(ins)

        const_ready = sv.v

        def emit_ln_transpose(tag, src_h, src_is_3d, dst, gate=None):
            with ExitStack() as px:
                xe = px.enter_context
                xt = [xe(nc.sbuf_tensor(f"{tag}x{i}", [128, D], F32)) for i in range(2)]
                stats = [xe(nc.sbuf_tensor(f"{tag}st{i}", [128, 24], F32)) for i in range(2)]
                mv = [xe(nc.sbuf_tensor(f"{tag}mv{i}", [128, 2], F32)) for i in range(2)]
                rstd = [xe(nc.sbuf_tensor(f"{tag}rs{i}", [128, 1], F32)) for i in range(2)]
                nmr = [xe(nc.sbuf_tensor(f"{tag}nm{i}", [128, 1], F32)) for i in range(2)]
                lt = [xe(nc.sbuf_tensor(f"{tag}lt{i}", [128, 1], F32)) for i in range(2)]
                xtm = xe(nc.sbuf_tensor(f"{tag}tm", [128, D], BF16))
                pst = [xe(nc.psum_tensor(f"{tag}ps{i}", [128, 1024], BF16)) for i in range(2)]

                ld = [SemC(nc, ctx, f"{tag}ld{i}") for i in range(2)]
                fr = [SemC(nc, ctx, f"{tag}fr{i}") for i in range(2)]
                ptfr = [SemC(nc, ctx, f"{tag}pf{i}") for i in range(2)]
                ptrd = [SemC(nc, ctx, f"{tag}pr{i}") for i in range(2)]
                mvv = {}
                tile_marks = {}

                def emit_load(i):
                    b = i % 2

                    @block.sync
                    def _(q, i=i, b=b):
                        if i == 0 and gate is not None:
                            gate(q)
                        if i >= 2:
                            q.wait_ge(fr[b].h, i // 2)
                        ins = q.dma_start(out=xt[b][:], in_=(src_h[i] if src_is_3d
                                          else src_h[i * 128:(i + 1) * 128, :]))
                        ld[b].tag(ins, 16)

                def emit_front(i):
                    b = i % 2

                    @block.vector
                    def _(v, i=i, b=b):
                        v.wait_ge(ld[b].h, 16 * (i // 2 + 1))
                        for c in range(4):
                            ins = v.bn_stats(out=stats[b][:, c * 6:(c + 1) * 6],
                                             in_=xt[b][:, c * 512:(c + 1) * 512])
                        sv.tag(ins)
                        v.wait_ge(sv.h, sv.v)
                        ins = v.bn_aggr(out=mv[b][:], in_=stats[b][:])
                        sv.tag(ins)
                    mvv[i] = sv.v

                def emit_tail(i):
                    b = i % 2

                    @scalar_section
                    def _(s, i=i, b=b, mvr=mvv[i]):
                        s.wait_ge(sv.h, mvr)
                        ins = s.activation(out=lt[b][:], in_=mv[b][:, 1:2], func=AF.Ln,
                                           bias=eps_t[:], scale=1.0)
                        ss.tag(ins)
                        s.wait_ge(ss.h, ss.v)
                        ins = s.activation(out=rstd[b][:], in_=lt[b][:], func=AF.Exp,
                                           bias=zero_t[:], scale=-0.5)
                        ss.tag(ins)

                    rstd_ready = ss.v

                    @block.vector
                    def _(v, b=b, rstd_ready=rstd_ready):
                        v.wait_ge(ss.h, rstd_ready)
                        ins = v.scalar_tensor_tensor(out=nmr[b][:], in0=mv[b][:, 0:1],
                                                     scalar=-1.0, in1=rstd[b][:],
                                                     op0=ALU.mult, op1=ALU.mult)
                        sv.tag(ins)

                    nmr_ready = sv.v
                    pt0, pt1 = ptrd[0].v, ptrd[1].v

                    @scalar_section
                    def _(s, i=i, b=b, nmr_ready=nmr_ready, pt0=pt0, pt1=pt1):
                        s.wait_ge(sv.h, nmr_ready)
                        if i >= 1:
                            s.wait_ge(ptrd[0].h, pt0)
                            s.wait_ge(ptrd[1].h, pt1)
                        ins = s.activation(out=xtm[:], in_=xt[b][:], func=AF.Identity,
                                           bias=nmr[b][:], scale=rstd[b][:])
                        fr[b].tag(ins, 1)

                    for gidx in range(4):
                        gi = i * 4 + gidx
                        pb = gi % 2

                        @block.tensor
                        def _(t, i=i, b=b, gidx=gidx, gi=gi, pb=pb):
                            if gidx == 0:
                                t.wait_ge(fr[b].h, i // 2 + 1)
                                if i == 0:
                                    t.wait_ge(sv.h, const_ready)
                            if gi >= 2:
                                t.wait_ge(ptfr[pb].h, gi // 2)
                            for u in range(4):
                                dc = gidx * 4 + u
                                ins = t.transpose(pst[pb][:, u * 128:(u + 1) * 128],
                                                  xtm[:, dc * 128:(dc + 1) * 128], ident_b[:])
                            ptrd[pb].tag(ins, 1)

                        @scalar_section
                        def _(s, i=i, gidx=gidx, pb=pb, want=ptrd[pb].v):
                            s.wait_ge(ptrd[pb].h, want)
                            dstv = dst[:].rearrange("p (c t) -> p c t", t=T)[
                                :, gidx * 4:(gidx + 1) * 4, i * 128:(i + 1) * 128]
                            ins = s.copy(out=dstv, in_=pst[pb][:, 0:512])
                            ptfr[pb].tag(ins, 1)
                    tile_marks[i] = (ptfr[0].v, ptfr[1].v)

                emit_load(0)
                emit_load(1)
                for i in range(NT):
                    if i + 2 < NT:
                        emit_load(i + 2)
                    emit_front(i)
                    if i >= 1:
                        emit_tail(i - 1)
                emit_tail(NT - 1)

                return ptfr[0].v, ptfr[1].v, ptfr, tile_marks

        p0f0, p0f1, p0_ptfr, p0_marks = emit_ln_transpose("p0", x_h, False, xfm)

        def wait_phase0_done(eng):
            eng.wait_ge(p0_ptfr[0].h, p0f0)
            eng.wait_ge(p0_ptfr[1].h, p0f1)

        if dbg:
            @block.sync
            def _(q):
                wait_phase0_done(q)
                for dc in range(ND):
                    ins = q.dma_start(out=xfm_h[dc], in_=xfm[:, dc * T:(dc + 1) * T])
                    sq.tag(ins, 16)
        dbg_p0_dump = sq.v

        if nphases >= 2:
            with ExitStack() as px:
                xe = px.enter_context
                wst = [xe(nc.sbuf_tensor(f"p1w{i}", [128, D], F32)) for i in range(2)]
                wbt = [xe(nc.sbuf_tensor(f"p1b{i}", [128, D], BF16)) for i in range(2)]
                row = [xe(nc.sbuf_tensor(f"p1r{i}", [128, T], BF16)) for i in range(2)]
                bq_all = xe(nc.sbuf_tensor("p1bq", [128, ND], F32))
                psm = [xe(nc.psum_tensor(f"p1ps{i}", [128, 512], F32)) for i in range(4)]

                wld = [SemC(nc, ctx, f"p1wl{i}") for i in range(2)]
                bql = SemC(nc, ctx, "p1bql")
                rfr = [SemC(nc, ctx, f"p1rf{i}") for i in range(2)]
                pse = [SemC(nc, ctx, f"p1pe{i}") for i in range(2)]
                psw = [SemC(nc, ctx, f"p1pw{i}") for i in range(2)]

                @block.sync
                def _(q):
                    wait_phase0_done(q)
                    for oc in range(ND):
                        ins = q.dma_start(
                            out=bq_all[:, oc:oc + 1],
                            in_=bq_h[oc * 128:(oc + 1) * 128].rearrange("(p u) -> p u", u=1))
                        bql.tag(ins, 16)

                bq_ready = bql.v
                cast_hist = {}
                mmdone_hist = {}

                projs = [("q", wq_h, q_h), ("k", wk_h, k_h)]
                for pi, (pname, w_hbm, dst_hbm) in enumerate(projs):
                    for oc in range(ND):
                        it = pi * ND + oc
                        b = it % 2

                        @block.sync
                        def _(q, oc=oc, it=it, b=b, w_hbm=w_hbm):
                            if it >= 2:
                                q.wait_ge(sv.h, cast_hist[it - 2])
                            src = w_hbm[:, oc * 128:(oc + 1) * 128].rearrange(
                                "(c p) o -> p c o", p=128)
                            ins = q.dma_start(out=wst[b][:].rearrange(
                                "p (c o) -> p c o", o=128), in_=src)
                            wld[b].tag(ins, 16)

                        @block.vector
                        def _(v, it=it, b=b):
                            v.wait_ge(wld[b].h, 16 * (it // 2 + 1))
                            if it >= 2:
                                v.wait_ge(psw[0].h, mmdone_hist[it - 2])
                            ins = v.tensor_copy(out=wbt[b][:], in_=wst[b][:])
                            sv.tag(ins)

                        cast_hist[it] = sv.v
                        ev_hist = []

                        for gp in range(4):
                            g0p = gp * 2
                            pp = gp % 2

                            @block.tensor
                            def _(t, it=it, b=b, gp=gp, g0p=g0p, pp=pp,
                                  cast_val=cast_hist[it],
                                  evw=pse[pp].v):
                                if gp == 0:
                                    t.wait_ge(sv.h, cast_val)
                                if it == 0:
                                    t.wait_ge(p0_ptfr[0].h, p0_marks[8 * gp + 7][0])
                                    t.wait_ge(p0_ptfr[1].h, p0_marks[8 * gp + 7][1])
                                if evw:
                                    t.wait_ge(pse[pp].h, evw)
                                for dc in range(ND):
                                    for pb2 in range(2):
                                        g = g0p + pb2
                                        ins = t.matmul(
                                            psm[2 * pp + pb2][:],
                                            lhsT=wbt[b][:, dc * 128:(dc + 1) * 128],
                                            rhs=xfm[:, dc * T + g * 512:
                                                    dc * T + (g + 1) * 512],
                                            start=(dc == 0), stop=(dc == ND - 1))
                                psw[0].tag(ins, 1)

                            rb = it % 2
                            for pb2 in range(2):
                                g = g0p + pb2
                                if pi == 0:
                                    @block.vector
                                    def _(v, it=it, oc=oc, g=g, pb2=pb2, pp=pp,
                                          rb=rb, mmv=psw[0].v):
                                        v.wait_ge(psw[0].h, mmv)
                                        if g == 0:
                                            v.wait_ge(bql.h, bq_ready)
                                            if it >= 2:
                                                v.wait_ge(rfr[rb].h, 16 * (it // 2))
                                        ins = v.tensor_scalar(
                                            out=row[rb][:, g * 512:(g + 1) * 512],
                                            in0=psm[2 * pp + pb2][:],
                                            scalar1=bq_all[:, oc:oc + 1],
                                            scalar2=QSCALE, op0=ALU.add, op1=ALU.mult)
                                        pse[pp].tag(ins, 1)
                                    ev_hist.append((pse[pp], pse[pp].v))
                                else:
                                    @scalar_section
                                    def _(s, it=it, g=g, pb2=pb2, pp=pp, rb=rb,
                                          mmv=psw[0].v):
                                        s.wait_ge(psw[0].h, mmv)
                                        if g == 0 and it >= 2:
                                            s.wait_ge(rfr[rb].h, 16 * (it // 2))
                                        ins = s.copy(out=row[rb][:, g * 512:(g + 1) * 512],
                                                     in_=psm[2 * pp + pb2][:])
                                        pse[pp].tag(ins, 1)
                                    ev_hist.append((pse[pp], pse[pp].v))

                        mmdone_hist[it] = psw[0].v
                        e0 = max([v for (s_, v) in ev_hist if s_ is pse[0]], default=0)
                        e1 = max([v for (s_, v) in ev_hist if s_ is pse[1]], default=0)

                        @block.sync
                        def _(q, oc=oc, it=it, dst_hbm=dst_hbm, e0=e0, e1=e1):
                            if e0:
                                q.wait_ge(pse[0].h, e0)
                            if e1:
                                q.wait_ge(pse[1].h, e1)
                            ins = q.dma_start(out=dst_hbm[oc], in_=row[it % 2][:])
                            rfr[it % 2].tag(ins, 16)

                p1r0, p1r1 = rfr[0].v, rfr[1].v

            with ExitStack() as px:
                xe = px.enter_context
                wvst = [xe(nc.sbuf_tensor(f"v1w{i}", [128, 1024], F32)) for i in range(2)]
                wvb = xe(nc.sbuf_tensor("v1wb", [128, ND * 1024], BF16))
                vrow = [xe(nc.sbuf_tensor(f"v1r{i}", [128, 1024], BF16)) for i in range(2)]
                ps_v = [xe(nc.psum_tensor(f"v1p{i}", [128, 1024], F32)) for i in range(2)]
                vwl = [SemC(nc, ctx, f"v1wl{i}") for i in range(2)]
                vst = [SemC(nc, ctx, f"v1st{i}") for i in range(2)]
                vcast_hist = {}
                vevac_hist = {}
                vmm_half_end = {}

                for half in range(2):
                    f0h = half * 1024
                    for dc in range(ND):
                        wit = half * ND + dc
                        wb = wit % 2

                        @block.sync
                        def _(q, dc=dc, wb=wb, wit=wit, f0h=f0h,
                              prev=vcast_hist.get(wit - 2, 0)):
                            if wit < 2:
                                q.wait_ge(rfr[0].h, p1r0)
                                q.wait_ge(rfr[1].h, p1r1)
                            if prev:
                                q.wait_ge(ss.h, prev)
                            ins = q.dma_start(
                                out=wvst[wb][:],
                                in_=wv_h[dc * 128:(dc + 1) * 128, f0h:f0h + 1024])
                            vwl[wb].tag(ins, 16)

                        @scalar_section
                        def _(s, dc=dc, wb=wb, wit=wit, half=half,
                              mmprev=vmm_half_end.get(half - 1, 0)):
                            s.wait_ge(vwl[wb].h, 16 * (wit // 2 + 1))
                            if half == 1 and dc == 0:
                                s.wait_ge(stn.h, mmprev)
                            ins = s.copy(out=wvb[:, dc * 1024:(dc + 1) * 1024],
                                         in_=wvst[wb][:])
                            ss.tag(ins)

                        vcast_hist[wit] = ss.v

                    cast_done_h = ss.v
                    for i in range(NT):
                        vt = half * NT + i
                        sl = vt % 2

                        @block.tensor
                        def _(t, i=i, sl=sl, cast_done_h=cast_done_h,
                              ev=vevac_hist.get(vt - 2, 0)):
                            if i == 0:
                                t.wait_ge(ss.h, cast_done_h)
                            if ev:
                                t.wait_ge(ss.h, ev)
                            for dc in range(ND):
                                for fg in range(2):
                                    ins = t.matmul(
                                        ps_v[sl][:, fg * 512:(fg + 1) * 512],
                                        lhsT=xfm[:, dc * T + i * 128:dc * T + (i + 1) * 128],
                                        rhs=wvb[:, dc * 1024 + fg * 512:
                                                dc * 1024 + (fg + 1) * 512],
                                        start=(dc == 0), stop=(dc == ND - 1))
                            stn.tag(ins, 1)

                        vmm = stn.v

                        @scalar_section
                        def _(s, sl=sl, vt=vt, vmm=vmm, stprev=vst[sl].v):
                            s.wait_ge(stn.h, vmm)
                            if vt >= 2:
                                s.wait_ge(vst[sl].h, stprev)
                            ins = s.copy(out=vrow[sl][:], in_=ps_v[sl][:])
                            ss.tag(ins)

                        vevac_hist[vt] = ss.v

                        @block.sync
                        def _(q, i=i, sl=sl, half=half, ev=ss.v):
                            q.wait_ge(ss.h, ev)
                            ins = q.dma_start(
                                out=v2_h[i][:, half * 1024:(half + 1) * 1024],
                                in_=vrow[sl][:])
                            vst[sl].tag(ins, 16)

                    vmm_half_end[half] = stn.v

                p1v0, p1v1 = vst[0].v, vst[1].v

            def wait_p1_done(eng):
                eng.wait_ge(rfr[0].h, p1r0)
                eng.wait_ge(rfr[1].h, p1r1)
                eng.wait_ge(vst[0].h, p1v0)
                eng.wait_ge(vst[1].h, p1v1)
                if dbg and dbg_p0_dump:
                    eng.wait_ge(sq.h, dbg_p0_dump)
        else:
            def wait_p1_done(eng):
                wait_phase0_done(eng)

        xfm_cm.__exit__(None, None, None)

        if nphases >= 3:
            with ExitStack() as px:
                xe = px.enter_context
                kr = [xe(nc.sbuf_tensor(f"p2k{i}", [128, T], BF16)) for i in range(4)]
                qr = [xe(nc.sbuf_tensor(f"p2q{i}", [128, T], BF16)) for i in range(4)]
                vtm = [xe(nc.sbuf_tensor(f"p2v{i}", [128, T], BF16)) for i in range(4)]
                orow = [xe(nc.sbuf_tensor(f"p2o{i}", [128, T], BF16)) for i in range(4)]
                pT = [xe(nc.sbuf_tensor(f"p2p{i}", [128, 512], BF16)) for i in range(4)]
                lrb = [xe(nc.sbuf_tensor(f"p2lr{i}", [128, 512], F32)) for i in range(4)]
                lsb = [xe(nc.sbuf_tensor(f"p2ls{i}", [128, 512], F32)) for i in range(2)]
                qsq = xe(nc.sbuf_tensor("p2sq", [128, T], BF16))
                qn = xe(nc.sbuf_tensor("p2qn", [128, 1], F32))
                kn = xe(nc.sbuf_tensor("p2kn", [128, 1], F32))
                gt_t = xe(nc.sbuf_tensor("p2gt", [128, 1], F32))
                mhb = xe(nc.sbuf_tensor("p2mh", [128, NH], F32))
                wcv = [xe(nc.sbuf_tensor(f"p2wc{i}", [128, D], I32)) for i in range(2)]
                wcb = [xe(nc.sbuf_tensor(f"p2wb{i}", [128, D], BF16)) for i in range(2)]

                ps_s = [xe(nc.psum_tensor(f"p2ss{i}", [128, 512], F32)) for i in range(3)]
                ps_o = [xe(nc.psum_tensor(f"p2po{i}", [128, 512], F32)) for i in range(2)]
                ps_l = [xe(nc.psum_tensor(f"p2pl{i}", [128, 512], F32)) for i in range(2)]
                ps_t = xe(nc.psum_tensor("p2pt", [128, 512], F32))

                rowld = [SemC(nc, ctx, f"p2rl{i}") for i in range(4)]
                ost = [SemC(nc, ctx, f"p2os{i}") for i in range(4)]
                wcl = [SemC(nc, ctx, f"p2wl{i}") for i in range(2)]
                wcs = [SemC(nc, ctx, f"p2ws{i}") for i in range(2)]

                head_te_end = {}
                wconv_hist = {}
                exp_hist = {}
                lmm_hist = {}
                last_oevac = {0: 0, 1: 0}
                last_lcopy = {0: 0, 1: 0}
                last_lrecip = {0: 0, 1: 0}
                last_norm = {}
                mhb_marker = {}
                tap_end = {}
                tap_red = [0]
                tap_tasks = []
                jg = [0]

                def bound_tasks(hh):
                    sl2 = hh % 4
                    ldv = 96 * (hh // 4 + 1)
                    st = {"sq_v": 0, "lastmm": 0}

                    def t_sq(which):
                        def run():
                            src = qr if which == 0 else kr

                            @block.vector
                            def _(v, which=which, sl2=sl2, ldv=ldv, src=src,
                                  tprev=(tap_end.get(hh - 1, 0) if hh else 0),
                                  mprev=(mhb_marker.get(hh - 1, 0)
                                         if which == 0 else 0),
                                  lastmm=st["lastmm"]):
                                v.wait_ge(rowld[sl2].h, ldv)
                                if which == 0 and tprev:
                                    v.wait_ge(stn.h, tprev)
                                if mprev:
                                    v.wait_ge(sv.h, mprev)
                                if which == 1:
                                    v.wait_ge(stn.h, lastmm)
                                ins = v.tensor_tensor(out=qsq[:], in0=src[sl2][:],
                                                      in1=src[sl2][:], op=ALU.mult)
                                sv.tag(ins)

                            st["sq_v"] = sv.v
                        return run

                    def t_mm(which, g8):
                        def run():
                            dstn = qn if which == 0 else kn

                            @block.tensor
                            def _(t, g8=g8, sq_v=st["sq_v"], hh=hh,
                                  which=which, warv=tap_red[0]):
                                if g8 == 0:
                                    t.wait_ge(sv.h, sq_v)
                                    if hh == 0 and which == 0:
                                        t.wait_ge(sv.h, const_ready)
                                if warv:
                                    t.wait_ge(sv.h, warv)
                                ins = t.matmul(ps_t[:], lhsT=ones_bb[:],
                                               rhs=qsq[:, g8 * 512:(g8 + 1) * 512],
                                               start=True, stop=True)
                                stn.tag(ins, 1)

                            mm_v = stn.v
                            st["lastmm"] = mm_v

                            @block.vector
                            def _(v, g8=g8, mm_v=mm_v, dstn=dstn,
                                  gtprev=tap_red[0]):
                                v.wait_ge(stn.h, mm_v)
                                if g8 == 0:
                                    ins = v.tensor_reduce(out=dstn[:], in_=ps_t[:],
                                                          axis=AX.X, op=ALU.max)
                                    sv.tag(ins)
                                else:
                                    if gtprev:
                                        v.wait_ge(sv.h, gtprev)
                                    ins = v.tensor_reduce(out=gt_t[:], in_=ps_t[:],
                                                          axis=AX.X, op=ALU.max)
                                    sv.tag(ins)
                                    v.wait_ge(sv.h, sv.v)
                                    ins = v.tensor_tensor(out=dstn[:], in0=dstn[:],
                                                          in1=gt_t[:], op=ALU.max)
                                    sv.tag(ins)

                            tap_red[0] = sv.v
                        return run

                    def t_fin():
                        def run():
                            @block.vector
                            def _(v, lastred=sv.v):
                                v.wait_ge(sv.h, lastred)
                                ins = v.tensor_tensor(out=qn[:], in0=qn[:], in1=kn[:],
                                                      op=ALU.mult)
                                sv.tag(ins)

                            m2_v = sv.v

                            @scalar_section
                            def _(s, m2_v=m2_v):
                                s.wait_ge(sv.h, m2_v)
                                ins = s.activation(out=gt_t[:], in_=qn[:], func=AF.Ln,
                                                   bias=zero_t[:], scale=1.0)
                                ss.tag(ins)
                                s.wait_ge(ss.h, ss.v)
                                ins = s.activation(out=gt_t[:], in_=gt_t[:],
                                                   func=AF.Exp, bias=zero_t[:],
                                                   scale=0.5)
                                ss.tag(ins)

                            sqrt_v = ss.v

                            @block.vector
                            def _(v, hh=hh, sqrt_v=sqrt_v):
                                v.wait_ge(ss.h, sqrt_v)
                                ins = v.tensor_scalar(out=mhb[:, hh:hh + 1],
                                                      in0=gt_t[:],
                                                      scalar1=-1.01, scalar2=None,
                                                      op0=ALU.mult, op1=ALU.bypass)
                                sv.tag(ins)

                            mhb_marker[hh] = sv.v
                            tap_end[hh] = stn.v
                        return run

                    seq = [t_sq(0)]
                    seq += [t_mm(0, g8) for g8 in range(8)]
                    seq += [t_sq(1)]
                    seq += [t_mm(1, g8) for g8 in range(8)]
                    seq += [t_fin()]
                    return seq

                def emit_loads(hh):
                    sl = hh % 4

                    @block.sync
                    def _(q, hh=hh, sl=sl):
                        if hh < 4:
                            wait_p1_done(q)
                        else:
                            q.wait_ge(stn.h, head_te_end[hh - 4])
                        for src, buf in ((q_h[hh], qr), (k_h[hh], kr)):
                            ins = q.dma_start(out=buf[sl][:], in_=src)
                            rowld[sl].tag(ins, 16)
                        for qu in range(4):
                            ins = q.dma_start(
                                out=vtm[sl][:, qu * 1024:(qu + 1) * 1024].rearrange(
                                    "p (c f) -> p c f", f=128),
                                in_=v2_h.rearrange("c p f -> p c f")[
                                    :, qu * 8:(qu + 1) * 8, hh * 128:(hh + 1) * 128])
                            rowld[sl].tag(ins, 16)

                for hh0 in range(4):
                    emit_loads(hh0)

                def emit_S(h, g, kt):
                    hs = h % 4
                    j = jg[0]
                    jg[0] += 1
                    bank = j % 3
                    slot = j % 4
                    r = kt - 4 * g
                    q0 = r * 128 if r > 0 else 0
                    first_exp = (g == 0 and kt == 0)

                    @block.tensor
                    def _(t, h=h, hs=hs, g=g, kt=kt, bank=bank, q0=q0,
                          ldv=96 * (h // 4 + 1),
                          bfree=exp_hist.get(bank, 0)):
                        if g == 0 and kt == 0:
                            t.wait_ge(rowld[hs].h, ldv)
                            if h == 0:
                                t.wait_ge(sv.h, const_ready)
                        if bfree:
                            t.wait_ge(ss.h, bfree)
                        ins = t.matmul(
                            ps_s[bank][:, q0:512],
                            lhsT=kr[hs][:, kt * 128:(kt + 1) * 128],
                            rhs=qr[hs][:, g * 512 + q0:(g + 1) * 512],
                            start=True, stop=True)
                        stn.tag(ins, 1)

                    s_v = stn.v
                    gate_v = None
                    if r >= 0:
                        @block.vector
                        def _(v, bank=bank, r=r, s_v=s_v):
                            v.wait_ge(stn.h, s_v)
                            ins = v.tensor_tensor(
                                out=ps_s[bank][:, r * 128:r * 128 + 128],
                                in0=ps_s[bank][:, r * 128:r * 128 + 128],
                                in1=mask_T[:], op=ALU.add)
                            sv.tag(ins)

                        gate_v = sv.v
                    if r > 0:
                        @block.vector
                        def _(v, slot=slot, r=r,
                              pfree=lmm_hist.get(slot, 0)):
                            if pfree:
                                v.wait_ge(stn.h, pfree)
                            ins = v.memset(pT[slot][:, 0:r * 128], 0.0)
                            sv.tag(ins)

                        gate_v = sv.v

                    @scalar_section
                    def _(s, h=h, bank=bank, slot=slot, q0=q0, r=r,
                          s_v=s_v, gate_v=gate_v,
                          pfree=lmm_hist.get(slot, 0),
                          mh_v=(mhb_marker[h] if first_exp else 0)):
                        if gate_v is not None:
                            s.wait_ge(sv.h, gate_v)
                        else:
                            s.wait_ge(stn.h, s_v)
                        if r <= 0 and pfree:
                            s.wait_ge(stn.h, pfree)
                        if mh_v:
                            s.wait_ge(sv.h, mh_v)
                        ins = s.activation(out=pT[slot][:, q0:512],
                                           in_=ps_s[bank][:, q0:512],
                                           func=AF.Exp, bias=mhb[:, h:h + 1],
                                           scale=1.0)
                        ss.tag(ins)

                    exp_hist[bank] = ss.v
                    return dict(h=h, g=g, kt=kt, slot=slot, exp_v=ss.v)

                def emit_PVl(info):
                    h = info["h"]
                    hs = h % 4
                    par = h % 2
                    g = info["g"]
                    kt = info["kt"]
                    slot = info["slot"]

                    @block.tensor
                    def _(t, hs=hs, par=par, g=g, kt=kt, slot=slot,
                          exp_v=info["exp_v"],
                          ofree=(last_oevac[par] if kt == 0 else 0),
                          lfree=(last_lcopy[par] if kt == 0 else 0)):
                        t.wait_ge(ss.h, exp_v)
                        if kt == 0 and ofree:
                            t.wait_ge(sv.h, ofree)
                        ins = t.matmul(ps_o[par][:],
                                       lhsT=vtm[hs][:, kt * 128:(kt + 1) * 128],
                                       rhs=pT[slot][:],
                                       start=(kt == 0), stop=(kt == 4 * g + 3))
                        stn.tag(ins, 1)
                        if kt == 0 and lfree:
                            t.wait_ge(sv.h, lfree)
                        ins = t.matmul(ps_l[par][:], lhsT=ones_bb[:],
                                       rhs=pT[slot][:],
                                       start=(kt == 0), stop=(kt == 4 * g + 3))
                        stn.tag(ins, 1)

                    lmm_hist[slot] = stn.v

                def emit_epilogue(g, h):
                    hs = h % 4
                    par = h % 2
                    grp_done = stn.v
                    lslot = 2 * par + g % 2

                    @block.vector
                    def _(v, h=h, hs=hs, par=par, g=g, grp_done=grp_done,
                          ostv=(16 * (h // 4) if (g <= 1 and h >= 4) else 0)):
                        v.wait_ge(stn.h, grp_done)
                        if ostv:
                            v.wait_ge(ost[hs].h, ostv)
                        ins = v.tensor_copy(
                            out=orow[hs][:, g * 512:(g + 1) * 512],
                            in_=ps_o[par][:])
                        sv.tag(ins)

                    last_oevac[par] = sv.v

                    @block.vector
                    def _(v, par=par, lrd=last_lrecip[par]):
                        if lrd:
                            v.wait_ge(sv.h, lrd)
                        ins = v.tensor_copy(out=lsb[par][:], in_=ps_l[par][:])
                        sv.tag(ins)

                    last_lcopy[par] = sv.v
                    cpl_v = sv.v

                    @block.vector
                    def _(v, par=par, lslot=lslot, cpl_v=cpl_v,
                          nwar=last_norm.get(lslot, 0)):
                        v.wait_ge(sv.h, cpl_v)
                        if nwar:
                            v.wait_ge(sv.h, nwar)
                        ins = v.reciprocal(out=lrb[lslot][:], in_=lsb[par][:])
                        sv.tag(ins)

                    last_lrecip[par] = sv.v
                    rec_v = sv.v

                    @block.vector
                    def _(v, hs=hs, g=g, lslot=lslot, rec_v=rec_v):
                        v.wait_ge(sv.h, rec_v)
                        ins = v.tensor_tensor(
                            out=orow[hs][:, g * 512:(g + 1) * 512],
                            in0=orow[hs][:, g * 512:(g + 1) * 512],
                            in1=lrb[lslot][:], op=ALU.mult)
                        sv.tag(ins)

                    last_norm[lslot] = sv.v

                    if g == 3 and h + 2 < NH:
                        tap_tasks.extend(bound_tasks(h + 2))

                for hp in range(NH // 2):
                    hA = 2 * hp
                    hB = hA + 1

                    for u in range(8):
                        ic = hp * 8 + u
                        sl2 = ic % 2

                        @block.sync
                        def _(q, ic=ic, sl2=sl2,
                              wv_prev=(wconv_hist.get(ic - 2, 0))):
                            if ic >= 2:
                                q.wait_ge(sg.h, wv_prev)
                            ins = q.dma_start(out=wcv[sl2][:],
                                              in_=wdT_h[ic * 128:(ic + 1) * 128, :])
                            wcl[sl2].tag(ins, 16)

                        @block.gpsimd
                        def _(g, ic=ic, sl2=sl2):
                            g.wait_ge(wcl[sl2].h, 16 * (ic // 2 + 1))
                            if ic >= 2:
                                g.wait_ge(wcs[sl2].h, 16 * (ic // 2))
                            ins = g.tensor_scalar(out=wcb[sl2][:], in0=wcv[sl2][:],
                                                  scalar1=-128.0, scalar2=None,
                                                  op0=ALU.add, op1=ALU.bypass)
                            sg.tag(ins)

                        wconv_hist[ic] = sg.v

                        @block.gpsimd
                        def _(g, ic=ic, sl2=sl2, cv=sg.v):
                            g.wait_ge(sg.h, cv)
                            ins = g.dma_start(out=wdb_h[ic], in_=wcb[sl2][:])
                            wcs[sl2].tag(ins, 16)

                    if hp == 0:
                        for _t in bound_tasks(0):
                            _t()
                        for _t in bound_tasks(1):
                            _t()

                    pend = []

                    def flush_one():
                        old = pend.pop(0)
                        emit_PVl(old)
                        if old["kt"] == 4 * old["g"] + 3:
                            emit_epilogue(old["g"], old["h"])
                        if tap_tasks:
                            tap_tasks.pop(0)()

                    seqA = [(hA, g, kt) for g in range(8) for kt in range(4 * g + 4)]
                    seqB = [(hB, g, kt) for g in range(8) for kt in range(4 * g + 4)]
                    stream = []
                    for a2, b2 in zip(seqA, seqB):
                        stream.append(a2)
                        stream.append(b2)
                    for (h2, g2, kt2) in stream:
                        pend.append(emit_S(h2, g2, kt2))
                        if len(pend) > 2:
                            flush_one()
                    while pend:
                        flush_one()
                    while tap_tasks:
                        tap_tasks.pop(0)()
                    head_te_end[hA] = stn.v
                    head_te_end[hB] = stn.v
                    if hA + 4 < NH:
                        emit_loads(hA + 4)
                        emit_loads(hB + 4)

                    for h2 in (hA, hB):
                        @block.sync
                        def _(q, h2=h2, ov=sv.v):
                            q.wait_ge(sv.h, ov)
                            ins = q.dma_start(out=o_h[h2], in_=orow[h2 % 4][:])
                            ost[h2 % 4].tag(ins, 16)

                p2ost = [ost[i].v for i in range(4)]
                p2w0, p2w1 = wcs[0].v, wcs[1].v

                def wait_p2_done(eng):
                    for i in range(4):
                        eng.wait_ge(ost[i].h, p2ost[i])
                    eng.wait_ge(wcs[0].h, p2w0)
                    eng.wait_ge(wcs[1].h, p2w1)
        else:
            wait_p2_done = wait_p1_done

        if nphases >= 4:
            with ExitStack() as px:
                xe = px.enter_context
                wst2 = [xe(nc.sbuf_tensor(f"p5w{i}", [128, D], F32)) for i in range(4)]
                w_ob = xe(nc.sbuf_tensor("p5wo", [128, ND * D], BF16))
                bv_all = xe(nc.sbuf_tensor("p5bv", [128, ND], F32))
                bo_row = xe(nc.sbuf_tensor("p5bo", [1, D], F32))
                beff = xe(nc.sbuf_tensor("p5be", [1, D], F32))
                beffb = xe(nc.sbuf_tensor("p5bb", [128, D], F32))
                oti = [xe(nc.sbuf_tensor(f"p5o{i}", [128, D], BF16)) for i in range(2)]
                xt2 = [xe(nc.sbuf_tensor(f"p5x{i}", [128, D], F32)) for i in range(3)]
                x1t = [xe(nc.sbuf_tensor(f"p5y{i}", [128, D], F32)) for i in range(2)]

                wl5 = [SemC(nc, ctx, f"p5wl{i}") for i in range(4)]
                bvl = SemC(nc, ctx, "p5bvl")
                otl = [SemC(nc, ctx, f"p5ot{i}") for i in range(2)]
                xl5 = [SemC(nc, ctx, f"p5xl{i}") for i in range(3)]
                x1s = [SemC(nc, ctx, f"p5x1{i}") for i in range(2)]

                with ExitStack() as bectx:
                    ps_be = bectx.enter_context(nc.psum_tensor("p5pbe", [1, D], F32))

                    @block.sync
                    def _(q):
                        wait_p2_done(q)
                        for c in range(ND):
                            ins = q.dma_start(
                                out=bv_all[:, c:c + 1],
                                in_=bv_h[c * 128:(c + 1) * 128].rearrange("(p u) -> p u", u=1))
                            bvl.tag(ins, 16)
                        ins = q.dma_start(out=bo_row[:], in_=bo_h.rearrange("(u d) -> u d", u=1))
                        bvl.tag(ins, 16)

                    bv_ready = bvl.v
                    cast5 = {}

                    for c in range(ND):
                        b = c % 4
                        qeng = block.sync if c % 2 == 0 else block.gpsimd

                        @qeng
                        def _(q, c=c, b=b, prev=(cast5.get(c - 4))):
                            if c < 2:
                                wait_p2_done(q)
                            if prev is not None:
                                q.wait_ge(stn.h, prev[0])
                                q.wait_ge(sv.h, prev[1])
                            src = wo_h[c * 128:(c + 1) * 128, :]
                            ins = q.dma_start(out=wst2[b][:], in_=src)
                            wl5[b].tag(ins, 16)

                        @block.tensor
                        def _(t, c=c, b=b, bv_ready=bv_ready):
                            t.wait_ge(wl5[b].h, 16 * (c // 4 + 1))
                            if c == 0:
                                t.wait_ge(bvl.h, bv_ready)
                            for oc4 in range(4):
                                ins = t.matmul(ps_be[:, oc4 * 512:(oc4 + 1) * 512],
                                               lhsT=bv_all[:, c:c + 1],
                                               rhs=wst2[b][:, oc4 * 512:(oc4 + 1) * 512],
                                               start=(c == 0), stop=(c == ND - 1))
                            stn.tag(ins, 1)

                        bemm = stn.v

                        @block.vector
                        def _(v, c=c, b=b, bemm=bemm):
                            v.wait_ge(stn.h, bemm)
                            ins = v.tensor_copy(out=w_ob[:, c * D:(c + 1) * D], in_=wst2[b][:])
                            sv.tag(ins)

                        cast5[c] = (bemm, sv.v)

                    be_done = stn.v
                    wob_done = sv.v

                    @scalar_section
                    def _(s, be_done=be_done):
                        s.wait_ge(stn.h, be_done)
                        ins = s.copy(out=beff[:], in_=ps_be[:])
                        ss.tag(ins)

                    @block.vector
                    def _(v, e=ss.v, bv_ready=bv_ready):
                        v.wait_ge(ss.h, e)
                        v.wait_ge(bvl.h, bv_ready)
                        ins = v.tensor_tensor(out=beff[:], in0=beff[:], in1=bo_row[:],
                                              op=ALU.add)
                        sv.tag(ins)

                    beff_done = sv.v

                ps_op = [xe(nc.psum_tensor(f"p5pop{i}", [128, D], F32)) for i in range(2)]

                @block.tensor
                def _(t, beff_done=beff_done):
                    t.wait_ge(sv.h, beff_done)
                    for oc4 in range(4):
                        ins = t.matmul(ps_op[1][:, oc4 * 512:(oc4 + 1) * 512],
                                       lhsT=ones1[0:1, :],
                                       rhs=beff[:, oc4 * 512:(oc4 + 1) * 512],
                                       start=True, stop=True)
                    stn.tag(ins, 1)

                bcast_mm5 = stn.v

                @scalar_section
                def _(s, bcast_mm5=bcast_mm5):
                    s.wait_ge(stn.h, bcast_mm5)
                    ins = s.copy(out=beffb[:], in_=ps_op[1][:])
                    ss.tag(ins)

                beffb_done = ss.v

                osb2 = [xe(nc.sbuf_tensor(f"p5ob{i}", [128, D], F32)) for i in range(2)]
                oln2 = [xe(nc.sbuf_tensor(f"p5olx{i}", [128, D], F32)) for i in range(2)]
                st5b = [xe(nc.sbuf_tensor(f"p5stx{i}", [128, 24], F32)) for i in range(2)]
                mv5b = [xe(nc.sbuf_tensor(f"p5mvx{i}", [128, 2], F32)) for i in range(2)]
                rs5b = [xe(nc.sbuf_tensor(f"p5rsx{i}", [128, 1], F32)) for i in range(2)]
                nm5b = [xe(nc.sbuf_tensor(f"p5nmx{i}", [128, 1], F32)) for i in range(2)]
                lt5b = [xe(nc.sbuf_tensor(f"p5ltx{i}", [128, 1], F32)) for i in range(2)]

                evop_hist = {}
                mm_hist5 = {}
                x1rd_hist = {}
                mvv5 = {}

                def p5_load_o(i):
                    b = i % 2

                    @block.sync
                    def _(q, i=i, b=b, mmprev=mm_hist5.get(i - 2, 0)):
                        if i >= 2:
                            q.wait_ge(stn.h, mmprev)
                        src = o_h.rearrange("c p t -> p c t")[:, :, i * 128:(i + 1) * 128]
                        ins = q.dma_start(out=oti[b][:].rearrange("p (c t) -> p c t", t=128),
                                          in_=src)
                        otl[b].tag(ins, 16)

                def p5_load_x(i):
                    b3 = i % 3

                    @block.sync
                    def _(q, i=i, b3=b3, xprev=x1rd_hist.get(i - 3, 0)):
                        if i >= 3:
                            q.wait_ge(sv.h, xprev)
                        ins = q.dma_start(out=xt2[b3][:], in_=x_h[i * 128:(i + 1) * 128, :])
                        xl5[b3].tag(ins, 16)

                def p5_mm(i):
                    b = i % 2

                    @block.tensor
                    def _(t, i=i, b=b, wob_done=wob_done, beffb_done=beffb_done,
                          evprev=evop_hist.get(i - 2, 0)):
                        t.wait_ge(otl[b].h, 16 * (i // 2 + 1))
                        if i == 0:
                            t.wait_ge(sv.h, wob_done)
                            t.wait_ge(ss.h, beffb_done)
                        if i == 1:
                            t.wait_ge(ss.h, beffb_done)
                        if i >= 2:
                            t.wait_ge(sv.h, evprev)
                        for c in range(ND):
                            for oc4 in range(4):
                                ins = t.matmul(
                                    ps_op[b][:, oc4 * 512:(oc4 + 1) * 512],
                                    lhsT=oti[b][:, c * 128:(c + 1) * 128],
                                    rhs=w_ob[:, c * D + oc4 * 512:c * D + (oc4 + 1) * 512],
                                    start=(c == 0), stop=(c == ND - 1))
                        stn.tag(ins, 1)
                    mm_hist5[i] = stn.v

                def p5_front(i):
                    b = i % 2

                    @block.vector
                    def _(v, i=i, b=b, opmm=mm_hist5[i], beffb_done=beffb_done):
                        v.wait_ge(stn.h, opmm)
                        if i == 0:
                            v.wait_ge(ss.h, beffb_done)
                        ins = v.scalar_tensor_tensor(out=osb2[b][:], in0=ps_op[b][:],
                                                     scalar=1.0, in1=beffb[:],
                                                     op0=ALU.mult, op1=ALU.add)
                        sv.tag(ins)
                        v.wait_ge(sv.h, sv.v)
                        for c in range(4):
                            ins = v.bn_stats(out=st5b[b][:, c * 6:(c + 1) * 6],
                                             in_=osb2[b][:, c * 512:(c + 1) * 512])
                        sv.tag(ins)
                        v.wait_ge(sv.h, sv.v)
                        ins = v.bn_aggr(out=mv5b[b][:], in_=st5b[b][:])
                        sv.tag(ins)
                    evop_hist[i] = sv.v - 2
                    mvv5[i] = sv.v

                def p5_tail(i):
                    b = i % 2

                    @scalar_section
                    def _(s, b=b, mvr=mvv5[i]):
                        s.wait_ge(sv.h, mvr)
                        ins = s.activation(out=lt5b[b][:], in_=mv5b[b][:, 1:2], func=AF.Ln,
                                           bias=eps_t[:], scale=1.0)
                        ss.tag(ins)
                        s.wait_ge(ss.h, ss.v)
                        ins = s.activation(out=rs5b[b][:], in_=lt5b[b][:], func=AF.Exp,
                                           bias=zero_t[:], scale=-0.5)
                        ss.tag(ins)

                    rstd_ready5 = ss.v

                    @block.vector
                    def _(v, b=b, rstd_ready5=rstd_ready5):
                        v.wait_ge(ss.h, rstd_ready5)
                        ins = v.scalar_tensor_tensor(out=nm5b[b][:], in0=mv5b[b][:, 0:1],
                                                     scalar=-1.0, in1=rs5b[b][:],
                                                     op0=ALU.mult, op1=ALU.mult)
                        sv.tag(ins)

                    nmr_ready5 = sv.v

                    @scalar_section
                    def _(s, b=b, nmr_ready5=nmr_ready5):
                        s.wait_ge(sv.h, nmr_ready5)
                        ins = s.activation(out=oln2[b][:], in_=osb2[b][:], func=AF.Identity,
                                           bias=nm5b[b][:], scale=rs5b[b][:])
                        ss.tag(ins)

                    oln_ready = ss.v

                    @block.vector
                    def _(v, i=i, b=b, oln_ready=oln_ready, x1prev=16 * (i // 2)):
                        v.wait_ge(ss.h, oln_ready)
                        v.wait_ge(xl5[i % 3].h, 16 * (i // 3 + 1))
                        if i >= 2:
                            v.wait_ge(x1s[b].h, x1prev)
                        ins = v.tensor_tensor(out=x1t[b][:], in0=xt2[i % 3][:],
                                              in1=oln2[b][:], op=ALU.add)
                        sv.tag(ins)

                    x1_ready = sv.v
                    x1rd_hist[i] = x1_ready

                    @block.sync
                    def _(q, i=i, b=b, x1_ready=x1_ready):
                        q.wait_ge(sv.h, x1_ready)
                        ins = q.dma_start(out=x1_h[i], in_=x1t[b][:])
                        x1s[b].tag(ins, 16)

                p5_load_o(0)
                p5_load_x(0)
                p5_load_o(1)
                p5_load_x(1)
                p5_load_x(2)
                for i in range(NT):
                    p5_mm(i)
                    if i + 2 < NT:
                        p5_load_o(i + 2)
                    p5_front(i)
                    if i >= 1:
                        p5_tail(i - 1)
                        if i + 2 < NT:
                            p5_load_x(i + 2)
                p5_tail(NT - 1)

                p5a, p5b = x1s[0].v, x1s[1].v

                def wait_p5_done(eng):
                    eng.wait_ge(x1s[0].h, p5a)
                    eng.wait_ge(x1s[1].h, p5b)
        else:
            wait_p5_done = wait_p2_done

        if nphases >= 5:
            xfm2_cm = nc.sbuf_tensor("xfm2", [128, ND * T], BF16)
            xfm2 = xfm2_cm.__enter__()
            f0a, f0b, f0_ptfr, _f0m = emit_ln_transpose("f0", x1_h, True, xfm2,
                                                  gate=wait_p5_done)

            if dbg:
                @block.sync
                def _(q):
                    q.wait_ge(f0_ptfr[0].h, f0a)
                    q.wait_ge(f0_ptfr[1].h, f0b)
                    for dc in range(ND):
                        ins = q.dma_start(out=xfm_h[dc], in_=xfm2[:, dc * T:(dc + 1) * T])
                        sq.tag(ins, 16)
            dbg_f0_dump = sq.v

            def wait_f0_done(eng):
                eng.wait_ge(f0_ptfr[0].h, f0a)
                eng.wait_ge(f0_ptfr[1].h, f0b)
                if dbg and dbg_f0_dump:
                    eng.wait_ge(sq.h, dbg_f0_dump)
        else:
            wait_f0_done = wait_p5_done

        if nphases >= 6:
            with ExitStack() as px:
                xe = px.enter_context
                gst = [xe(nc.sbuf_tensor(f"f1g{i}", [128, D], I32)) for i in range(2)]
                ust = [xe(nc.sbuf_tensor(f"f1u{i}", [128, D], I32)) for i in range(2)]
                gwb = [xe(nc.sbuf_tensor(f"f1gb{i}", [128, D], BF16)) for i in range(2)]
                uwb = [xe(nc.sbuf_tensor(f"f1ub{i}", [128, D], BF16)) for i in range(2)]
                sgb = [xe(nc.sbuf_tensor(f"f1sg{i}", [128, D], BF16)) for i in range(2)]
                hbf = [xe(nc.sbuf_tensor(f"f1h{i}", [128, D], BF16)) for i in range(2)]
                gsc = [xe(nc.sbuf_tensor(f"f1gs{i}", [128, 1], F32)) for i in range(2)]
                usc = [xe(nc.sbuf_tensor(f"f1us{i}", [128, 1], F32)) for i in range(2)]
                ps_g = xe(nc.psum_tensor("f1pg", [128, D], F32))
                ps_u = xe(nc.psum_tensor("f1pu", [128, D], F32))

                gl = [SemC(nc, ctx, f"f1gl{i}") for i in range(2)]
                scl = [SemC(nc, ctx, f"f1sc{i}") for i in range(2)]
                hst = [SemC(nc, ctx, f"f1hs{i}") for i in range(2)]

                deq_hist = {}
                gmm_hist = {}
                umm_hist = {}
                silu_hist = {}
                hstt_hist = {}

                def f1_load(ic):
                    b = ic % 2

                    @block.sync
                    def _(q, ic=ic, b=b, dqprev=deq_hist.get(ic - 2, 0),
                          husers=hstt_hist.get((ic - 2, 1), 0)):
                        if ic == 0:
                            wait_f0_done(q)
                        if ic >= 2:
                            q.wait_ge(ss.h, dqprev)
                            q.wait_ge(sv.h, husers)
                        for w_hbm, dstt, sem in ((wgT_h, gst, gl), (wuT_h, ust, gl)):
                            src = w_hbm[:, ic * 128:(ic + 1) * 128].rearrange(
                                "(c p) o -> p c o", p=128)
                            ins = q.dma_start(out=dstt[b][:].rearrange(
                                "p (c o) -> p c o", o=128), in_=src)
                            sem[b].tag(ins, 16)
                        for v_hbm, dstt in ((gs_h, gsc), (us_h, usc)):
                            ins = q.dma_start(
                                out=dstt[b][:],
                                in_=v_hbm[ic * 128:(ic + 1) * 128].rearrange("(p u) -> p u", u=1))
                            scl[b].tag(ins, 16)

                def f1_deq(ic):
                    b = ic % 2

                    @scalar_section
                    def _(s, ic=ic, b=b, mmprev=(gmm_hist.get((ic - 2, 1), 0),
                                                 umm_hist.get((ic - 2, 1), 0))):
                        s.wait_ge(gl[b].h, 32 * (ic // 2 + 1))
                        if ic >= 2:
                            s.wait_ge(stn.h, max(mmprev))
                        ins = s.activation(out=gwb[b][:], in_=gst[b][:], func=AF.Copy,
                                           bias=-128.0, scale=1.0)
                        ss.tag(ins)
                        ins = s.activation(out=uwb[b][:], in_=ust[b][:], func=AF.Copy,
                                           bias=-128.0, scale=1.0)
                        ss.tag(ins)

                    deq_hist[ic] = ss.v

                f1_load(0)
                f1_load(1)
                f1_deq(0)
                for ic in range(NI):
                    b = ic % 2
                    for half in range(2):
                        t0h = half * 2048

                        @block.tensor
                        def _(t, ic=ic, b=b, half=half, t0h=t0h, dq=deq_hist[ic],
                              sl=silu_hist.get((ic - 1, 1) if half == 0 else (ic, 0), 0)):
                            if half == 0:
                                t.wait_ge(ss.h, dq)
                            prevg = silu_hist.get((ic, half - 1), silu_hist.get((ic - 1, 1), 0))
                            t.wait_ge(ss.h, prevg)
                            for d in range(ND):
                                for bk in range(4):
                                    ins = t.matmul(
                                        ps_g[:, bk * 512:(bk + 1) * 512],
                                        lhsT=gwb[b][:, d * 128:(d + 1) * 128],
                                        rhs=xfm2[:, d * T + t0h + bk * 512:
                                                 d * T + t0h + (bk + 1) * 512],
                                        start=(d == 0), stop=(d == ND - 1))
                            stn.tag(ins, 1)

                        gmm_hist[(ic, half)] = stn.v

                        @scalar_section
                        def _(s, ic=ic, b=b, half=half, gmm=gmm_hist[(ic, half)],
                              scv=16 * (ic // 2 + 1) * 2,
                              sgprev=hstt_hist.get((ic - 1, half), 0)):
                            s.wait_ge(stn.h, gmm)
                            if half == 0:
                                s.wait_ge(scl[b].h, scv)
                            if ic >= 1:
                                s.wait_ge(sv.h, sgprev)
                            ins = s.activation(out=sgb[half][:], in_=ps_g[:], func=AF.Silu,
                                               bias=zero_t[:], scale=gsc[b][:])
                            ss.tag(ins)

                        silu_hist[(ic, half)] = ss.v

                        @block.tensor
                        def _(t, ic=ic, b=b, half=half, t0h=t0h,
                              prevu=hstt_hist.get((ic, half - 1),
                                                  hstt_hist.get((ic - 1, 1), 0))):
                            t.wait_ge(sv.h, prevu)
                            for d in range(ND):
                                for bk in range(4):
                                    ins = t.matmul(
                                        ps_u[:, bk * 512:(bk + 1) * 512],
                                        lhsT=uwb[b][:, d * 128:(d + 1) * 128],
                                        rhs=xfm2[:, d * T + t0h + bk * 512:
                                                 d * T + t0h + (bk + 1) * 512],
                                        start=(d == 0), stop=(d == ND - 1))
                            stn.tag(ins, 1)

                        umm_hist[(ic, half)] = stn.v

                        @block.vector
                        def _(v, ic=ic, b=b, half=half, umm=umm_hist[(ic, half)],
                              silu_v=silu_hist[(ic, half)],
                              hprev=hst[half].v):
                            v.wait_ge(stn.h, umm)
                            v.wait_ge(ss.h, silu_v)
                            if ic >= 1:
                                v.wait_ge(hst[half].h, hprev)
                            ins = v.scalar_tensor_tensor(out=hbf[half][:], in0=ps_u[:],
                                                         scalar=usc[b][:], in1=sgb[half][:],
                                                         op0=ALU.mult, op1=ALU.mult)
                            sv.tag(ins)

                        hstt_hist[(ic, half)] = sv.v

                        @block.sync
                        def _(q, ic=ic, half=half, t0h=t0h, hv=hstt_hist[(ic, half)]):
                            q.wait_ge(sv.h, hv)
                            ins = q.dma_start(out=h_h[ic][:, t0h:t0h + 2048],
                                              in_=hbf[half][:])
                            hst[half].tag(ins, 16)

                        if half == 0 and ic + 1 < NI:
                            f1_deq(ic + 1)
                        if half == 1 and ic + 2 < NI:
                            f1_load(ic + 2)

                f1a, f1b = hst[0].v, hst[1].v

                def wait_f1_done(eng):
                    eng.wait_ge(hst[0].h, f1a)
                    eng.wait_ge(hst[1].h, f1b)
        else:
            wait_f1_done = wait_f0_done

        if nphases >= 5:
            xfm2_cm.__exit__(None, None, None)

        if nphases >= 7:
            HGRP = [(0, 6), (6, 12), (12, 16)]
            with ExitStack() as px:
                xe = px.enter_context
                hb = [xe(nc.sbuf_tensor(f"f2h{i}", [128, NI * 512], BF16)) for i in range(2)]
                wdt = [xe(nc.sbuf_tensor(f"f2w{i}", [128, 768], BF16)) for i in range(8)]
                dob = [xe(nc.sbuf_tensor(f"f2d{i}", [128, ND * 512], BF16)) for i in range(2)]
                dsc = xe(nc.sbuf_tensor("f2ds", [128, ND], F32))
                dtm = [xe(nc.sbuf_tensor(f"f2tm{i}", [128, D], BF16)) for i in range(2)]
                dnr = xe(nc.sbuf_tensor("f2dn", [128, D], F32))
                xt6 = [xe(nc.sbuf_tensor(f"f2x{i}", [128, D], F32)) for i in range(2)]
                st6 = xe(nc.sbuf_tensor("f2st", [128, 24], F32))
                mv6 = xe(nc.sbuf_tensor("f2mv", [128, 2], F32))
                rs6 = xe(nc.sbuf_tensor("f2rs", [128, 1], F32))
                nm6 = xe(nc.sbuf_tensor("f2nm", [128, 1], F32))
                lt6 = xe(nc.sbuf_tensor("f2lt", [128, 1], F32))
                ps_d = xe(nc.psum_tensor("f2pd", [128, 6 * 512], F32))
                ps_t = xe(nc.psum_tensor("f2pt", [128, 2048], BF16))

                hbl = [SemC(nc, ctx, f"f2hl{i}") for i in range(2)]
                wdl = [SemC(nc, ctx, f"f2wl{i}") for i in range(8)]
                dsl = SemC(nc, ctx, "f2dsl")
                xl6 = [SemC(nc, ctx, f"f2xl{i}") for i in range(2)]
                ots = [SemC(nc, ctx, f"f2os{i}") for i in range(2)]

                @block.sync
                def _(q):
                    wait_f1_done(q)
                    for c in range(ND):
                        ins = q.dma_start(
                            out=dsc[:, c:c + 1],
                            in_=ds_h[c * 128:(c + 1) * 128].rearrange("(p u) -> p u", u=1))
                        dsl.tag(ins, 16)

                ds_ready = dsl.v
                wd_hist = {}
                dev_hist = {}
                grp_end = {}
                trev_hist = {}
                dtmrd_hist = {}
                NB = 8
                tr_tasks = []

                def emit_group(tb, gi, hbslot):
                    h0, h1 = HGRP[gi]
                    nhc = h1 - h0
                    for ic in range(NI):
                        git = (tb * 3 + gi) * NI + ic
                        ws = git % 8
                        qeng = block.sync if ic % 2 == 0 else block.gpsimd

                        @qeng
                        def _(q, gi=gi, ic=ic, ws=ws, git=git, h0=h0, h1=h1):
                            if git >= 8:
                                q.wait_ge(stn.h, wd_hist[git - 8])
                            src = wdb_h[ic][:, h0 * 128:h1 * 128]
                            ins = q.dma_start(out=wdt[ws][:, 0:(h1 - h0) * 128],
                                              in_=src)
                            wdl[ws].tag(ins, 16)

                        @block.tensor
                        def _(t, tb=tb, gi=gi, ic=ic, ws=ws, git=git, nhc=nhc,
                              hbslot=hbslot,
                              evprev=dev_hist.get((tb, gi - 1) if gi else
                                                  (tb - 1, 2), 0)):
                            t.wait_ge(wdl[ws].h, 16 * (git // 8 + 1))
                            if ic == 0:
                                t.wait_ge(hbl[hbslot].h, 16 * (tb // 2 + 1))
                                if evprev:
                                    t.wait_ge(sv.h, evprev)
                            for u in range(nhc):
                                ins = t.matmul(
                                    ps_d[:, u * 512:(u + 1) * 512],
                                    lhsT=wdt[ws][:, u * 128:(u + 1) * 128],
                                    rhs=hb[hbslot][:, ic * 512:(ic + 1) * 512],
                                    start=(ic == 0), stop=(ic == NI - 1))
                            stn.tag(ins, 1)

                        wd_hist[git] = stn.v

                    mm_end = stn.v

                    @block.vector
                    def _(v, tb=tb, gi=gi, h0=h0, nhc=nhc, mm_end=mm_end,
                          ds_ready=ds_ready):
                        v.wait_ge(stn.h, mm_end)
                        if tb == 0 and gi == 0:
                            v.wait_ge(dsl.h, ds_ready)
                        for u in range(nhc):
                            ins = v.tensor_scalar(
                                out=dob[tb % 2][:, (h0 + u) * 512:(h0 + u + 1) * 512],
                                in0=ps_d[:, u * 512:(u + 1) * 512],
                                scalar1=dsc[:, h0 + u:h0 + u + 1], scalar2=None,
                                op0=ALU.mult, op1=ALU.bypass)
                        sv.tag(ins)

                    dev_hist[(tb, gi)] = sv.v

                def make_tr(tb, tc):
                    def run():
                        gt = tb * 4 + tc
                        ob = gt % 2

                        @block.sync
                        def _(q, gt=gt, ob=ob):
                            if gt >= 2:
                                q.wait_ge(ots[ob].h, 16 * (gt // 2))
                            ins = q.dma_start(out=xt6[ob][:], in_=x1_h[gt])
                            xl6[ob].tag(ins, 16)

                        @block.tensor
                        def _(t, tb=tb, tc=tc, devs=dev_hist[(tb, 2)],
                              trprev=trev_hist.get((tb, tc - 1) if tc >= 1 else
                                                   (tb - 1, 3), 0)):
                            t.wait_ge(sv.h, devs)
                            if trprev:
                                t.wait_ge(ss.h, trprev)
                            for hc in range(ND):
                                ins = t.transpose(
                                    ps_t[:, hc * 128:(hc + 1) * 128],
                                    dob[tb % 2][:, hc * 512 + tc * 128:
                                                hc * 512 + (tc + 1) * 128],
                                    ident_b[:])
                            stn.tag(ins, 1)

                        trmm = stn.v

                        @scalar_section
                        def _(s, tb=tb, tc=tc, trmm=trmm,
                              dtmprev=dtmrd_hist.get((tb, tc - 2) if tc >= 2 else
                                                     (tb - 1, tc + 2), 0)):
                            s.wait_ge(stn.h, trmm)
                            if dtmprev:
                                s.wait_ge(sv.h, dtmprev)
                            ins = s.copy(out=dtm[tc % 2][:], in_=ps_t[:])
                            ss.tag(ins)

                        trev_hist[(tb, tc)] = ss.v
                        dtm_ready = ss.v

                        @block.vector
                        def _(v, tc=tc, dtm_ready=dtm_ready):
                            v.wait_ge(ss.h, dtm_ready)
                            for c in range(4):
                                ins = v.bn_stats(out=st6[:, c * 6:(c + 1) * 6],
                                                 in_=dtm[tc % 2][:, c * 512:(c + 1) * 512])
                            sv.tag(ins)
                            v.wait_ge(sv.h, sv.v)
                            ins = v.bn_aggr(out=mv6[:], in_=st6[:])
                            sv.tag(ins)

                        mv_ready6 = sv.v
                        dtmrd_hist[(tb, tc)] = mv_ready6

                        @scalar_section
                        def _(s, mv_ready6=mv_ready6):
                            s.wait_ge(sv.h, mv_ready6)
                            ins = s.activation(out=lt6[:], in_=mv6[:, 1:2], func=AF.Ln,
                                               bias=eps_t[:], scale=1.0)
                            ss.tag(ins)
                            s.wait_ge(ss.h, ss.v)
                            ins = s.activation(out=rs6[:], in_=lt6[:], func=AF.Exp,
                                               bias=zero_t[:], scale=-0.5)
                            ss.tag(ins)

                        rstd_ready6 = ss.v

                        @block.vector
                        def _(v, rstd_ready6=rstd_ready6):
                            v.wait_ge(ss.h, rstd_ready6)
                            ins = v.scalar_tensor_tensor(out=nm6[:], in0=mv6[:, 0:1],
                                                         scalar=-1.0, in1=rs6[:],
                                                         op0=ALU.mult, op1=ALU.mult)
                            sv.tag(ins)

                        nmr_ready6 = sv.v

                        @scalar_section
                        def _(s, tc=tc, nmr_ready6=nmr_ready6):
                            s.wait_ge(sv.h, nmr_ready6)
                            ins = s.activation(out=dnr[:], in_=dtm[tc % 2][:],
                                               func=AF.Identity, bias=nm6[:], scale=rs6[:])
                            ss.tag(ins)

                        ln3_ready = ss.v

                        @block.vector
                        def _(v, gt=gt, ob=ob, ln3_ready=ln3_ready):
                            v.wait_ge(ss.h, ln3_ready)
                            v.wait_ge(xl6[ob].h, 16 * (gt // 2 + 1))
                            ins = v.tensor_tensor(out=xt6[ob][:], in0=xt6[ob][:],
                                                  in1=dnr[:], op=ALU.add)
                            sv.tag(ins)

                        out_ready = sv.v

                        @block.sync
                        def _(q, gt=gt, ob=ob, out_ready=out_ready):
                            q.wait_ge(sv.h, out_ready)
                            ins = q.dma_start(out=out_h[gt * 128:(gt + 1) * 128, :],
                                              in_=xt6[ob][:])
                            ots[ob].tag(ins, 16)
                    return run

                @block.gpsimd
                def _(g):
                    wait_f1_done(g)

                for tb in range(NB):
                    hbslot = tb % 2

                    @block.gpsimd
                    def _(q, tb=tb, hbslot=hbslot, prev=grp_end.get(tb - 2, 0)):
                        if prev:
                            q.wait_ge(stn.h, prev)
                        src = h_h.rearrange("c p t -> p c t")[:, :, tb * 512:(tb + 1) * 512]
                        ins = q.dma_start(out=hb[hbslot][:].rearrange(
                            "p (c t) -> p c t", t=512), in_=src)
                        hbl[hbslot].tag(ins, 16)

                    for gi in range(3):
                        emit_group(tb, gi, hbslot)
                        if tr_tasks:
                            tr_tasks.pop(0)()
                        if gi == 2 and tr_tasks:
                            tr_tasks.pop(0)()
                    grp_end[tb] = stn.v
                    for tc in range(4):
                        tr_tasks.append(make_tr(tb, tc))

                while tr_tasks:
                    tr_tasks.pop(0)()

                f2a, f2b = ots[0].v, ots[1].v

                def wait_f2_done(eng):
                    eng.wait_ge(ots[0].h, f2a)
                    eng.wait_ge(ots[1].h, f2b)
        else:
            wait_f2_done = wait_f1_done


        _scr.flush()

        @block.sync
        def _(q):
            wait_f2_done(q)
            q.wait_ge(sq.h, sq.v)

    return nc



def _prep_inputs(
    L0_attn_query, L0_attn_w_q, L0_attn_b_q, L0_attn_w_k, L0_attn_b_k,
    L0_attn_w_v, L0_attn_b_v, L0_attn_w_o, L0_attn_b_o,
    L0_ffn_gate_w, L0_ffn_gate_scale, L0_ffn_up_w, L0_ffn_up_scale,
    L0_ffn_down_w, L0_ffn_down_scale,
):
    base = {
        "wq": np.ascontiguousarray(np.asarray(L0_attn_w_q, dtype=np.float32)),
        "wk": np.ascontiguousarray(np.asarray(L0_attn_w_k, dtype=np.float32)),
        "wv": np.ascontiguousarray(np.asarray(L0_attn_w_v, dtype=np.float32)),
        "wo": np.ascontiguousarray(np.asarray(L0_attn_w_o, dtype=np.float32)),
        "bq": np.ascontiguousarray(np.asarray(L0_attn_b_q, dtype=np.float32)),
        "bv": np.ascontiguousarray(np.asarray(L0_attn_b_v, dtype=np.float32)),
        "bo": np.ascontiguousarray(np.asarray(L0_attn_b_o, dtype=np.float32)),
        "wgT": np.ascontiguousarray(np.asarray(L0_ffn_gate_w).T),
        "wuT": np.ascontiguousarray(np.asarray(L0_ffn_up_w).T),
        "wdT": np.ascontiguousarray(np.asarray(L0_ffn_down_w).T),
        "gs": np.ascontiguousarray(np.asarray(L0_ffn_gate_scale, dtype=np.float32)),
        "us": np.ascontiguousarray(np.asarray(L0_ffn_up_scale, dtype=np.float32)),
        "ds": np.ascontiguousarray(np.asarray(L0_ffn_down_scale, dtype=np.float32)),
    }
    xq = np.asarray(L0_attn_query)
    return [dict(base, x=np.ascontiguousarray(xq[b])) for b in range(8)]


def kernel(**inputs):
    in_maps = _prep_inputs(**inputs)
    nc = build()
    res = run_bass_kernel_spmd(nc, in_maps, core_ids=list(range(8)))
    return np.stack([np.asarray(r["out"]) for r in res.results], axis=0)


# version 17
# speedup vs baseline: 1.2401x; 1.2401x over previous
import numpy as np
from contextlib import ExitStack

import concourse.bass as bass
import concourse.mybir as mybir
from concourse.bass_utils import run_bass_kernel_spmd

F32 = mybir.dt.float32
BF16 = mybir.dt.bfloat16
I32 = mybir.dt.int32
AF = mybir.ActivationFunctionType
ALU = mybir.AluOpType
AX = mybir.AxisListType

T = 4096
D = 2048
NH = 16
FF = 8192
NT = T // 128
ND = D // 128
NI = FF // 128
EPS = 1e-5
QSCALE = float(1.0 / np.sqrt(128.0))

S_BANKS = 5
S_CAP = S_BANKS * 512


class SemC:

    def __init__(self, nc, ctx, name):
        self.h = ctx.enter_context(nc.semaphore(name))
        self.v = 0

    def tag(self, ins, n=1):
        ins.then_inc(self.h, n)
        self.v += n
        return self.v


def build(nphases=99, dbg=False):
    nc = bass.Bass("TRN2", target_bir_lowering=False)

    x_h = nc.dram_tensor("x", [T, D], F32, kind="ExternalInput").ap()
    wq_h = nc.dram_tensor("wq", [D, D], F32, kind="ExternalInput").ap()
    wk_h = nc.dram_tensor("wk", [D, D], F32, kind="ExternalInput").ap()
    wv_h = nc.dram_tensor("wv", [D, D], F32, kind="ExternalInput").ap()
    wo_h = nc.dram_tensor("wo", [D, D], F32, kind="ExternalInput").ap()
    bq_h = nc.dram_tensor("bq", [D], F32, kind="ExternalInput").ap()
    bv_h = nc.dram_tensor("bv", [D], F32, kind="ExternalInput").ap()
    bo_h = nc.dram_tensor("bo", [D], F32, kind="ExternalInput").ap()
    wgT_h = nc.dram_tensor("wgT", [D, FF], I32, kind="ExternalInput").ap()
    wuT_h = nc.dram_tensor("wuT", [D, FF], I32, kind="ExternalInput").ap()
    wdT_h = nc.dram_tensor("wdT", [FF, D], I32, kind="ExternalInput").ap()
    gs_h = nc.dram_tensor("gs", [FF], F32, kind="ExternalInput").ap()
    us_h = nc.dram_tensor("us", [FF], F32, kind="ExternalInput").ap()
    ds_h = nc.dram_tensor("ds", [D], F32, kind="ExternalInput").ap()

    out_h = nc.dram_tensor("out", [T, D], F32, kind="ExternalOutput").ap()

    sk = "ExternalOutput" if dbg else "Internal"
    q_h = nc.dram_tensor("q_s", [ND, 128, T], BF16, kind=sk).ap()
    k_h = nc.dram_tensor("k_s", [ND, 128, T], BF16, kind=sk).ap()
    v2_h = nc.dram_tensor("v_s", [NT, 128, D], BF16, kind=sk).ap()
    o_h = nc.dram_tensor("o_s", [ND, 128, T], BF16, kind=sk).ap()
    x1_h = nc.dram_tensor("x1_s", [NT, 128, D], F32, kind=sk).ap()
    h_h = nc.dram_tensor("h_s", [NI, 128, T], BF16, kind=sk).ap()
    wdb_h = nc.dram_tensor("wdb_s", [NI, 128, D], BF16, kind=sk).ap()
    xfm_h = (nc.dram_tensor("xfm_s", [ND, 128, T], BF16, kind="ExternalOutput").ap()
             if dbg else None)

    with ExitStack() as ctx:
        E = ctx.enter_context
        block = E(nc.Block())

        class _ScH:
            __slots__ = ("incs",)

            def __init__(self):
                self.incs = []

            def then_inc(self, sem, n):
                self.incs.append((sem, n))

        class _ScalarRec:
            def __init__(self):
                self.cmds = []

            def wait_ge(self, sem, v):
                self.cmds.append(("w", sem, v))

            def _op(self, name, a, k):
                h = _ScH()
                self.cmds.append(("o", name, a, k, h))
                return h

            def activation(self, *a, **k):
                return self._op("activation", a, k)

            def copy(self, *a, **k):
                return self._op("copy", a, k)

            def flush(self):
                if not self.cmds:
                    return
                cmds, self.cmds = self.cmds, []

                @block.scalar
                def _(s):
                    for c in cmds:
                        if c[0] == "w":
                            s.wait_ge(c[1], c[2])
                        else:
                            ins = getattr(s, c[1])(*c[2], **c[3])
                            for sem, n in c[4].incs:
                                ins.then_inc(sem, n)

        _scr = _ScalarRec()

        def scalar_section(fn):
            fn(_scr)
            if len(_scr.cmds) >= 100:
                _scr.flush()

        ident_b = E(nc.sbuf_tensor("ident_b", [128, 128], BF16))
        ident_f = E(nc.sbuf_tensor("ident_f", [128, 128], F32))
        mask_T = E(nc.sbuf_tensor("mask_T", [128, 128], F32))
        ones1 = E(nc.sbuf_tensor("ones1", [128, 128], F32))
        ones_bb = E(nc.sbuf_tensor("ones_bb", [128, 128], BF16))
        eps_t = E(nc.sbuf_tensor("eps_t", [128, 1], F32))
        zero_t = E(nc.sbuf_tensor("zero_t", [128, 1], F32))
        iota_i = E(nc.sbuf_tensor("iota_i", [128, 128], I32))
        iota_j = E(nc.sbuf_tensor("iota_j", [128, 128], I32))
        xfm_cm = nc.sbuf_tensor("xfm", [128, ND * T], BF16)
        xfm = xfm_cm.__enter__()

        sv = SemC(nc, ctx, "sv")
        ss = SemC(nc, ctx, "ss")
        stn = SemC(nc, ctx, "stn")
        sg = SemC(nc, ctx, "sg")
        sq = SemC(nc, ctx, "sq")

        @block.gpsimd
        def _(g):
            g.iota(iota_j[:], pattern=[[1, 128]], base=0, channel_multiplier=0)
            ins = g.iota(iota_i[:], pattern=[[0, 128]], base=0, channel_multiplier=1)
            sg.tag(ins)

        @block.vector
        def _(v):
            v.memset(eps_t[:], EPS)
            v.memset(zero_t[:], 0.0)
            v.memset(ones1[:], 1.0)
            v.memset(ones_bb[:], 1.0)
            v.wait_ge(sg.h, sg.v)
            v.tensor_tensor(out=ident_b[:], in0=iota_j[:], in1=iota_i[:], op=ALU.is_equal)
            v.tensor_tensor(out=ident_f[:], in0=iota_j[:], in1=iota_i[:], op=ALU.is_equal)
            ins = v.tensor_tensor(out=mask_T[:], in0=iota_i[:], in1=iota_j[:], op=ALU.is_gt)
            sv.tag(ins)
            v.wait_ge(sv.h, sv.v)
            ins = v.tensor_scalar(out=mask_T[:], in0=mask_T[:], scalar1=-1e10, scalar2=None,
                                  op0=ALU.mult, op1=ALU.bypass)
            sv.tag(ins)

        const_ready = sv.v

        def emit_ln_transpose(tag, src_h, src_is_3d, dst, gate=None):
            with ExitStack() as px:
                xe = px.enter_context
                xt = [xe(nc.sbuf_tensor(f"{tag}x{i}", [128, D], F32)) for i in range(2)]
                stats = [xe(nc.sbuf_tensor(f"{tag}st{i}", [128, 24], F32)) for i in range(2)]
                mv = [xe(nc.sbuf_tensor(f"{tag}mv{i}", [128, 2], F32)) for i in range(2)]
                rstd = [xe(nc.sbuf_tensor(f"{tag}rs{i}", [128, 1], F32)) for i in range(2)]
                nmr = [xe(nc.sbuf_tensor(f"{tag}nm{i}", [128, 1], F32)) for i in range(2)]
                lt = [xe(nc.sbuf_tensor(f"{tag}lt{i}", [128, 1], F32)) for i in range(2)]
                xtm = xe(nc.sbuf_tensor(f"{tag}tm", [128, D], BF16))
                pst = [xe(nc.psum_tensor(f"{tag}ps{i}", [128, 1024], BF16)) for i in range(2)]

                ld = [SemC(nc, ctx, f"{tag}ld{i}") for i in range(2)]
                fr = [SemC(nc, ctx, f"{tag}fr{i}") for i in range(2)]
                ptfr = [SemC(nc, ctx, f"{tag}pf{i}") for i in range(2)]
                ptrd = [SemC(nc, ctx, f"{tag}pr{i}") for i in range(2)]
                mvv = {}
                tile_marks = {}

                def emit_load(i):
                    b = i % 2

                    @block.sync
                    def _(q, i=i, b=b):
                        if i == 0 and gate is not None:
                            gate(q)
                        if i >= 2:
                            q.wait_ge(fr[b].h, i // 2)
                        ins = q.dma_start(out=xt[b][:], in_=(src_h[i] if src_is_3d
                                          else src_h[i * 128:(i + 1) * 128, :]))
                        ld[b].tag(ins, 16)

                def emit_front(i):
                    b = i % 2

                    @block.vector
                    def _(v, i=i, b=b):
                        v.wait_ge(ld[b].h, 16 * (i // 2 + 1))
                        for c in range(4):
                            ins = v.bn_stats(out=stats[b][:, c * 6:(c + 1) * 6],
                                             in_=xt[b][:, c * 512:(c + 1) * 512])
                        sv.tag(ins)
                        v.wait_ge(sv.h, sv.v)
                        ins = v.bn_aggr(out=mv[b][:], in_=stats[b][:])
                        sv.tag(ins)
                    mvv[i] = sv.v

                def emit_tail(i):
                    b = i % 2

                    @scalar_section
                    def _(s, i=i, b=b, mvr=mvv[i]):
                        s.wait_ge(sv.h, mvr)
                        ins = s.activation(out=lt[b][:], in_=mv[b][:, 1:2], func=AF.Ln,
                                           bias=eps_t[:], scale=1.0)
                        ss.tag(ins)
                        s.wait_ge(ss.h, ss.v)
                        ins = s.activation(out=rstd[b][:], in_=lt[b][:], func=AF.Exp,
                                           bias=zero_t[:], scale=-0.5)
                        ss.tag(ins)

                    rstd_ready = ss.v

                    @block.vector
                    def _(v, b=b, rstd_ready=rstd_ready):
                        v.wait_ge(ss.h, rstd_ready)
                        ins = v.scalar_tensor_tensor(out=nmr[b][:], in0=mv[b][:, 0:1],
                                                     scalar=-1.0, in1=rstd[b][:],
                                                     op0=ALU.mult, op1=ALU.mult)
                        sv.tag(ins)

                    nmr_ready = sv.v
                    pt0, pt1 = ptrd[0].v, ptrd[1].v

                    @scalar_section
                    def _(s, i=i, b=b, nmr_ready=nmr_ready, pt0=pt0, pt1=pt1):
                        s.wait_ge(sv.h, nmr_ready)
                        if i >= 1:
                            s.wait_ge(ptrd[0].h, pt0)
                            s.wait_ge(ptrd[1].h, pt1)
                        ins = s.activation(out=xtm[:], in_=xt[b][:], func=AF.Identity,
                                           bias=nmr[b][:], scale=rstd[b][:])
                        fr[b].tag(ins, 1)

                    for gidx in range(4):
                        gi = i * 4 + gidx
                        pb = gi % 2

                        @block.tensor
                        def _(t, i=i, b=b, gidx=gidx, gi=gi, pb=pb):
                            if gidx == 0:
                                t.wait_ge(fr[b].h, i // 2 + 1)
                                if i == 0:
                                    t.wait_ge(sv.h, const_ready)
                            if gi >= 2:
                                t.wait_ge(ptfr[pb].h, gi // 2)
                            for u in range(4):
                                dc = gidx * 4 + u
                                ins = t.transpose(pst[pb][:, u * 128:(u + 1) * 128],
                                                  xtm[:, dc * 128:(dc + 1) * 128], ident_b[:])
                            ptrd[pb].tag(ins, 1)

                        @scalar_section
                        def _(s, i=i, gidx=gidx, pb=pb, want=ptrd[pb].v):
                            s.wait_ge(ptrd[pb].h, want)
                            dstv = dst[:].rearrange("p (c t) -> p c t", t=T)[
                                :, gidx * 4:(gidx + 1) * 4, i * 128:(i + 1) * 128]
                            ins = s.copy(out=dstv, in_=pst[pb][:, 0:512])
                            ptfr[pb].tag(ins, 1)
                    tile_marks[i] = (ptfr[0].v, ptfr[1].v)

                emit_load(0)
                emit_load(1)
                for i in range(NT):
                    if i + 2 < NT:
                        emit_load(i + 2)
                    emit_front(i)
                    if i >= 1:
                        emit_tail(i - 1)
                emit_tail(NT - 1)

                return ptfr[0].v, ptfr[1].v, ptfr, tile_marks

        p0f0, p0f1, p0_ptfr, p0_marks = emit_ln_transpose("p0", x_h, False, xfm)

        def wait_phase0_done(eng):
            eng.wait_ge(p0_ptfr[0].h, p0f0)
            eng.wait_ge(p0_ptfr[1].h, p0f1)

        if dbg:
            @block.sync
            def _(q):
                wait_phase0_done(q)
                for dc in range(ND):
                    ins = q.dma_start(out=xfm_h[dc], in_=xfm[:, dc * T:(dc + 1) * T])
                    sq.tag(ins, 16)
        dbg_p0_dump = sq.v

        if nphases >= 2:
            with ExitStack() as px:
                xe = px.enter_context
                wst = [xe(nc.sbuf_tensor(f"p1w{i}", [128, D], F32)) for i in range(2)]
                wbt = [xe(nc.sbuf_tensor(f"p1b{i}", [128, D], BF16)) for i in range(2)]
                row = [xe(nc.sbuf_tensor(f"p1r{i}", [128, T], BF16)) for i in range(2)]
                bq_all = xe(nc.sbuf_tensor("p1bq", [128, ND], F32))
                psm = [xe(nc.psum_tensor(f"p1ps{i}", [128, 512], F32)) for i in range(4)]

                wld = [SemC(nc, ctx, f"p1wl{i}") for i in range(2)]
                bql = SemC(nc, ctx, "p1bql")
                rfr = [SemC(nc, ctx, f"p1rf{i}") for i in range(2)]
                pse = [SemC(nc, ctx, f"p1pe{i}") for i in range(2)]
                psw = [SemC(nc, ctx, f"p1pw{i}") for i in range(2)]

                @block.sync
                def _(q):
                    wait_phase0_done(q)
                    for oc in range(ND):
                        ins = q.dma_start(
                            out=bq_all[:, oc:oc + 1],
                            in_=bq_h[oc * 128:(oc + 1) * 128].rearrange("(p u) -> p u", u=1))
                        bql.tag(ins, 16)

                bq_ready = bql.v
                cast_hist = {}
                mmdone_hist = {}

                projs = [("q", wq_h, q_h), ("k", wk_h, k_h)]
                for pi, (pname, w_hbm, dst_hbm) in enumerate(projs):
                    for oc in range(ND):
                        it = pi * ND + oc
                        b = it % 2

                        @block.sync
                        def _(q, oc=oc, it=it, b=b, w_hbm=w_hbm):
                            if it >= 2:
                                q.wait_ge(sv.h, cast_hist[it - 2])
                            src = w_hbm[:, oc * 128:(oc + 1) * 128].rearrange(
                                "(c p) o -> p c o", p=128)
                            ins = q.dma_start(out=wst[b][:].rearrange(
                                "p (c o) -> p c o", o=128), in_=src)
                            wld[b].tag(ins, 16)

                        @block.vector
                        def _(v, it=it, b=b):
                            v.wait_ge(wld[b].h, 16 * (it // 2 + 1))
                            if it >= 2:
                                v.wait_ge(psw[0].h, mmdone_hist[it - 2])
                            ins = v.tensor_copy(out=wbt[b][:], in_=wst[b][:])
                            sv.tag(ins)

                        cast_hist[it] = sv.v
                        ev_hist = []

                        for gp in range(4):
                            g0p = gp * 2
                            pp = gp % 2

                            @block.tensor
                            def _(t, it=it, b=b, gp=gp, g0p=g0p, pp=pp,
                                  cast_val=cast_hist[it],
                                  evw=pse[pp].v):
                                if gp == 0:
                                    t.wait_ge(sv.h, cast_val)
                                if it == 0:
                                    t.wait_ge(p0_ptfr[0].h, p0_marks[8 * gp + 7][0])
                                    t.wait_ge(p0_ptfr[1].h, p0_marks[8 * gp + 7][1])
                                if evw:
                                    t.wait_ge(pse[pp].h, evw)
                                for dc in range(ND):
                                    for pb2 in range(2):
                                        g = g0p + pb2
                                        ins = t.matmul(
                                            psm[2 * pp + pb2][:],
                                            lhsT=wbt[b][:, dc * 128:(dc + 1) * 128],
                                            rhs=xfm[:, dc * T + g * 512:
                                                    dc * T + (g + 1) * 512],
                                            start=(dc == 0), stop=(dc == ND - 1))
                                psw[0].tag(ins, 1)

                            rb = it % 2
                            for pb2 in range(2):
                                g = g0p + pb2
                                if pi == 0:
                                    @block.vector
                                    def _(v, it=it, oc=oc, g=g, pb2=pb2, pp=pp,
                                          rb=rb, mmv=psw[0].v):
                                        v.wait_ge(psw[0].h, mmv)
                                        if g == 0:
                                            v.wait_ge(bql.h, bq_ready)
                                            if it >= 2:
                                                v.wait_ge(rfr[rb].h, 16 * (it // 2))
                                        ins = v.tensor_scalar(
                                            out=row[rb][:, g * 512:(g + 1) * 512],
                                            in0=psm[2 * pp + pb2][:],
                                            scalar1=bq_all[:, oc:oc + 1],
                                            scalar2=QSCALE, op0=ALU.add, op1=ALU.mult)
                                        pse[pp].tag(ins, 1)
                                    ev_hist.append((pse[pp], pse[pp].v))
                                else:
                                    @scalar_section
                                    def _(s, it=it, g=g, pb2=pb2, pp=pp, rb=rb,
                                          mmv=psw[0].v):
                                        s.wait_ge(psw[0].h, mmv)
                                        if g == 0 and it >= 2:
                                            s.wait_ge(rfr[rb].h, 16 * (it // 2))
                                        ins = s.copy(out=row[rb][:, g * 512:(g + 1) * 512],
                                                     in_=psm[2 * pp + pb2][:])
                                        pse[pp].tag(ins, 1)
                                    ev_hist.append((pse[pp], pse[pp].v))

                        mmdone_hist[it] = psw[0].v
                        e0 = max([v for (s_, v) in ev_hist if s_ is pse[0]], default=0)
                        e1 = max([v for (s_, v) in ev_hist if s_ is pse[1]], default=0)

                        @block.sync
                        def _(q, oc=oc, it=it, dst_hbm=dst_hbm, e0=e0, e1=e1):
                            if e0:
                                q.wait_ge(pse[0].h, e0)
                            if e1:
                                q.wait_ge(pse[1].h, e1)
                            ins = q.dma_start(out=dst_hbm[oc], in_=row[it % 2][:])
                            rfr[it % 2].tag(ins, 16)

                p1r0, p1r1 = rfr[0].v, rfr[1].v

            with ExitStack() as px:
                xe = px.enter_context
                wvst = [xe(nc.sbuf_tensor(f"v1w{i}", [128, 1024], F32)) for i in range(2)]
                wvb = xe(nc.sbuf_tensor("v1wb", [128, ND * 1024], BF16))
                vrow = [xe(nc.sbuf_tensor(f"v1r{i}", [128, 1024], BF16)) for i in range(2)]
                ps_v = [xe(nc.psum_tensor(f"v1p{i}", [128, 1024], F32)) for i in range(2)]
                vwl = [SemC(nc, ctx, f"v1wl{i}") for i in range(2)]
                vst = [SemC(nc, ctx, f"v1st{i}") for i in range(2)]
                vcast_hist = {}
                vevac_hist = {}
                vmm_half_end = {}

                for half in range(2):
                    f0h = half * 1024
                    for dc in range(ND):
                        wit = half * ND + dc
                        wb = wit % 2

                        @block.sync
                        def _(q, dc=dc, wb=wb, wit=wit, f0h=f0h,
                              prev=vcast_hist.get(wit - 2, 0)):
                            if wit < 2:
                                q.wait_ge(rfr[0].h, p1r0)
                                q.wait_ge(rfr[1].h, p1r1)
                            if prev:
                                q.wait_ge(ss.h, prev)
                            ins = q.dma_start(
                                out=wvst[wb][:],
                                in_=wv_h[dc * 128:(dc + 1) * 128, f0h:f0h + 1024])
                            vwl[wb].tag(ins, 16)

                        @scalar_section
                        def _(s, dc=dc, wb=wb, wit=wit, half=half,
                              mmprev=vmm_half_end.get(half - 1, 0)):
                            s.wait_ge(vwl[wb].h, 16 * (wit // 2 + 1))
                            if half == 1 and dc == 0:
                                s.wait_ge(stn.h, mmprev)
                            ins = s.copy(out=wvb[:, dc * 1024:(dc + 1) * 1024],
                                         in_=wvst[wb][:])
                            ss.tag(ins)

                        vcast_hist[wit] = ss.v

                    cast_done_h = ss.v
                    for i in range(NT):
                        vt = half * NT + i
                        sl = vt % 2

                        @block.tensor
                        def _(t, i=i, sl=sl, cast_done_h=cast_done_h,
                              ev=vevac_hist.get(vt - 2, 0)):
                            if i == 0:
                                t.wait_ge(ss.h, cast_done_h)
                            if ev:
                                t.wait_ge(ss.h, ev)
                            for dc in range(ND):
                                for fg in range(2):
                                    ins = t.matmul(
                                        ps_v[sl][:, fg * 512:(fg + 1) * 512],
                                        lhsT=xfm[:, dc * T + i * 128:dc * T + (i + 1) * 128],
                                        rhs=wvb[:, dc * 1024 + fg * 512:
                                                dc * 1024 + (fg + 1) * 512],
                                        start=(dc == 0), stop=(dc == ND - 1))
                            stn.tag(ins, 1)

                        vmm = stn.v

                        @scalar_section
                        def _(s, sl=sl, vt=vt, vmm=vmm, stprev=vst[sl].v):
                            s.wait_ge(stn.h, vmm)
                            if vt >= 2:
                                s.wait_ge(vst[sl].h, stprev)
                            ins = s.copy(out=vrow[sl][:], in_=ps_v[sl][:])
                            ss.tag(ins)

                        vevac_hist[vt] = ss.v

                        @block.sync
                        def _(q, i=i, sl=sl, half=half, ev=ss.v):
                            q.wait_ge(ss.h, ev)
                            ins = q.dma_start(
                                out=v2_h[i][:, half * 1024:(half + 1) * 1024],
                                in_=vrow[sl][:])
                            vst[sl].tag(ins, 16)

                    vmm_half_end[half] = stn.v

                p1v0, p1v1 = vst[0].v, vst[1].v

            def wait_p1_done(eng):
                eng.wait_ge(rfr[0].h, p1r0)
                eng.wait_ge(rfr[1].h, p1r1)
                eng.wait_ge(vst[0].h, p1v0)
                eng.wait_ge(vst[1].h, p1v1)
                if dbg and dbg_p0_dump:
                    eng.wait_ge(sq.h, dbg_p0_dump)
        else:
            def wait_p1_done(eng):
                wait_phase0_done(eng)

        xfm_cm.__exit__(None, None, None)

        if nphases >= 3:
            with ExitStack() as px:
                xe = px.enter_context
                kr = [xe(nc.sbuf_tensor(f"p2k{i}", [128, T], BF16)) for i in range(4)]
                qr = [xe(nc.sbuf_tensor(f"p2q{i}", [128, T], BF16)) for i in range(4)]
                vtm = [xe(nc.sbuf_tensor(f"p2v{i}", [128, T], BF16)) for i in range(4)]
                orow = [xe(nc.sbuf_tensor(f"p2o{i}", [128, T], BF16)) for i in range(4)]
                pT = [xe(nc.sbuf_tensor(f"p2p{i}", [128, 512], BF16)) for i in range(4)]
                lrb = [xe(nc.sbuf_tensor(f"p2lr{i}", [128, 512], F32)) for i in range(4)]
                lsb = [xe(nc.sbuf_tensor(f"p2ls{i}", [128, 512], F32)) for i in range(2)]
                qsq = xe(nc.sbuf_tensor("p2sq", [128, T], BF16))
                qn = xe(nc.sbuf_tensor("p2qn", [128, 1], F32))
                kn = xe(nc.sbuf_tensor("p2kn", [128, 1], F32))
                gt_t = xe(nc.sbuf_tensor("p2gt", [128, 1], F32))
                mhb = xe(nc.sbuf_tensor("p2mh", [128, NH], F32))
                wcv = [xe(nc.sbuf_tensor(f"p2wc{i}", [128, D], I32)) for i in range(4)]
                wcb = [xe(nc.sbuf_tensor(f"p2wb{i}", [128, D], BF16)) for i in range(2)]

                ps_s = [xe(nc.psum_tensor(f"p2ss{i}", [128, 512], F32)) for i in range(3)]
                ps_o = [xe(nc.psum_tensor(f"p2po{i}", [128, 512], F32)) for i in range(2)]
                ps_l = [xe(nc.psum_tensor(f"p2pl{i}", [128, 512], F32)) for i in range(2)]
                ps_t = xe(nc.psum_tensor("p2pt", [128, 512], F32))

                rowld = [SemC(nc, ctx, f"p2rl{i}") for i in range(4)]
                ost = [SemC(nc, ctx, f"p2os{i}") for i in range(4)]
                wcl = [SemC(nc, ctx, f"p2wl{i}") for i in range(4)]
                wcs = [SemC(nc, ctx, f"p2ws{i}") for i in range(2)]

                head_te_end = {}
                wconv_hist = {}
                exp_hist = {}
                lmm_hist = {}
                last_oevac = {0: 0, 1: 0}
                last_lcopy = {0: 0, 1: 0}
                last_lrecip = {0: 0, 1: 0}
                last_norm = {}
                mhb_marker = {}
                tap_end = {}
                tap_red = [0]
                tap_tasks = []
                jg = [0]

                def bound_tasks(hh):
                    sl2 = hh % 4
                    ldv = 96 * (hh // 4 + 1)
                    st = {"sq_v": 0, "lastmm": 0}

                    def t_sq(which):
                        def run():
                            src = qr if which == 0 else kr

                            @block.vector
                            def _(v, which=which, sl2=sl2, ldv=ldv, src=src,
                                  tprev=(tap_end.get(hh - 1, 0) if hh else 0),
                                  mprev=(mhb_marker.get(hh - 1, 0)
                                         if which == 0 else 0),
                                  lastmm=st["lastmm"]):
                                v.wait_ge(rowld[sl2].h, ldv)
                                if which == 0 and tprev:
                                    v.wait_ge(stn.h, tprev)
                                if mprev:
                                    v.wait_ge(sv.h, mprev)
                                if which == 1:
                                    v.wait_ge(stn.h, lastmm)
                                ins = v.tensor_tensor(out=qsq[:], in0=src[sl2][:],
                                                      in1=src[sl2][:], op=ALU.mult)
                                sv.tag(ins)

                            st["sq_v"] = sv.v
                        return run

                    def t_mm(which, g8):
                        def run():
                            dstn = qn if which == 0 else kn

                            @block.tensor
                            def _(t, g8=g8, sq_v=st["sq_v"], hh=hh,
                                  which=which, warv=tap_red[0]):
                                if g8 == 0:
                                    t.wait_ge(sv.h, sq_v)
                                    if hh == 0 and which == 0:
                                        t.wait_ge(sv.h, const_ready)
                                if warv:
                                    t.wait_ge(sv.h, warv)
                                ins = t.matmul(ps_t[:], lhsT=ones_bb[:],
                                               rhs=qsq[:, g8 * 512:(g8 + 1) * 512],
                                               start=True, stop=True)
                                stn.tag(ins, 1)

                            mm_v = stn.v
                            st["lastmm"] = mm_v

                            @block.vector
                            def _(v, g8=g8, mm_v=mm_v, dstn=dstn,
                                  gtprev=tap_red[0]):
                                v.wait_ge(stn.h, mm_v)
                                if g8 == 0:
                                    ins = v.tensor_reduce(out=dstn[:], in_=ps_t[:],
                                                          axis=AX.X, op=ALU.max)
                                    sv.tag(ins)
                                else:
                                    if gtprev:
                                        v.wait_ge(sv.h, gtprev)
                                    ins = v.tensor_reduce(out=gt_t[:], in_=ps_t[:],
                                                          axis=AX.X, op=ALU.max)
                                    sv.tag(ins)
                                    v.wait_ge(sv.h, sv.v)
                                    ins = v.tensor_tensor(out=dstn[:], in0=dstn[:],
                                                          in1=gt_t[:], op=ALU.max)
                                    sv.tag(ins)

                            tap_red[0] = sv.v
                        return run

                    def t_fin():
                        def run():
                            @block.vector
                            def _(v, lastred=sv.v):
                                v.wait_ge(sv.h, lastred)
                                ins = v.tensor_tensor(out=qn[:], in0=qn[:], in1=kn[:],
                                                      op=ALU.mult)
                                sv.tag(ins)

                            m2_v = sv.v

                            @scalar_section
                            def _(s, m2_v=m2_v):
                                s.wait_ge(sv.h, m2_v)
                                ins = s.activation(out=gt_t[:], in_=qn[:], func=AF.Ln,
                                                   bias=zero_t[:], scale=1.0)
                                ss.tag(ins)
                                s.wait_ge(ss.h, ss.v)
                                ins = s.activation(out=gt_t[:], in_=gt_t[:],
                                                   func=AF.Exp, bias=zero_t[:],
                                                   scale=0.5)
                                ss.tag(ins)

                            sqrt_v = ss.v

                            @block.vector
                            def _(v, hh=hh, sqrt_v=sqrt_v):
                                v.wait_ge(ss.h, sqrt_v)
                                ins = v.tensor_scalar(out=mhb[:, hh:hh + 1],
                                                      in0=gt_t[:],
                                                      scalar1=-1.01, scalar2=None,
                                                      op0=ALU.mult, op1=ALU.bypass)
                                sv.tag(ins)

                            mhb_marker[hh] = sv.v
                            tap_end[hh] = stn.v
                        return run

                    seq = [t_sq(0)]
                    seq += [t_mm(0, g8) for g8 in range(8)]
                    seq += [t_sq(1)]
                    seq += [t_mm(1, g8) for g8 in range(8)]
                    seq += [t_fin()]
                    return seq

                def emit_loads(hh):
                    sl = hh % 4

                    @block.sync
                    def _(q, hh=hh, sl=sl):
                        if hh < 4:
                            wait_p1_done(q)
                        else:
                            q.wait_ge(stn.h, head_te_end[hh - 4])
                        for src, buf in ((q_h[hh], qr), (k_h[hh], kr)):
                            ins = q.dma_start(out=buf[sl][:], in_=src)
                            rowld[sl].tag(ins, 16)
                        for qu in range(4):
                            ins = q.dma_start(
                                out=vtm[sl][:, qu * 1024:(qu + 1) * 1024].rearrange(
                                    "p (c f) -> p c f", f=128),
                                in_=v2_h.rearrange("c p f -> p c f")[
                                    :, qu * 8:(qu + 1) * 8, hh * 128:(hh + 1) * 128])
                            rowld[sl].tag(ins, 16)

                for hh0 in range(4):
                    emit_loads(hh0)

                def emit_S(h, g, kt):
                    hs = h % 4
                    j = jg[0]
                    jg[0] += 1
                    bank = j % 3
                    slot = j % 4
                    r = kt - 4 * g
                    q0 = r * 128 if r > 0 else 0
                    first_exp = (g == 0 and kt == 0)

                    @block.tensor
                    def _(t, h=h, hs=hs, g=g, kt=kt, bank=bank, q0=q0,
                          ldv=96 * (h // 4 + 1),
                          bfree=exp_hist.get(bank, 0)):
                        if g == 0 and kt == 0:
                            t.wait_ge(rowld[hs].h, ldv)
                            if h == 0:
                                t.wait_ge(sv.h, const_ready)
                        if bfree:
                            t.wait_ge(ss.h, bfree)
                        ins = t.matmul(
                            ps_s[bank][:, q0:512],
                            lhsT=kr[hs][:, kt * 128:(kt + 1) * 128],
                            rhs=qr[hs][:, g * 512 + q0:(g + 1) * 512],
                            start=True, stop=True)
                        stn.tag(ins, 1)

                    s_v = stn.v
                    gate_v = None
                    if r >= 0:
                        @block.vector
                        def _(v, bank=bank, r=r, s_v=s_v):
                            v.wait_ge(stn.h, s_v)
                            ins = v.tensor_tensor(
                                out=ps_s[bank][:, r * 128:r * 128 + 128],
                                in0=ps_s[bank][:, r * 128:r * 128 + 128],
                                in1=mask_T[:], op=ALU.add)
                            sv.tag(ins)

                        gate_v = sv.v
                    if r > 0:
                        @block.vector
                        def _(v, slot=slot, r=r,
                              pfree=lmm_hist.get(slot, 0)):
                            if pfree:
                                v.wait_ge(stn.h, pfree)
                            ins = v.memset(pT[slot][:, 0:r * 128], 0.0)
                            sv.tag(ins)

                        gate_v = sv.v

                    @scalar_section
                    def _(s, h=h, bank=bank, slot=slot, q0=q0, r=r,
                          s_v=s_v, gate_v=gate_v,
                          pfree=lmm_hist.get(slot, 0),
                          mh_v=(mhb_marker[h] if first_exp else 0)):
                        if gate_v is not None:
                            s.wait_ge(sv.h, gate_v)
                        else:
                            s.wait_ge(stn.h, s_v)
                        if r <= 0 and pfree:
                            s.wait_ge(stn.h, pfree)
                        if mh_v:
                            s.wait_ge(sv.h, mh_v)
                        ins = s.activation(out=pT[slot][:, q0:512],
                                           in_=ps_s[bank][:, q0:512],
                                           func=AF.Exp, bias=mhb[:, h:h + 1],
                                           scale=1.0)
                        ss.tag(ins)

                    exp_hist[bank] = ss.v
                    return dict(h=h, g=g, kt=kt, slot=slot, exp_v=ss.v)

                def emit_PVl(info):
                    h = info["h"]
                    hs = h % 4
                    par = h % 2
                    g = info["g"]
                    kt = info["kt"]
                    slot = info["slot"]

                    @block.tensor
                    def _(t, hs=hs, par=par, g=g, kt=kt, slot=slot,
                          exp_v=info["exp_v"],
                          ofree=(last_oevac[par] if kt == 0 else 0),
                          lfree=(last_lcopy[par] if kt == 0 else 0)):
                        t.wait_ge(ss.h, exp_v)
                        if kt == 0 and ofree:
                            t.wait_ge(sv.h, ofree)
                        ins = t.matmul(ps_o[par][:],
                                       lhsT=vtm[hs][:, kt * 128:(kt + 1) * 128],
                                       rhs=pT[slot][:],
                                       start=(kt == 0), stop=(kt == 4 * g + 3))
                        stn.tag(ins, 1)
                        if kt == 0 and lfree:
                            t.wait_ge(sv.h, lfree)
                        ins = t.matmul(ps_l[par][:], lhsT=ones_bb[:],
                                       rhs=pT[slot][:],
                                       start=(kt == 0), stop=(kt == 4 * g + 3))
                        stn.tag(ins, 1)

                    lmm_hist[slot] = stn.v

                def emit_epilogue(g, h):
                    hs = h % 4
                    par = h % 2
                    grp_done = stn.v
                    lslot = 2 * par + g % 2

                    @block.vector
                    def _(v, h=h, hs=hs, par=par, g=g, grp_done=grp_done,
                          ostv=(16 * (h // 4) if (g <= 1 and h >= 4) else 0)):
                        v.wait_ge(stn.h, grp_done)
                        if ostv:
                            v.wait_ge(ost[hs].h, ostv)
                        ins = v.tensor_copy(
                            out=orow[hs][:, g * 512:(g + 1) * 512],
                            in_=ps_o[par][:])
                        sv.tag(ins)

                    last_oevac[par] = sv.v

                    @block.vector
                    def _(v, par=par, lrd=last_lrecip[par]):
                        if lrd:
                            v.wait_ge(sv.h, lrd)
                        ins = v.tensor_copy(out=lsb[par][:], in_=ps_l[par][:])
                        sv.tag(ins)

                    last_lcopy[par] = sv.v
                    cpl_v = sv.v

                    @block.vector
                    def _(v, par=par, lslot=lslot, cpl_v=cpl_v,
                          nwar=last_norm.get(lslot, 0)):
                        v.wait_ge(sv.h, cpl_v)
                        if nwar:
                            v.wait_ge(sv.h, nwar)
                        ins = v.reciprocal(out=lrb[lslot][:], in_=lsb[par][:])
                        sv.tag(ins)

                    last_lrecip[par] = sv.v
                    rec_v = sv.v

                    @block.vector
                    def _(v, hs=hs, g=g, lslot=lslot, rec_v=rec_v):
                        v.wait_ge(sv.h, rec_v)
                        ins = v.tensor_tensor(
                            out=orow[hs][:, g * 512:(g + 1) * 512],
                            in0=orow[hs][:, g * 512:(g + 1) * 512],
                            in1=lrb[lslot][:], op=ALU.mult)
                        sv.tag(ins)

                    last_norm[lslot] = sv.v

                    if g == 3 and h + 2 < NH:
                        tap_tasks.extend(bound_tasks(h + 2))

                for hp in range(NH // 2):
                    hA = 2 * hp
                    hB = hA + 1

                    def emit_conv_load(ic, gate):
                        sl4 = ic % 4

                        @block.sync
                        def _(q, ic=ic, sl4=sl4, gate=gate):
                            if gate:
                                q.wait_ge(sv.h, gate)
                            ins = q.dma_start(out=wcv[sl4][:],
                                              in_=wdT_h[ic * 128:(ic + 1) * 128, :])
                            wcl[sl4].tag(ins, 16)

                    for u in range(4):
                        emit_conv_load(hp * 8 + u, 0)

                    for u in range(8):
                        ic = hp * 8 + u

                        def conv_task(ic=ic):
                            sl4 = ic % 4

                            @block.vector
                            def _(v, ic=ic, sl4=sl4):
                                v.wait_ge(wcl[sl4].h, 16 * (ic // 4 + 1))
                                if ic >= 2:
                                    v.wait_ge(wcs[ic % 2].h, 16 * (ic // 2))
                                ins = v.tensor_scalar(out=wcb[ic % 2][:],
                                                      in0=wcv[sl4][:],
                                                      scalar1=-128.0, scalar2=None,
                                                      op0=ALU.add, op1=ALU.bypass)
                                sv.tag(ins)

                            wconv_hist[ic] = sv.v

                            @block.gpsimd
                            def _(g, ic=ic, cv=sv.v):
                                g.wait_ge(sv.h, cv)
                                ins = g.dma_start(out=wdb_h[ic], in_=wcb[ic % 2][:])
                                wcs[ic % 2].tag(ins, 16)

                            if ic % 8 < 4:
                                emit_conv_load(ic + 4, sv.v)

                        tap_tasks.append(conv_task)

                    if hp == 0:
                        for _t in bound_tasks(0):
                            _t()
                        for _t in bound_tasks(1):
                            _t()

                    pend = []

                    def flush_one():
                        old = pend.pop(0)
                        emit_PVl(old)
                        if old["kt"] == 4 * old["g"] + 3:
                            emit_epilogue(old["g"], old["h"])
                        if tap_tasks:
                            tap_tasks.pop(0)()

                    seqA = [(hA, g, kt) for g in range(8) for kt in range(4 * g + 4)]
                    seqB = [(hB, g, kt) for g in range(8) for kt in range(4 * g + 4)]
                    stream = []
                    for a2, b2 in zip(seqA, seqB):
                        stream.append(a2)
                        stream.append(b2)
                    for (h2, g2, kt2) in stream:
                        pend.append(emit_S(h2, g2, kt2))
                        if len(pend) > 2:
                            flush_one()
                    while pend:
                        flush_one()
                    while tap_tasks:
                        tap_tasks.pop(0)()
                    head_te_end[hA] = stn.v
                    head_te_end[hB] = stn.v
                    if hA + 4 < NH:
                        emit_loads(hA + 4)
                        emit_loads(hB + 4)

                    for h2 in (hA, hB):
                        @block.sync
                        def _(q, h2=h2, ov=sv.v):
                            q.wait_ge(sv.h, ov)
                            ins = q.dma_start(out=o_h[h2], in_=orow[h2 % 4][:])
                            ost[h2 % 4].tag(ins, 16)

                p2ost = [ost[i].v for i in range(4)]
                p2w0, p2w1 = wcs[0].v, wcs[1].v

                def wait_p2_done(eng):
                    for i in range(4):
                        eng.wait_ge(ost[i].h, p2ost[i])
                    eng.wait_ge(wcs[0].h, p2w0)
                    eng.wait_ge(wcs[1].h, p2w1)
        else:
            wait_p2_done = wait_p1_done

        if nphases >= 4:
            with ExitStack() as px:
                xe = px.enter_context
                wst2 = [xe(nc.sbuf_tensor(f"p5w{i}", [128, D], F32)) for i in range(4)]
                w_ob = xe(nc.sbuf_tensor("p5wo", [128, ND * D], BF16))
                bv_all = xe(nc.sbuf_tensor("p5bv", [128, ND], F32))
                bo_row = xe(nc.sbuf_tensor("p5bo", [1, D], F32))
                beff = xe(nc.sbuf_tensor("p5be", [1, D], F32))
                beffb = xe(nc.sbuf_tensor("p5bb", [128, D], F32))
                oti = [xe(nc.sbuf_tensor(f"p5o{i}", [128, D], BF16)) for i in range(2)]
                xt2 = [xe(nc.sbuf_tensor(f"p5x{i}", [128, D], F32)) for i in range(3)]
                x1t = [xe(nc.sbuf_tensor(f"p5y{i}", [128, D], F32)) for i in range(2)]

                wl5 = [SemC(nc, ctx, f"p5wl{i}") for i in range(4)]
                bvl = SemC(nc, ctx, "p5bvl")
                otl = [SemC(nc, ctx, f"p5ot{i}") for i in range(2)]
                xl5 = [SemC(nc, ctx, f"p5xl{i}") for i in range(3)]
                x1s = [SemC(nc, ctx, f"p5x1{i}") for i in range(2)]

                with ExitStack() as bectx:
                    ps_be = bectx.enter_context(nc.psum_tensor("p5pbe", [1, D], F32))

                    @block.sync
                    def _(q):
                        wait_p2_done(q)
                        for c in range(ND):
                            ins = q.dma_start(
                                out=bv_all[:, c:c + 1],
                                in_=bv_h[c * 128:(c + 1) * 128].rearrange("(p u) -> p u", u=1))
                            bvl.tag(ins, 16)
                        ins = q.dma_start(out=bo_row[:], in_=bo_h.rearrange("(u d) -> u d", u=1))
                        bvl.tag(ins, 16)

                    bv_ready = bvl.v
                    cast5 = {}

                    for c in range(ND):
                        b = c % 4
                        qeng = block.sync if c % 2 == 0 else block.gpsimd

                        @qeng
                        def _(q, c=c, b=b, prev=(cast5.get(c - 4))):
                            if c < 2:
                                wait_p2_done(q)
                            if prev is not None:
                                q.wait_ge(stn.h, prev[0])
                                q.wait_ge(sv.h, prev[1])
                            src = wo_h[c * 128:(c + 1) * 128, :]
                            ins = q.dma_start(out=wst2[b][:], in_=src)
                            wl5[b].tag(ins, 16)

                        @block.tensor
                        def _(t, c=c, b=b, bv_ready=bv_ready):
                            t.wait_ge(wl5[b].h, 16 * (c // 4 + 1))
                            if c == 0:
                                t.wait_ge(bvl.h, bv_ready)
                            for oc4 in range(4):
                                ins = t.matmul(ps_be[:, oc4 * 512:(oc4 + 1) * 512],
                                               lhsT=bv_all[:, c:c + 1],
                                               rhs=wst2[b][:, oc4 * 512:(oc4 + 1) * 512],
                                               start=(c == 0), stop=(c == ND - 1))
                            stn.tag(ins, 1)

                        bemm = stn.v

                        @block.vector
                        def _(v, c=c, b=b, bemm=bemm):
                            v.wait_ge(stn.h, bemm)
                            ins = v.tensor_copy(out=w_ob[:, c * D:(c + 1) * D], in_=wst2[b][:])
                            sv.tag(ins)

                        cast5[c] = (bemm, sv.v)

                    be_done = stn.v
                    wob_done = sv.v

                    @scalar_section
                    def _(s, be_done=be_done):
                        s.wait_ge(stn.h, be_done)
                        ins = s.copy(out=beff[:], in_=ps_be[:])
                        ss.tag(ins)

                    @block.vector
                    def _(v, e=ss.v, bv_ready=bv_ready):
                        v.wait_ge(ss.h, e)
                        v.wait_ge(bvl.h, bv_ready)
                        ins = v.tensor_tensor(out=beff[:], in0=beff[:], in1=bo_row[:],
                                              op=ALU.add)
                        sv.tag(ins)

                    beff_done = sv.v

                ps_op = [xe(nc.psum_tensor(f"p5pop{i}", [128, D], F32)) for i in range(2)]

                @block.tensor
                def _(t, beff_done=beff_done):
                    t.wait_ge(sv.h, beff_done)
                    for oc4 in range(4):
                        ins = t.matmul(ps_op[1][:, oc4 * 512:(oc4 + 1) * 512],
                                       lhsT=ones1[0:1, :],
                                       rhs=beff[:, oc4 * 512:(oc4 + 1) * 512],
                                       start=True, stop=True)
                    stn.tag(ins, 1)

                bcast_mm5 = stn.v

                @scalar_section
                def _(s, bcast_mm5=bcast_mm5):
                    s.wait_ge(stn.h, bcast_mm5)
                    ins = s.copy(out=beffb[:], in_=ps_op[1][:])
                    ss.tag(ins)

                beffb_done = ss.v

                osb2 = [xe(nc.sbuf_tensor(f"p5ob{i}", [128, D], F32)) for i in range(2)]
                oln2 = [xe(nc.sbuf_tensor(f"p5olx{i}", [128, D], F32)) for i in range(2)]
                st5b = [xe(nc.sbuf_tensor(f"p5stx{i}", [128, 24], F32)) for i in range(2)]
                mv5b = [xe(nc.sbuf_tensor(f"p5mvx{i}", [128, 2], F32)) for i in range(2)]
                rs5b = [xe(nc.sbuf_tensor(f"p5rsx{i}", [128, 1], F32)) for i in range(2)]
                nm5b = [xe(nc.sbuf_tensor(f"p5nmx{i}", [128, 1], F32)) for i in range(2)]
                lt5b = [xe(nc.sbuf_tensor(f"p5ltx{i}", [128, 1], F32)) for i in range(2)]

                evop_hist = {}
                mm_hist5 = {}
                x1rd_hist = {}
                mvv5 = {}

                def p5_load_o(i):
                    b = i % 2

                    @block.sync
                    def _(q, i=i, b=b, mmprev=mm_hist5.get(i - 2, 0)):
                        if i >= 2:
                            q.wait_ge(stn.h, mmprev)
                        src = o_h.rearrange("c p t -> p c t")[:, :, i * 128:(i + 1) * 128]
                        ins = q.dma_start(out=oti[b][:].rearrange("p (c t) -> p c t", t=128),
                                          in_=src)
                        otl[b].tag(ins, 16)

                def p5_load_x(i):
                    b3 = i % 3

                    @block.sync
                    def _(q, i=i, b3=b3, xprev=x1rd_hist.get(i - 3, 0)):
                        if i >= 3:
                            q.wait_ge(sv.h, xprev)
                        ins = q.dma_start(out=xt2[b3][:], in_=x_h[i * 128:(i + 1) * 128, :])
                        xl5[b3].tag(ins, 16)

                def p5_mm(i):
                    b = i % 2

                    @block.tensor
                    def _(t, i=i, b=b, wob_done=wob_done, beffb_done=beffb_done,
                          evprev=evop_hist.get(i - 2, 0)):
                        t.wait_ge(otl[b].h, 16 * (i // 2 + 1))
                        if i == 0:
                            t.wait_ge(sv.h, wob_done)
                            t.wait_ge(ss.h, beffb_done)
                        if i == 1:
                            t.wait_ge(ss.h, beffb_done)
                        if i >= 2:
                            t.wait_ge(sv.h, evprev)
                        for c in range(ND):
                            for oc4 in range(4):
                                ins = t.matmul(
                                    ps_op[b][:, oc4 * 512:(oc4 + 1) * 512],
                                    lhsT=oti[b][:, c * 128:(c + 1) * 128],
                                    rhs=w_ob[:, c * D + oc4 * 512:c * D + (oc4 + 1) * 512],
                                    start=(c == 0), stop=(c == ND - 1))
                        stn.tag(ins, 1)
                    mm_hist5[i] = stn.v

                def p5_front(i):
                    b = i % 2

                    @block.vector
                    def _(v, i=i, b=b, opmm=mm_hist5[i], beffb_done=beffb_done):
                        v.wait_ge(stn.h, opmm)
                        if i == 0:
                            v.wait_ge(ss.h, beffb_done)
                        ins = v.scalar_tensor_tensor(out=osb2[b][:], in0=ps_op[b][:],
                                                     scalar=1.0, in1=beffb[:],
                                                     op0=ALU.mult, op1=ALU.add)
                        sv.tag(ins)
                        v.wait_ge(sv.h, sv.v)
                        for c in range(4):
                            ins = v.bn_stats(out=st5b[b][:, c * 6:(c + 1) * 6],
                                             in_=osb2[b][:, c * 512:(c + 1) * 512])
                        sv.tag(ins)
                        v.wait_ge(sv.h, sv.v)
                        ins = v.bn_aggr(out=mv5b[b][:], in_=st5b[b][:])
                        sv.tag(ins)
                    evop_hist[i] = sv.v - 2
                    mvv5[i] = sv.v

                def p5_tail(i):
                    b = i % 2

                    @scalar_section
                    def _(s, b=b, mvr=mvv5[i]):
                        s.wait_ge(sv.h, mvr)
                        ins = s.activation(out=lt5b[b][:], in_=mv5b[b][:, 1:2], func=AF.Ln,
                                           bias=eps_t[:], scale=1.0)
                        ss.tag(ins)
                        s.wait_ge(ss.h, ss.v)
                        ins = s.activation(out=rs5b[b][:], in_=lt5b[b][:], func=AF.Exp,
                                           bias=zero_t[:], scale=-0.5)
                        ss.tag(ins)

                    rstd_ready5 = ss.v

                    @block.vector
                    def _(v, b=b, rstd_ready5=rstd_ready5):
                        v.wait_ge(ss.h, rstd_ready5)
                        ins = v.scalar_tensor_tensor(out=nm5b[b][:], in0=mv5b[b][:, 0:1],
                                                     scalar=-1.0, in1=rs5b[b][:],
                                                     op0=ALU.mult, op1=ALU.mult)
                        sv.tag(ins)

                    nmr_ready5 = sv.v

                    @scalar_section
                    def _(s, b=b, nmr_ready5=nmr_ready5):
                        s.wait_ge(sv.h, nmr_ready5)
                        ins = s.activation(out=oln2[b][:], in_=osb2[b][:], func=AF.Identity,
                                           bias=nm5b[b][:], scale=rs5b[b][:])
                        ss.tag(ins)

                    oln_ready = ss.v

                    @block.vector
                    def _(v, i=i, b=b, oln_ready=oln_ready, x1prev=16 * (i // 2)):
                        v.wait_ge(ss.h, oln_ready)
                        v.wait_ge(xl5[i % 3].h, 16 * (i // 3 + 1))
                        if i >= 2:
                            v.wait_ge(x1s[b].h, x1prev)
                        ins = v.tensor_tensor(out=x1t[b][:], in0=xt2[i % 3][:],
                                              in1=oln2[b][:], op=ALU.add)
                        sv.tag(ins)

                    x1_ready = sv.v
                    x1rd_hist[i] = x1_ready

                    @block.sync
                    def _(q, i=i, b=b, x1_ready=x1_ready):
                        q.wait_ge(sv.h, x1_ready)
                        ins = q.dma_start(out=x1_h[i], in_=x1t[b][:])
                        x1s[b].tag(ins, 16)

                p5_load_o(0)
                p5_load_x(0)
                p5_load_o(1)
                p5_load_x(1)
                p5_load_x(2)
                for i in range(NT):
                    p5_mm(i)
                    if i + 2 < NT:
                        p5_load_o(i + 2)
                    p5_front(i)
                    if i >= 1:
                        p5_tail(i - 1)
                        if i + 2 < NT:
                            p5_load_x(i + 2)
                p5_tail(NT - 1)

                p5a, p5b = x1s[0].v, x1s[1].v

                def wait_p5_done(eng):
                    eng.wait_ge(x1s[0].h, p5a)
                    eng.wait_ge(x1s[1].h, p5b)
        else:
            wait_p5_done = wait_p2_done

        if nphases >= 5:
            xfm2_cm = nc.sbuf_tensor("xfm2", [128, ND * T], BF16)
            xfm2 = xfm2_cm.__enter__()
            f0a, f0b, f0_ptfr, _f0m = emit_ln_transpose("f0", x1_h, True, xfm2,
                                                  gate=wait_p5_done)

            if dbg:
                @block.sync
                def _(q):
                    q.wait_ge(f0_ptfr[0].h, f0a)
                    q.wait_ge(f0_ptfr[1].h, f0b)
                    for dc in range(ND):
                        ins = q.dma_start(out=xfm_h[dc], in_=xfm2[:, dc * T:(dc + 1) * T])
                        sq.tag(ins, 16)
            dbg_f0_dump = sq.v

            def wait_f0_done(eng):
                eng.wait_ge(f0_ptfr[0].h, f0a)
                eng.wait_ge(f0_ptfr[1].h, f0b)
                if dbg and dbg_f0_dump:
                    eng.wait_ge(sq.h, dbg_f0_dump)
        else:
            wait_f0_done = wait_p5_done

        if nphases >= 6:
            with ExitStack() as px:
                xe = px.enter_context
                gst = [xe(nc.sbuf_tensor(f"f1g{i}", [128, D], I32)) for i in range(2)]
                ust = [xe(nc.sbuf_tensor(f"f1u{i}", [128, D], I32)) for i in range(2)]
                gwb = [xe(nc.sbuf_tensor(f"f1gb{i}", [128, D], BF16)) for i in range(2)]
                uwb = [xe(nc.sbuf_tensor(f"f1ub{i}", [128, D], BF16)) for i in range(2)]
                sgb = [xe(nc.sbuf_tensor(f"f1sg{i}", [128, D], BF16)) for i in range(2)]
                hbf = [xe(nc.sbuf_tensor(f"f1h{i}", [128, D], BF16)) for i in range(2)]
                gsc = [xe(nc.sbuf_tensor(f"f1gs{i}", [128, 1], F32)) for i in range(2)]
                usc = [xe(nc.sbuf_tensor(f"f1us{i}", [128, 1], F32)) for i in range(2)]
                ps_g = xe(nc.psum_tensor("f1pg", [128, D], F32))
                ps_u = xe(nc.psum_tensor("f1pu", [128, D], F32))

                gl = [SemC(nc, ctx, f"f1gl{i}") for i in range(2)]
                scl = [SemC(nc, ctx, f"f1sc{i}") for i in range(2)]
                hst = [SemC(nc, ctx, f"f1hs{i}") for i in range(2)]

                deq_hist = {}
                gmm_hist = {}
                umm_hist = {}
                silu_hist = {}
                hstt_hist = {}

                def f1_load(ic):
                    b = ic % 2

                    @block.sync
                    def _(q, ic=ic, b=b, dqprev=deq_hist.get(ic - 2, 0),
                          husers=hstt_hist.get((ic - 2, 1), 0)):
                        if ic == 0:
                            wait_f0_done(q)
                        if ic >= 2:
                            q.wait_ge(ss.h, dqprev)
                            q.wait_ge(sv.h, husers)
                        for w_hbm, dstt, sem in ((wgT_h, gst, gl), (wuT_h, ust, gl)):
                            src = w_hbm[:, ic * 128:(ic + 1) * 128].rearrange(
                                "(c p) o -> p c o", p=128)
                            ins = q.dma_start(out=dstt[b][:].rearrange(
                                "p (c o) -> p c o", o=128), in_=src)
                            sem[b].tag(ins, 16)
                        for v_hbm, dstt in ((gs_h, gsc), (us_h, usc)):
                            ins = q.dma_start(
                                out=dstt[b][:],
                                in_=v_hbm[ic * 128:(ic + 1) * 128].rearrange("(p u) -> p u", u=1))
                            scl[b].tag(ins, 16)

                def f1_deq(ic):
                    b = ic % 2

                    @scalar_section
                    def _(s, ic=ic, b=b, mmprev=(gmm_hist.get((ic - 2, 1), 0),
                                                 umm_hist.get((ic - 2, 1), 0))):
                        s.wait_ge(gl[b].h, 32 * (ic // 2 + 1))
                        if ic >= 2:
                            s.wait_ge(stn.h, max(mmprev))
                        ins = s.activation(out=gwb[b][:], in_=gst[b][:], func=AF.Copy,
                                           bias=-128.0, scale=1.0)
                        ss.tag(ins)
                        ins = s.activation(out=uwb[b][:], in_=ust[b][:], func=AF.Copy,
                                           bias=-128.0, scale=1.0)
                        ss.tag(ins)

                    deq_hist[ic] = ss.v

                f1_load(0)
                f1_load(1)
                f1_deq(0)
                for ic in range(NI):
                    b = ic % 2
                    for half in range(2):
                        t0h = half * 2048

                        @block.tensor
                        def _(t, ic=ic, b=b, half=half, t0h=t0h, dq=deq_hist[ic],
                              sl=silu_hist.get((ic - 1, 1) if half == 0 else (ic, 0), 0)):
                            if half == 0:
                                t.wait_ge(ss.h, dq)
                            prevg = silu_hist.get((ic, half - 1), silu_hist.get((ic - 1, 1), 0))
                            t.wait_ge(ss.h, prevg)
                            for d in range(ND):
                                for bk in range(4):
                                    ins = t.matmul(
                                        ps_g[:, bk * 512:(bk + 1) * 512],
                                        lhsT=gwb[b][:, d * 128:(d + 1) * 128],
                                        rhs=xfm2[:, d * T + t0h + bk * 512:
                                                 d * T + t0h + (bk + 1) * 512],
                                        start=(d == 0), stop=(d == ND - 1))
                            stn.tag(ins, 1)

                        gmm_hist[(ic, half)] = stn.v

                        @scalar_section
                        def _(s, ic=ic, b=b, half=half, gmm=gmm_hist[(ic, half)],
                              scv=16 * (ic // 2 + 1) * 2,
                              sgprev=hstt_hist.get((ic - 1, half), 0)):
                            s.wait_ge(stn.h, gmm)
                            if half == 0:
                                s.wait_ge(scl[b].h, scv)
                            if ic >= 1:
                                s.wait_ge(sv.h, sgprev)
                            ins = s.activation(out=sgb[half][:], in_=ps_g[:], func=AF.Silu,
                                               bias=zero_t[:], scale=gsc[b][:])
                            ss.tag(ins)

                        silu_hist[(ic, half)] = ss.v

                        @block.tensor
                        def _(t, ic=ic, b=b, half=half, t0h=t0h,
                              prevu=hstt_hist.get((ic, half - 1),
                                                  hstt_hist.get((ic - 1, 1), 0))):
                            t.wait_ge(sv.h, prevu)
                            for d in range(ND):
                                for bk in range(4):
                                    ins = t.matmul(
                                        ps_u[:, bk * 512:(bk + 1) * 512],
                                        lhsT=uwb[b][:, d * 128:(d + 1) * 128],
                                        rhs=xfm2[:, d * T + t0h + bk * 512:
                                                 d * T + t0h + (bk + 1) * 512],
                                        start=(d == 0), stop=(d == ND - 1))
                            stn.tag(ins, 1)

                        umm_hist[(ic, half)] = stn.v

                        @block.vector
                        def _(v, ic=ic, b=b, half=half, umm=umm_hist[(ic, half)],
                              silu_v=silu_hist[(ic, half)],
                              hprev=hst[half].v):
                            v.wait_ge(stn.h, umm)
                            v.wait_ge(ss.h, silu_v)
                            if ic >= 1:
                                v.wait_ge(hst[half].h, hprev)
                            ins = v.scalar_tensor_tensor(out=hbf[half][:], in0=ps_u[:],
                                                         scalar=usc[b][:], in1=sgb[half][:],
                                                         op0=ALU.mult, op1=ALU.mult)
                            sv.tag(ins)

                        hstt_hist[(ic, half)] = sv.v

                        @block.sync
                        def _(q, ic=ic, half=half, t0h=t0h, hv=hstt_hist[(ic, half)]):
                            q.wait_ge(sv.h, hv)
                            ins = q.dma_start(out=h_h[ic][:, t0h:t0h + 2048],
                                              in_=hbf[half][:])
                            hst[half].tag(ins, 16)

                        if half == 0 and ic + 1 < NI:
                            f1_deq(ic + 1)
                        if half == 1 and ic + 2 < NI:
                            f1_load(ic + 2)

                f1a, f1b = hst[0].v, hst[1].v

                def wait_f1_done(eng):
                    eng.wait_ge(hst[0].h, f1a)
                    eng.wait_ge(hst[1].h, f1b)
        else:
            wait_f1_done = wait_f0_done

        if nphases >= 5:
            xfm2_cm.__exit__(None, None, None)

        if nphases >= 7:
            HGRP = [(0, 6), (6, 12), (12, 16)]
            with ExitStack() as px:
                xe = px.enter_context
                hb = [xe(nc.sbuf_tensor(f"f2h{i}", [128, NI * 512], BF16)) for i in range(2)]
                wdt = [xe(nc.sbuf_tensor(f"f2w{i}", [128, 768], BF16)) for i in range(8)]
                dob = [xe(nc.sbuf_tensor(f"f2d{i}", [128, ND * 512], BF16)) for i in range(2)]
                dsc = xe(nc.sbuf_tensor("f2ds", [128, ND], F32))
                dtm = [xe(nc.sbuf_tensor(f"f2tm{i}", [128, D], BF16)) for i in range(2)]
                dnr = xe(nc.sbuf_tensor("f2dn", [128, D], F32))
                xt6 = [xe(nc.sbuf_tensor(f"f2x{i}", [128, D], F32)) for i in range(2)]
                st6 = xe(nc.sbuf_tensor("f2st", [128, 24], F32))
                mv6 = xe(nc.sbuf_tensor("f2mv", [128, 2], F32))
                rs6 = xe(nc.sbuf_tensor("f2rs", [128, 1], F32))
                nm6 = xe(nc.sbuf_tensor("f2nm", [128, 1], F32))
                lt6 = xe(nc.sbuf_tensor("f2lt", [128, 1], F32))
                ps_d = xe(nc.psum_tensor("f2pd", [128, 6 * 512], F32))
                ps_t = xe(nc.psum_tensor("f2pt", [128, 2048], BF16))

                hbl = [SemC(nc, ctx, f"f2hl{i}") for i in range(2)]
                wdl = [SemC(nc, ctx, f"f2wl{i}") for i in range(8)]
                dsl = SemC(nc, ctx, "f2dsl")
                xl6 = [SemC(nc, ctx, f"f2xl{i}") for i in range(2)]
                ots = [SemC(nc, ctx, f"f2os{i}") for i in range(2)]

                @block.sync
                def _(q):
                    wait_f1_done(q)
                    for c in range(ND):
                        ins = q.dma_start(
                            out=dsc[:, c:c + 1],
                            in_=ds_h[c * 128:(c + 1) * 128].rearrange("(p u) -> p u", u=1))
                        dsl.tag(ins, 16)

                ds_ready = dsl.v
                wd_hist = {}
                dev_hist = {}
                grp_end = {}
                trev_hist = {}
                dtmrd_hist = {}
                NB = 8
                tr_tasks = []

                def emit_group(tb, gi, hbslot):
                    h0, h1 = HGRP[gi]
                    nhc = h1 - h0
                    for ic in range(NI):
                        git = (tb * 3 + gi) * NI + ic
                        ws = git % 8
                        qeng = block.sync if ic % 2 == 0 else block.gpsimd

                        @qeng
                        def _(q, gi=gi, ic=ic, ws=ws, git=git, h0=h0, h1=h1):
                            if git >= 8:
                                q.wait_ge(stn.h, wd_hist[git - 8])
                            src = wdb_h[ic][:, h0 * 128:h1 * 128]
                            ins = q.dma_start(out=wdt[ws][:, 0:(h1 - h0) * 128],
                                              in_=src)
                            wdl[ws].tag(ins, 16)

                        @block.tensor
                        def _(t, tb=tb, gi=gi, ic=ic, ws=ws, git=git, nhc=nhc,
                              hbslot=hbslot,
                              evprev=dev_hist.get((tb, gi - 1) if gi else
                                                  (tb - 1, 2), 0)):
                            t.wait_ge(wdl[ws].h, 16 * (git // 8 + 1))
                            if ic == 0:
                                t.wait_ge(hbl[hbslot].h, 16 * (tb // 2 + 1))
                                if evprev:
                                    t.wait_ge(sv.h, evprev)
                            for u in range(nhc):
                                ins = t.matmul(
                                    ps_d[:, u * 512:(u + 1) * 512],
                                    lhsT=wdt[ws][:, u * 128:(u + 1) * 128],
                                    rhs=hb[hbslot][:, ic * 512:(ic + 1) * 512],
                                    start=(ic == 0), stop=(ic == NI - 1))
                            stn.tag(ins, 1)

                        wd_hist[git] = stn.v

                    mm_end = stn.v

                    @block.vector
                    def _(v, tb=tb, gi=gi, h0=h0, nhc=nhc, mm_end=mm_end,
                          ds_ready=ds_ready):
                        v.wait_ge(stn.h, mm_end)
                        if tb == 0 and gi == 0:
                            v.wait_ge(dsl.h, ds_ready)
                        for u in range(nhc):
                            ins = v.tensor_scalar(
                                out=dob[tb % 2][:, (h0 + u) * 512:(h0 + u + 1) * 512],
                                in0=ps_d[:, u * 512:(u + 1) * 512],
                                scalar1=dsc[:, h0 + u:h0 + u + 1], scalar2=None,
                                op0=ALU.mult, op1=ALU.bypass)
                        sv.tag(ins)

                    dev_hist[(tb, gi)] = sv.v

                def make_tr(tb, tc):
                    def run():
                        gt = tb * 4 + tc
                        ob = gt % 2

                        @block.sync
                        def _(q, gt=gt, ob=ob):
                            if gt >= 2:
                                q.wait_ge(ots[ob].h, 16 * (gt // 2))
                            ins = q.dma_start(out=xt6[ob][:], in_=x1_h[gt])
                            xl6[ob].tag(ins, 16)

                        @block.tensor
                        def _(t, tb=tb, tc=tc, devs=dev_hist[(tb, 2)],
                              trprev=trev_hist.get((tb, tc - 1) if tc >= 1 else
                                                   (tb - 1, 3), 0)):
                            t.wait_ge(sv.h, devs)
                            if trprev:
                                t.wait_ge(ss.h, trprev)
                            for hc in range(ND):
                                ins = t.transpose(
                                    ps_t[:, hc * 128:(hc + 1) * 128],
                                    dob[tb % 2][:, hc * 512 + tc * 128:
                                                hc * 512 + (tc + 1) * 128],
                                    ident_b[:])
                            stn.tag(ins, 1)

                        trmm = stn.v

                        @scalar_section
                        def _(s, tb=tb, tc=tc, trmm=trmm,
                              dtmprev=dtmrd_hist.get((tb, tc - 2) if tc >= 2 else
                                                     (tb - 1, tc + 2), 0)):
                            s.wait_ge(stn.h, trmm)
                            if dtmprev:
                                s.wait_ge(sv.h, dtmprev)
                            ins = s.copy(out=dtm[tc % 2][:], in_=ps_t[:])
                            ss.tag(ins)

                        trev_hist[(tb, tc)] = ss.v
                        dtm_ready = ss.v

                        @block.vector
                        def _(v, tc=tc, dtm_ready=dtm_ready):
                            v.wait_ge(ss.h, dtm_ready)
                            for c in range(4):
                                ins = v.bn_stats(out=st6[:, c * 6:(c + 1) * 6],
                                                 in_=dtm[tc % 2][:, c * 512:(c + 1) * 512])
                            sv.tag(ins)
                            v.wait_ge(sv.h, sv.v)
                            ins = v.bn_aggr(out=mv6[:], in_=st6[:])
                            sv.tag(ins)

                        mv_ready6 = sv.v
                        dtmrd_hist[(tb, tc)] = mv_ready6

                        @scalar_section
                        def _(s, mv_ready6=mv_ready6):
                            s.wait_ge(sv.h, mv_ready6)
                            ins = s.activation(out=lt6[:], in_=mv6[:, 1:2], func=AF.Ln,
                                               bias=eps_t[:], scale=1.0)
                            ss.tag(ins)
                            s.wait_ge(ss.h, ss.v)
                            ins = s.activation(out=rs6[:], in_=lt6[:], func=AF.Exp,
                                               bias=zero_t[:], scale=-0.5)
                            ss.tag(ins)

                        rstd_ready6 = ss.v

                        @block.vector
                        def _(v, rstd_ready6=rstd_ready6):
                            v.wait_ge(ss.h, rstd_ready6)
                            ins = v.scalar_tensor_tensor(out=nm6[:], in0=mv6[:, 0:1],
                                                         scalar=-1.0, in1=rs6[:],
                                                         op0=ALU.mult, op1=ALU.mult)
                            sv.tag(ins)

                        nmr_ready6 = sv.v

                        @scalar_section
                        def _(s, tc=tc, nmr_ready6=nmr_ready6):
                            s.wait_ge(sv.h, nmr_ready6)
                            ins = s.activation(out=dnr[:], in_=dtm[tc % 2][:],
                                               func=AF.Identity, bias=nm6[:], scale=rs6[:])
                            ss.tag(ins)

                        ln3_ready = ss.v

                        @block.vector
                        def _(v, gt=gt, ob=ob, ln3_ready=ln3_ready):
                            v.wait_ge(ss.h, ln3_ready)
                            v.wait_ge(xl6[ob].h, 16 * (gt // 2 + 1))
                            ins = v.tensor_tensor(out=xt6[ob][:], in0=xt6[ob][:],
                                                  in1=dnr[:], op=ALU.add)
                            sv.tag(ins)

                        out_ready = sv.v

                        @block.sync
                        def _(q, gt=gt, ob=ob, out_ready=out_ready):
                            q.wait_ge(sv.h, out_ready)
                            ins = q.dma_start(out=out_h[gt * 128:(gt + 1) * 128, :],
                                              in_=xt6[ob][:])
                            ots[ob].tag(ins, 16)
                    return run

                @block.gpsimd
                def _(g):
                    wait_f1_done(g)

                for tb in range(NB):
                    hbslot = tb % 2

                    @block.gpsimd
                    def _(q, tb=tb, hbslot=hbslot, prev=grp_end.get(tb - 2, 0)):
                        if prev:
                            q.wait_ge(stn.h, prev)
                        src = h_h.rearrange("c p t -> p c t")[:, :, tb * 512:(tb + 1) * 512]
                        ins = q.dma_start(out=hb[hbslot][:].rearrange(
                            "p (c t) -> p c t", t=512), in_=src)
                        hbl[hbslot].tag(ins, 16)

                    for gi in range(3):
                        emit_group(tb, gi, hbslot)
                        if tr_tasks:
                            tr_tasks.pop(0)()
                        if gi == 2 and tr_tasks:
                            tr_tasks.pop(0)()
                    grp_end[tb] = stn.v
                    for tc in range(4):
                        tr_tasks.append(make_tr(tb, tc))

                while tr_tasks:
                    tr_tasks.pop(0)()

                f2a, f2b = ots[0].v, ots[1].v

                def wait_f2_done(eng):
                    eng.wait_ge(ots[0].h, f2a)
                    eng.wait_ge(ots[1].h, f2b)
        else:
            wait_f2_done = wait_f1_done


        _scr.flush()

        @block.sync
        def _(q):
            wait_f2_done(q)
            q.wait_ge(sq.h, sq.v)

    return nc



def _prep_inputs(
    L0_attn_query, L0_attn_w_q, L0_attn_b_q, L0_attn_w_k, L0_attn_b_k,
    L0_attn_w_v, L0_attn_b_v, L0_attn_w_o, L0_attn_b_o,
    L0_ffn_gate_w, L0_ffn_gate_scale, L0_ffn_up_w, L0_ffn_up_scale,
    L0_ffn_down_w, L0_ffn_down_scale,
):
    base = {
        "wq": np.ascontiguousarray(np.asarray(L0_attn_w_q, dtype=np.float32)),
        "wk": np.ascontiguousarray(np.asarray(L0_attn_w_k, dtype=np.float32)),
        "wv": np.ascontiguousarray(np.asarray(L0_attn_w_v, dtype=np.float32)),
        "wo": np.ascontiguousarray(np.asarray(L0_attn_w_o, dtype=np.float32)),
        "bq": np.ascontiguousarray(np.asarray(L0_attn_b_q, dtype=np.float32)),
        "bv": np.ascontiguousarray(np.asarray(L0_attn_b_v, dtype=np.float32)),
        "bo": np.ascontiguousarray(np.asarray(L0_attn_b_o, dtype=np.float32)),
        "wgT": np.ascontiguousarray(np.asarray(L0_ffn_gate_w).T),
        "wuT": np.ascontiguousarray(np.asarray(L0_ffn_up_w).T),
        "wdT": np.ascontiguousarray(np.asarray(L0_ffn_down_w).T),
        "gs": np.ascontiguousarray(np.asarray(L0_ffn_gate_scale, dtype=np.float32)),
        "us": np.ascontiguousarray(np.asarray(L0_ffn_up_scale, dtype=np.float32)),
        "ds": np.ascontiguousarray(np.asarray(L0_ffn_down_scale, dtype=np.float32)),
    }
    xq = np.asarray(L0_attn_query)
    return [dict(base, x=np.ascontiguousarray(xq[b])) for b in range(8)]


def kernel(**inputs):
    in_maps = _prep_inputs(**inputs)
    nc = build()
    res = run_bass_kernel_spmd(nc, in_maps, core_ids=list(range(8)))
    return np.stack([np.asarray(r["out"]) for r in res.results], axis=0)


# version 18
# speedup vs baseline: 1.3077x; 1.0545x over previous
import numpy as np
from contextlib import ExitStack

import concourse.bass as bass
import concourse.mybir as mybir
from concourse.bass_utils import run_bass_kernel_spmd

F32 = mybir.dt.float32
BF16 = mybir.dt.bfloat16
I32 = mybir.dt.int32
AF = mybir.ActivationFunctionType
ALU = mybir.AluOpType
AX = mybir.AxisListType

T = 4096
D = 2048
NH = 16
FF = 8192
NT = T // 128
ND = D // 128
NI = FF // 128
EPS = 1e-5
QSCALE = float(1.0 / np.sqrt(128.0))

S_BANKS = 5
S_CAP = S_BANKS * 512


class SemC:

    def __init__(self, nc, ctx, name):
        self.h = ctx.enter_context(nc.semaphore(name))
        self.v = 0

    def tag(self, ins, n=1):
        ins.then_inc(self.h, n)
        self.v += n
        return self.v


def build(nphases=99, dbg=False):
    nc = bass.Bass("TRN2", target_bir_lowering=False)

    x_h = nc.dram_tensor("x", [T, D], F32, kind="ExternalInput").ap()
    wq_h = nc.dram_tensor("wq", [D, D], F32, kind="ExternalInput").ap()
    wk_h = nc.dram_tensor("wk", [D, D], F32, kind="ExternalInput").ap()
    wv_h = nc.dram_tensor("wv", [D, D], F32, kind="ExternalInput").ap()
    wo_h = nc.dram_tensor("wo", [D, D], F32, kind="ExternalInput").ap()
    bq_h = nc.dram_tensor("bq", [D], F32, kind="ExternalInput").ap()
    bv_h = nc.dram_tensor("bv", [D], F32, kind="ExternalInput").ap()
    bo_h = nc.dram_tensor("bo", [D], F32, kind="ExternalInput").ap()
    wgT_h = nc.dram_tensor("wgT", [D, FF], I32, kind="ExternalInput").ap()
    wuT_h = nc.dram_tensor("wuT", [D, FF], I32, kind="ExternalInput").ap()
    wdT_h = nc.dram_tensor("wdT", [FF, D], I32, kind="ExternalInput").ap()
    gs_h = nc.dram_tensor("gs", [FF], F32, kind="ExternalInput").ap()
    us_h = nc.dram_tensor("us", [FF], F32, kind="ExternalInput").ap()
    ds_h = nc.dram_tensor("ds", [D], F32, kind="ExternalInput").ap()

    out_h = nc.dram_tensor("out", [T, D], F32, kind="ExternalOutput").ap()

    sk = "ExternalOutput" if dbg else "Internal"
    q_h = nc.dram_tensor("q_s", [ND, 128, T], BF16, kind=sk).ap()
    k_h = nc.dram_tensor("k_s", [ND, 128, T], BF16, kind=sk).ap()
    v2_h = nc.dram_tensor("v_s", [NT, 128, D], BF16, kind=sk).ap()
    o_h = nc.dram_tensor("o_s", [ND, 128, T], BF16, kind=sk).ap()
    x1_h = nc.dram_tensor("x1_s", [NT, 128, D], F32, kind=sk).ap()
    h_h = nc.dram_tensor("h_s", [NI, 128, T], BF16, kind=sk).ap()
    wdb_h = nc.dram_tensor("wdb_s", [NI, 128, D], BF16, kind=sk).ap()
    xfm_h = (nc.dram_tensor("xfm_s", [ND, 128, T], BF16, kind="ExternalOutput").ap()
             if dbg else None)

    with ExitStack() as ctx:
        E = ctx.enter_context
        block = E(nc.Block())

        class _ScH:
            __slots__ = ("incs",)

            def __init__(self):
                self.incs = []

            def then_inc(self, sem, n):
                self.incs.append((sem, n))

        class _ScalarRec:
            def __init__(self):
                self.cmds = []

            def wait_ge(self, sem, v):
                self.cmds.append(("w", sem, v))

            def _op(self, name, a, k):
                h = _ScH()
                self.cmds.append(("o", name, a, k, h))
                return h

            def activation(self, *a, **k):
                return self._op("activation", a, k)

            def copy(self, *a, **k):
                return self._op("copy", a, k)

            def flush(self):
                if not self.cmds:
                    return
                cmds, self.cmds = self.cmds, []

                @block.scalar
                def _(s):
                    for c in cmds:
                        if c[0] == "w":
                            s.wait_ge(c[1], c[2])
                        else:
                            ins = getattr(s, c[1])(*c[2], **c[3])
                            for sem, n in c[4].incs:
                                ins.then_inc(sem, n)

        _scr = _ScalarRec()

        def scalar_section(fn):
            fn(_scr)
            if len(_scr.cmds) >= 100:
                _scr.flush()

        ident_b = E(nc.sbuf_tensor("ident_b", [128, 128], BF16))
        ident_f = E(nc.sbuf_tensor("ident_f", [128, 128], F32))
        mask_T = E(nc.sbuf_tensor("mask_T", [128, 128], F32))
        ones1 = E(nc.sbuf_tensor("ones1", [128, 128], F32))
        ones_bb = E(nc.sbuf_tensor("ones_bb", [128, 128], BF16))
        eps_t = E(nc.sbuf_tensor("eps_t", [128, 1], F32))
        zero_t = E(nc.sbuf_tensor("zero_t", [128, 1], F32))
        iota_i = E(nc.sbuf_tensor("iota_i", [128, 128], I32))
        iota_j = E(nc.sbuf_tensor("iota_j", [128, 128], I32))
        xfm_cm = nc.sbuf_tensor("xfm", [128, ND * T], BF16)
        xfm = xfm_cm.__enter__()

        sv = SemC(nc, ctx, "sv")
        ss = SemC(nc, ctx, "ss")
        stn = SemC(nc, ctx, "stn")
        sg = SemC(nc, ctx, "sg")
        sq = SemC(nc, ctx, "sq")

        @block.gpsimd
        def _(g):
            g.iota(iota_j[:], pattern=[[1, 128]], base=0, channel_multiplier=0)
            ins = g.iota(iota_i[:], pattern=[[0, 128]], base=0, channel_multiplier=1)
            sg.tag(ins)

        @block.vector
        def _(v):
            v.memset(eps_t[:], EPS)
            v.memset(zero_t[:], 0.0)
            v.memset(ones1[:], 1.0)
            v.memset(ones_bb[:], 1.0)
            v.wait_ge(sg.h, sg.v)
            v.tensor_tensor(out=ident_b[:], in0=iota_j[:], in1=iota_i[:], op=ALU.is_equal)
            v.tensor_tensor(out=ident_f[:], in0=iota_j[:], in1=iota_i[:], op=ALU.is_equal)
            ins = v.tensor_tensor(out=mask_T[:], in0=iota_i[:], in1=iota_j[:], op=ALU.is_gt)
            sv.tag(ins)
            v.wait_ge(sv.h, sv.v)
            ins = v.tensor_scalar(out=mask_T[:], in0=mask_T[:], scalar1=-1e10, scalar2=None,
                                  op0=ALU.mult, op1=ALU.bypass)
            sv.tag(ins)

        const_ready = sv.v

        def emit_ln_transpose(tag, src_h, src_is_3d, dst, gate=None):
            with ExitStack() as px:
                xe = px.enter_context
                xt = [xe(nc.sbuf_tensor(f"{tag}x{i}", [128, D], F32)) for i in range(2)]
                stats = [xe(nc.sbuf_tensor(f"{tag}st{i}", [128, 24], F32)) for i in range(2)]
                mv = [xe(nc.sbuf_tensor(f"{tag}mv{i}", [128, 2], F32)) for i in range(2)]
                rstd = [xe(nc.sbuf_tensor(f"{tag}rs{i}", [128, 1], F32)) for i in range(2)]
                nmr = [xe(nc.sbuf_tensor(f"{tag}nm{i}", [128, 1], F32)) for i in range(2)]
                lt = [xe(nc.sbuf_tensor(f"{tag}lt{i}", [128, 1], F32)) for i in range(2)]
                xtm = xe(nc.sbuf_tensor(f"{tag}tm", [128, D], BF16))
                pst = [xe(nc.psum_tensor(f"{tag}ps{i}", [128, 1024], BF16)) for i in range(2)]

                ld = [SemC(nc, ctx, f"{tag}ld{i}") for i in range(2)]
                fr = [SemC(nc, ctx, f"{tag}fr{i}") for i in range(2)]
                ptfr = [SemC(nc, ctx, f"{tag}pf{i}") for i in range(2)]
                ptrd = [SemC(nc, ctx, f"{tag}pr{i}") for i in range(2)]
                mvv = {}
                tile_marks = {}

                def emit_load(i):
                    b = i % 2

                    @block.sync
                    def _(q, i=i, b=b):
                        if i == 0 and gate is not None:
                            gate(q)
                        if i >= 2:
                            q.wait_ge(fr[b].h, i // 2)
                        ins = q.dma_start(out=xt[b][:], in_=(src_h[i] if src_is_3d
                                          else src_h[i * 128:(i + 1) * 128, :]))
                        ld[b].tag(ins, 16)

                def emit_front(i):
                    b = i % 2

                    @block.vector
                    def _(v, i=i, b=b):
                        v.wait_ge(ld[b].h, 16 * (i // 2 + 1))
                        for c in range(4):
                            ins = v.bn_stats(out=stats[b][:, c * 6:(c + 1) * 6],
                                             in_=xt[b][:, c * 512:(c + 1) * 512])
                        sv.tag(ins)
                        v.wait_ge(sv.h, sv.v)
                        ins = v.bn_aggr(out=mv[b][:], in_=stats[b][:])
                        sv.tag(ins)
                    mvv[i] = sv.v

                def emit_tail(i):
                    b = i % 2

                    @scalar_section
                    def _(s, i=i, b=b, mvr=mvv[i]):
                        s.wait_ge(sv.h, mvr)
                        ins = s.activation(out=lt[b][:], in_=mv[b][:, 1:2], func=AF.Ln,
                                           bias=eps_t[:], scale=1.0)
                        ss.tag(ins)
                        s.wait_ge(ss.h, ss.v)
                        ins = s.activation(out=rstd[b][:], in_=lt[b][:], func=AF.Exp,
                                           bias=zero_t[:], scale=-0.5)
                        ss.tag(ins)

                    rstd_ready = ss.v

                    @block.vector
                    def _(v, b=b, rstd_ready=rstd_ready):
                        v.wait_ge(ss.h, rstd_ready)
                        ins = v.scalar_tensor_tensor(out=nmr[b][:], in0=mv[b][:, 0:1],
                                                     scalar=-1.0, in1=rstd[b][:],
                                                     op0=ALU.mult, op1=ALU.mult)
                        sv.tag(ins)

                    nmr_ready = sv.v
                    pt0, pt1 = ptrd[0].v, ptrd[1].v

                    @scalar_section
                    def _(s, i=i, b=b, nmr_ready=nmr_ready, pt0=pt0, pt1=pt1):
                        s.wait_ge(sv.h, nmr_ready)
                        if i >= 1:
                            s.wait_ge(ptrd[0].h, pt0)
                            s.wait_ge(ptrd[1].h, pt1)
                        ins = s.activation(out=xtm[:], in_=xt[b][:], func=AF.Identity,
                                           bias=nmr[b][:], scale=rstd[b][:])
                        fr[b].tag(ins, 1)

                    for gidx in range(4):
                        gi = i * 4 + gidx
                        pb = gi % 2

                        @block.tensor
                        def _(t, i=i, b=b, gidx=gidx, gi=gi, pb=pb):
                            if gidx == 0:
                                t.wait_ge(fr[b].h, i // 2 + 1)
                                if i == 0:
                                    t.wait_ge(sv.h, const_ready)
                            if gi >= 2:
                                t.wait_ge(ptfr[pb].h, gi // 2)
                            for u in range(4):
                                dc = gidx * 4 + u
                                ins = t.transpose(pst[pb][:, u * 128:(u + 1) * 128],
                                                  xtm[:, dc * 128:(dc + 1) * 128], ident_b[:])
                            ptrd[pb].tag(ins, 1)

                        @scalar_section
                        def _(s, i=i, gidx=gidx, pb=pb, want=ptrd[pb].v):
                            s.wait_ge(ptrd[pb].h, want)
                            dstv = dst[:].rearrange("p (c t) -> p c t", t=T)[
                                :, gidx * 4:(gidx + 1) * 4, i * 128:(i + 1) * 128]
                            ins = s.copy(out=dstv, in_=pst[pb][:, 0:512])
                            ptfr[pb].tag(ins, 1)
                    tile_marks[i] = (ptfr[0].v, ptfr[1].v)

                emit_load(0)
                emit_load(1)
                for i in range(NT):
                    if i + 2 < NT:
                        emit_load(i + 2)
                    emit_front(i)
                    if i >= 1:
                        emit_tail(i - 1)
                emit_tail(NT - 1)

                return ptfr[0].v, ptfr[1].v, ptfr, tile_marks

        p0f0, p0f1, p0_ptfr, p0_marks = emit_ln_transpose("p0", x_h, False, xfm)

        def wait_phase0_done(eng):
            eng.wait_ge(p0_ptfr[0].h, p0f0)
            eng.wait_ge(p0_ptfr[1].h, p0f1)

        if dbg:
            @block.sync
            def _(q):
                wait_phase0_done(q)
                for dc in range(ND):
                    ins = q.dma_start(out=xfm_h[dc], in_=xfm[:, dc * T:(dc + 1) * T])
                    sq.tag(ins, 16)
        dbg_p0_dump = sq.v

        if nphases >= 2:
            with ExitStack() as px:
                xe = px.enter_context
                wst = [xe(nc.sbuf_tensor(f"p1w{i}", [128, D], F32)) for i in range(2)]
                wbt = [xe(nc.sbuf_tensor(f"p1b{i}", [128, D], BF16)) for i in range(2)]
                row = [xe(nc.sbuf_tensor(f"p1r{i}", [128, T], BF16)) for i in range(2)]
                bq_all = xe(nc.sbuf_tensor("p1bq", [128, ND], F32))
                psm = [xe(nc.psum_tensor(f"p1ps{i}", [128, 512], F32)) for i in range(4)]

                wld = [SemC(nc, ctx, f"p1wl{i}") for i in range(2)]
                bql = SemC(nc, ctx, "p1bql")
                rfr = [SemC(nc, ctx, f"p1rf{i}") for i in range(2)]
                pse = [SemC(nc, ctx, f"p1pe{i}") for i in range(2)]
                psw = [SemC(nc, ctx, f"p1pw{i}") for i in range(2)]

                @block.sync
                def _(q):
                    wait_phase0_done(q)
                    for oc in range(ND):
                        ins = q.dma_start(
                            out=bq_all[:, oc:oc + 1],
                            in_=bq_h[oc * 128:(oc + 1) * 128].rearrange("(p u) -> p u", u=1))
                        bql.tag(ins, 16)

                bq_ready = bql.v
                cast_hist = {}
                mmdone_hist = {}

                projs = [("q", wq_h, q_h), ("k", wk_h, k_h)]
                for pi, (pname, w_hbm, dst_hbm) in enumerate(projs):
                    for oc in range(ND):
                        it = pi * ND + oc
                        b = it % 2

                        @block.sync
                        def _(q, oc=oc, it=it, b=b, w_hbm=w_hbm):
                            if it >= 2:
                                q.wait_ge(sv.h, cast_hist[it - 2])
                            src = w_hbm[:, oc * 128:(oc + 1) * 128].rearrange(
                                "(c p) o -> p c o", p=128)
                            ins = q.dma_start(out=wst[b][:].rearrange(
                                "p (c o) -> p c o", o=128), in_=src)
                            wld[b].tag(ins, 16)

                        @block.vector
                        def _(v, it=it, b=b):
                            v.wait_ge(wld[b].h, 16 * (it // 2 + 1))
                            if it >= 2:
                                v.wait_ge(psw[0].h, mmdone_hist[it - 2])
                            ins = v.tensor_copy(out=wbt[b][:], in_=wst[b][:])
                            sv.tag(ins)

                        cast_hist[it] = sv.v
                        ev_hist = []

                        for gp in range(4):
                            g0p = gp * 2
                            pp = gp % 2

                            @block.tensor
                            def _(t, it=it, b=b, gp=gp, g0p=g0p, pp=pp,
                                  cast_val=cast_hist[it],
                                  evw=pse[pp].v):
                                if gp == 0:
                                    t.wait_ge(sv.h, cast_val)
                                if it == 0:
                                    t.wait_ge(p0_ptfr[0].h, p0_marks[8 * gp + 7][0])
                                    t.wait_ge(p0_ptfr[1].h, p0_marks[8 * gp + 7][1])
                                if evw:
                                    t.wait_ge(pse[pp].h, evw)
                                for dc in range(ND):
                                    for pb2 in range(2):
                                        g = g0p + pb2
                                        ins = t.matmul(
                                            psm[2 * pp + pb2][:],
                                            lhsT=wbt[b][:, dc * 128:(dc + 1) * 128],
                                            rhs=xfm[:, dc * T + g * 512:
                                                    dc * T + (g + 1) * 512],
                                            start=(dc == 0), stop=(dc == ND - 1))
                                psw[0].tag(ins, 1)

                            rb = it % 2
                            for pb2 in range(2):
                                g = g0p + pb2
                                if pi == 0:
                                    @block.vector
                                    def _(v, it=it, oc=oc, g=g, pb2=pb2, pp=pp,
                                          rb=rb, mmv=psw[0].v):
                                        v.wait_ge(psw[0].h, mmv)
                                        if g == 0:
                                            v.wait_ge(bql.h, bq_ready)
                                            if it >= 2:
                                                v.wait_ge(rfr[rb].h, 16 * (it // 2))
                                        ins = v.tensor_scalar(
                                            out=row[rb][:, g * 512:(g + 1) * 512],
                                            in0=psm[2 * pp + pb2][:],
                                            scalar1=bq_all[:, oc:oc + 1],
                                            scalar2=QSCALE, op0=ALU.add, op1=ALU.mult)
                                        pse[pp].tag(ins, 1)
                                    ev_hist.append((pse[pp], pse[pp].v))
                                else:
                                    @scalar_section
                                    def _(s, it=it, g=g, pb2=pb2, pp=pp, rb=rb,
                                          mmv=psw[0].v):
                                        s.wait_ge(psw[0].h, mmv)
                                        if g == 0 and it >= 2:
                                            s.wait_ge(rfr[rb].h, 16 * (it // 2))
                                        ins = s.copy(out=row[rb][:, g * 512:(g + 1) * 512],
                                                     in_=psm[2 * pp + pb2][:])
                                        pse[pp].tag(ins, 1)
                                    ev_hist.append((pse[pp], pse[pp].v))

                        mmdone_hist[it] = psw[0].v
                        e0 = max([v for (s_, v) in ev_hist if s_ is pse[0]], default=0)
                        e1 = max([v for (s_, v) in ev_hist if s_ is pse[1]], default=0)

                        @block.sync
                        def _(q, oc=oc, it=it, dst_hbm=dst_hbm, e0=e0, e1=e1):
                            if e0:
                                q.wait_ge(pse[0].h, e0)
                            if e1:
                                q.wait_ge(pse[1].h, e1)
                            ins = q.dma_start(out=dst_hbm[oc], in_=row[it % 2][:])
                            rfr[it % 2].tag(ins, 16)

                p1r0, p1r1 = rfr[0].v, rfr[1].v

            with ExitStack() as px:
                xe = px.enter_context
                wvst = [xe(nc.sbuf_tensor(f"v1w{i}", [128, 1024], F32)) for i in range(2)]
                wvb = xe(nc.sbuf_tensor("v1wb", [128, ND * 1024], BF16))
                vrow = [xe(nc.sbuf_tensor(f"v1r{i}", [128, 1024], BF16)) for i in range(2)]
                ps_v = [xe(nc.psum_tensor(f"v1p{i}", [128, 1024], F32)) for i in range(2)]
                vwl = [SemC(nc, ctx, f"v1wl{i}") for i in range(2)]
                vst = [SemC(nc, ctx, f"v1st{i}") for i in range(2)]
                vcast_hist = {}
                vevac_hist = {}
                vmm_half_end = {}

                for half in range(2):
                    f0h = half * 1024
                    for dc in range(ND):
                        wit = half * ND + dc
                        wb = wit % 2

                        @block.sync
                        def _(q, dc=dc, wb=wb, wit=wit, f0h=f0h,
                              prev=vcast_hist.get(wit - 2, 0)):
                            if wit < 2:
                                q.wait_ge(rfr[0].h, p1r0)
                                q.wait_ge(rfr[1].h, p1r1)
                            if prev:
                                q.wait_ge(ss.h, prev)
                            ins = q.dma_start(
                                out=wvst[wb][:],
                                in_=wv_h[dc * 128:(dc + 1) * 128, f0h:f0h + 1024])
                            vwl[wb].tag(ins, 16)

                        @scalar_section
                        def _(s, dc=dc, wb=wb, wit=wit, half=half,
                              mmprev=vmm_half_end.get(half - 1, 0)):
                            s.wait_ge(vwl[wb].h, 16 * (wit // 2 + 1))
                            if half == 1 and dc == 0:
                                s.wait_ge(stn.h, mmprev)
                            ins = s.copy(out=wvb[:, dc * 1024:(dc + 1) * 1024],
                                         in_=wvst[wb][:])
                            ss.tag(ins)

                        vcast_hist[wit] = ss.v

                    cast_done_h = ss.v
                    for i in range(NT):
                        vt = half * NT + i
                        sl = vt % 2

                        @block.tensor
                        def _(t, i=i, sl=sl, cast_done_h=cast_done_h,
                              ev=vevac_hist.get(vt - 2, 0)):
                            if i == 0:
                                t.wait_ge(ss.h, cast_done_h)
                            if ev:
                                t.wait_ge(ss.h, ev)
                            for dc in range(ND):
                                for fg in range(2):
                                    ins = t.matmul(
                                        ps_v[sl][:, fg * 512:(fg + 1) * 512],
                                        lhsT=xfm[:, dc * T + i * 128:dc * T + (i + 1) * 128],
                                        rhs=wvb[:, dc * 1024 + fg * 512:
                                                dc * 1024 + (fg + 1) * 512],
                                        start=(dc == 0), stop=(dc == ND - 1))
                            stn.tag(ins, 1)

                        vmm = stn.v

                        @scalar_section
                        def _(s, sl=sl, vt=vt, vmm=vmm, stprev=vst[sl].v):
                            s.wait_ge(stn.h, vmm)
                            if vt >= 2:
                                s.wait_ge(vst[sl].h, stprev)
                            ins = s.copy(out=vrow[sl][:], in_=ps_v[sl][:])
                            ss.tag(ins)

                        vevac_hist[vt] = ss.v

                        @block.sync
                        def _(q, i=i, sl=sl, half=half, ev=ss.v):
                            q.wait_ge(ss.h, ev)
                            ins = q.dma_start(
                                out=v2_h[i][:, half * 1024:(half + 1) * 1024],
                                in_=vrow[sl][:])
                            vst[sl].tag(ins, 16)

                    vmm_half_end[half] = stn.v

                p1v0, p1v1 = vst[0].v, vst[1].v

            def wait_p1_done(eng):
                eng.wait_ge(rfr[0].h, p1r0)
                eng.wait_ge(rfr[1].h, p1r1)
                eng.wait_ge(vst[0].h, p1v0)
                eng.wait_ge(vst[1].h, p1v1)
                if dbg and dbg_p0_dump:
                    eng.wait_ge(sq.h, dbg_p0_dump)
        else:
            def wait_p1_done(eng):
                wait_phase0_done(eng)

        xfm_cm.__exit__(None, None, None)

        if nphases >= 3:
            with ExitStack() as px:
                xe = px.enter_context
                kr = [xe(nc.sbuf_tensor(f"p2k{i}", [128, T], BF16)) for i in range(4)]
                qr = [xe(nc.sbuf_tensor(f"p2q{i}", [128, T], BF16)) for i in range(4)]
                vtm = [xe(nc.sbuf_tensor(f"p2v{i}", [128, T], BF16)) for i in range(4)]
                orow = [xe(nc.sbuf_tensor(f"p2o{i}", [128, T], BF16)) for i in range(4)]
                pT = [xe(nc.sbuf_tensor(f"p2p{i}", [128, 512], BF16)) for i in range(4)]
                lrb = [xe(nc.sbuf_tensor(f"p2lr{i}", [128, 512], F32)) for i in range(4)]
                qsq = xe(nc.sbuf_tensor("p2sq", [128, T], BF16))
                qn = xe(nc.sbuf_tensor("p2qn", [128, 1], F32))
                kn = xe(nc.sbuf_tensor("p2kn", [128, 1], F32))
                gt_t = xe(nc.sbuf_tensor("p2gt", [128, 1], F32))
                mhb = xe(nc.sbuf_tensor("p2mh", [128, NH], F32))
                wcv = [xe(nc.sbuf_tensor(f"p2wc{i}", [128, D], I32)) for i in range(4)]
                wcb = [xe(nc.sbuf_tensor(f"p2wb{i}", [128, D], BF16)) for i in range(2)]

                ps_s = [xe(nc.psum_tensor(f"p2ss{i}", [128, 512], F32)) for i in range(3)]
                ps_o = [xe(nc.psum_tensor(f"p2po{i}", [128, 512], F32)) for i in range(2)]
                ps_l = [xe(nc.psum_tensor(f"p2pl{i}", [128, 512], F32)) for i in range(2)]
                ps_t = xe(nc.psum_tensor("p2pt", [128, 512], F32))

                rowld = [SemC(nc, ctx, f"p2rl{i}") for i in range(4)]
                ost = [SemC(nc, ctx, f"p2os{i}") for i in range(4)]
                wcl = [SemC(nc, ctx, f"p2wl{i}") for i in range(4)]
                wcs = [SemC(nc, ctx, f"p2ws{i}") for i in range(2)]

                head_te_end = {}
                wconv_hist = {}
                exp_hist = {}
                lmm_hist = {}
                last_oevac = {0: 0, 1: 0}
                last_recip = {0: 0, 1: 0}
                mhb_marker = {}
                tap_end = {}
                tap_red = [0]
                tap_tasks = []
                jg = [0]

                def bound_tasks(hh):
                    sl2 = hh % 4
                    ldv = 96 * (hh // 4 + 1)
                    st = {"sq_v": 0, "lastmm": 0}

                    def t_sq(which):
                        def run():
                            src = qr if which == 0 else kr

                            @block.vector
                            def _(v, which=which, sl2=sl2, ldv=ldv, src=src,
                                  tprev=(tap_end.get(hh - 1, 0) if hh else 0),
                                  mprev=(mhb_marker.get(hh - 1, 0)
                                         if which == 0 else 0),
                                  lastmm=st["lastmm"]):
                                v.wait_ge(rowld[sl2].h, ldv)
                                if which == 0 and tprev:
                                    v.wait_ge(stn.h, tprev)
                                if mprev:
                                    v.wait_ge(sv.h, mprev)
                                if which == 1:
                                    v.wait_ge(stn.h, lastmm)
                                ins = v.tensor_tensor(out=qsq[:], in0=src[sl2][:],
                                                      in1=src[sl2][:], op=ALU.mult)
                                sv.tag(ins)

                            st["sq_v"] = sv.v
                        return run

                    def t_mm(which, g8):
                        def run():
                            dstn = qn if which == 0 else kn

                            @block.tensor
                            def _(t, g8=g8, sq_v=st["sq_v"], hh=hh,
                                  which=which, warv=tap_red[0]):
                                if g8 == 0:
                                    t.wait_ge(sv.h, sq_v)
                                    if hh == 0 and which == 0:
                                        t.wait_ge(sv.h, const_ready)
                                if warv:
                                    t.wait_ge(sv.h, warv)
                                ins = t.matmul(ps_t[:], lhsT=ones_bb[:],
                                               rhs=qsq[:, g8 * 512:(g8 + 1) * 512],
                                               start=True, stop=True)
                                stn.tag(ins, 1)

                            mm_v = stn.v
                            st["lastmm"] = mm_v

                            @block.vector
                            def _(v, g8=g8, mm_v=mm_v, dstn=dstn,
                                  gtprev=tap_red[0]):
                                v.wait_ge(stn.h, mm_v)
                                if g8 == 0:
                                    ins = v.tensor_reduce(out=dstn[:], in_=ps_t[:],
                                                          axis=AX.X, op=ALU.max)
                                    sv.tag(ins)
                                else:
                                    if gtprev:
                                        v.wait_ge(sv.h, gtprev)
                                    ins = v.tensor_reduce(out=gt_t[:], in_=ps_t[:],
                                                          axis=AX.X, op=ALU.max)
                                    sv.tag(ins)
                                    v.wait_ge(sv.h, sv.v)
                                    ins = v.tensor_tensor(out=dstn[:], in0=dstn[:],
                                                          in1=gt_t[:], op=ALU.max)
                                    sv.tag(ins)

                            tap_red[0] = sv.v
                        return run

                    def t_fin():
                        def run():
                            @block.vector
                            def _(v, lastred=sv.v):
                                v.wait_ge(sv.h, lastred)
                                ins = v.tensor_tensor(out=qn[:], in0=qn[:], in1=kn[:],
                                                      op=ALU.mult)
                                sv.tag(ins)

                            m2_v = sv.v

                            @scalar_section
                            def _(s, m2_v=m2_v):
                                s.wait_ge(sv.h, m2_v)
                                ins = s.activation(out=gt_t[:], in_=qn[:], func=AF.Ln,
                                                   bias=zero_t[:], scale=1.0)
                                ss.tag(ins)
                                s.wait_ge(ss.h, ss.v)
                                ins = s.activation(out=gt_t[:], in_=gt_t[:],
                                                   func=AF.Exp, bias=zero_t[:],
                                                   scale=0.5)
                                ss.tag(ins)

                            sqrt_v = ss.v

                            @block.vector
                            def _(v, hh=hh, sqrt_v=sqrt_v):
                                v.wait_ge(ss.h, sqrt_v)
                                ins = v.tensor_scalar(out=mhb[:, hh:hh + 1],
                                                      in0=gt_t[:],
                                                      scalar1=-1.01, scalar2=None,
                                                      op0=ALU.mult, op1=ALU.bypass)
                                sv.tag(ins)

                            mhb_marker[hh] = sv.v
                            tap_end[hh] = stn.v
                        return run

                    seq = [t_sq(0)]
                    seq += [t_mm(0, g8) for g8 in range(8)]
                    seq += [t_sq(1)]
                    seq += [t_mm(1, g8) for g8 in range(8)]
                    seq += [t_fin()]
                    return seq

                def emit_loads(hh):
                    sl = hh % 4

                    @block.sync
                    def _(q, hh=hh, sl=sl):
                        if hh < 4:
                            wait_p1_done(q)
                        else:
                            q.wait_ge(stn.h, head_te_end[hh - 4])
                        for src, buf in ((q_h[hh], qr), (k_h[hh], kr)):
                            ins = q.dma_start(out=buf[sl][:], in_=src)
                            rowld[sl].tag(ins, 16)
                        for qu in range(4):
                            ins = q.dma_start(
                                out=vtm[sl][:, qu * 1024:(qu + 1) * 1024].rearrange(
                                    "p (c f) -> p c f", f=128),
                                in_=v2_h.rearrange("c p f -> p c f")[
                                    :, qu * 8:(qu + 1) * 8, hh * 128:(hh + 1) * 128])
                            rowld[sl].tag(ins, 16)

                for hh0 in range(4):
                    emit_loads(hh0)

                def emit_S(h, g, kt):
                    hs = h % 4
                    j = jg[0]
                    jg[0] += 1
                    bank = j % 3
                    slot = j % 4
                    r = kt - 4 * g
                    q0 = r * 128 if r > 0 else 0
                    first_exp = (g == 0 and kt == 0)

                    @block.tensor
                    def _(t, h=h, hs=hs, g=g, kt=kt, bank=bank, q0=q0,
                          ldv=96 * (h // 4 + 1),
                          bfree=exp_hist.get(bank, 0)):
                        if g == 0 and kt == 0:
                            t.wait_ge(rowld[hs].h, ldv)
                            if h == 0:
                                t.wait_ge(sv.h, const_ready)
                        if bfree:
                            t.wait_ge(ss.h, bfree)
                        ins = t.matmul(
                            ps_s[bank][:, q0:512],
                            lhsT=kr[hs][:, kt * 128:(kt + 1) * 128],
                            rhs=qr[hs][:, g * 512 + q0:(g + 1) * 512],
                            start=True, stop=True)
                        stn.tag(ins, 1)

                    s_v = stn.v
                    gate_v = None
                    if r >= 0:
                        @block.vector
                        def _(v, bank=bank, r=r, s_v=s_v):
                            v.wait_ge(stn.h, s_v)
                            ins = v.tensor_tensor(
                                out=ps_s[bank][:, r * 128:r * 128 + 128],
                                in0=ps_s[bank][:, r * 128:r * 128 + 128],
                                in1=mask_T[:], op=ALU.add)
                            sv.tag(ins)

                        gate_v = sv.v
                    if r > 0:
                        @block.vector
                        def _(v, slot=slot, r=r,
                              pfree=lmm_hist.get(slot, 0)):
                            if pfree:
                                v.wait_ge(stn.h, pfree)
                            ins = v.memset(pT[slot][:, 0:r * 128], 0.0)
                            sv.tag(ins)

                        gate_v = sv.v

                    @scalar_section
                    def _(s, h=h, bank=bank, slot=slot, q0=q0, r=r,
                          s_v=s_v, gate_v=gate_v,
                          pfree=lmm_hist.get(slot, 0),
                          mh_v=(mhb_marker[h] if first_exp else 0)):
                        if gate_v is not None:
                            s.wait_ge(sv.h, gate_v)
                        else:
                            s.wait_ge(stn.h, s_v)
                        if r <= 0 and pfree:
                            s.wait_ge(stn.h, pfree)
                        if mh_v:
                            s.wait_ge(sv.h, mh_v)
                        ins = s.activation(out=pT[slot][:, q0:512],
                                           in_=ps_s[bank][:, q0:512],
                                           func=AF.Exp, bias=mhb[:, h:h + 1],
                                           scale=1.0)
                        ss.tag(ins)

                    exp_hist[bank] = ss.v
                    return dict(h=h, g=g, kt=kt, slot=slot, exp_v=ss.v)

                def emit_PVl(info):
                    h = info["h"]
                    hs = h % 4
                    par = h % 2
                    g = info["g"]
                    kt = info["kt"]
                    slot = info["slot"]

                    @block.tensor
                    def _(t, hs=hs, par=par, g=g, kt=kt, slot=slot,
                          exp_v=info["exp_v"],
                          ofree=(last_oevac[par] if kt == 0 else 0),
                          lfree=(last_recip[par] if kt == 0 else 0)):
                        t.wait_ge(ss.h, exp_v)
                        if kt == 0 and ofree:
                            t.wait_ge(sv.h, ofree)
                        ins = t.matmul(ps_o[par][:],
                                       lhsT=vtm[hs][:, kt * 128:(kt + 1) * 128],
                                       rhs=pT[slot][:],
                                       start=(kt == 0), stop=(kt == 4 * g + 3))
                        stn.tag(ins, 1)
                        if kt == 0 and lfree:
                            t.wait_ge(sv.h, lfree)
                        ins = t.matmul(ps_l[par][:], lhsT=ones_bb[:],
                                       rhs=pT[slot][:],
                                       start=(kt == 0), stop=(kt == 4 * g + 3))
                        stn.tag(ins, 1)

                    lmm_hist[slot] = stn.v

                def emit_epilogue(g, h):
                    hs = h % 4
                    par = h % 2
                    grp_done = stn.v
                    lslot = 2 * par + g % 2

                    @block.vector
                    def _(v, par=par, lslot=lslot, grp_done=grp_done,
                          lwar=last_oevac[par]):
                        v.wait_ge(stn.h, grp_done)
                        if lwar:
                            v.wait_ge(sv.h, lwar)
                        ins = v.reciprocal(out=lrb[lslot][:], in_=ps_l[par][:])
                        sv.tag(ins)

                    last_recip[par] = sv.v
                    recip_val = sv.v

                    @block.vector
                    def _(v, h=h, hs=hs, par=par, g=g, lslot=lslot,
                          rv=recip_val,
                          ostv=(16 * (h // 4) if (g <= 1 and h >= 4) else 0)):
                        v.wait_ge(sv.h, rv)
                        if ostv:
                            v.wait_ge(ost[hs].h, ostv)
                        ins = v.tensor_tensor(
                            out=orow[hs][:, g * 512:(g + 1) * 512],
                            in0=ps_o[par][:], in1=lrb[lslot][:], op=ALU.mult)
                        sv.tag(ins)

                    last_oevac[par] = sv.v

                    if g == 3 and h + 2 < NH:
                        tap_tasks.extend(bound_tasks(h + 2))

                for hp in range(NH // 2):
                    hA = 2 * hp
                    hB = hA + 1

                    def emit_conv_load(ic, gate):
                        sl4 = ic % 4

                        @block.sync
                        def _(q, ic=ic, sl4=sl4, gate=gate):
                            if gate:
                                q.wait_ge(sv.h, gate)
                            ins = q.dma_start(out=wcv[sl4][:],
                                              in_=wdT_h[ic * 128:(ic + 1) * 128, :])
                            wcl[sl4].tag(ins, 16)

                    for u in range(4):
                        emit_conv_load(hp * 8 + u, 0)

                    for u in range(8):
                        ic = hp * 8 + u

                        def conv_task(ic=ic):
                            sl4 = ic % 4

                            @block.vector
                            def _(v, ic=ic, sl4=sl4):
                                v.wait_ge(wcl[sl4].h, 16 * (ic // 4 + 1))
                                if ic >= 2:
                                    v.wait_ge(wcs[ic % 2].h, 16 * (ic // 2))
                                ins = v.tensor_scalar(out=wcb[ic % 2][:],
                                                      in0=wcv[sl4][:],
                                                      scalar1=-128.0, scalar2=None,
                                                      op0=ALU.add, op1=ALU.bypass)
                                sv.tag(ins)

                            wconv_hist[ic] = sv.v

                            @block.gpsimd
                            def _(g, ic=ic, cv=sv.v):
                                g.wait_ge(sv.h, cv)
                                ins = g.dma_start(out=wdb_h[ic], in_=wcb[ic % 2][:])
                                wcs[ic % 2].tag(ins, 16)

                            if ic % 8 < 4:
                                emit_conv_load(ic + 4, sv.v)

                        tap_tasks.append(conv_task)

                    if hp == 0:
                        for _t in bound_tasks(0):
                            _t()
                        for _t in bound_tasks(1):
                            _t()

                    pend = []

                    flush_n = [0]

                    def flush_one():
                        old = pend.pop(0)
                        emit_PVl(old)
                        if old["kt"] == 4 * old["g"] + 3:
                            emit_epilogue(old["g"], old["h"])
                        flush_n[0] += 1
                        if tap_tasks and flush_n[0] % 6 == 0:
                            tap_tasks.pop(0)()

                    seqA = [(hA, g, kt) for g in range(8) for kt in range(4 * g + 4)]
                    seqB = [(hB, g, kt) for g in range(8) for kt in range(4 * g + 4)]
                    stream = []
                    for a2, b2 in zip(seqA, seqB):
                        stream.append(a2)
                        stream.append(b2)
                    for (h2, g2, kt2) in stream:
                        pend.append(emit_S(h2, g2, kt2))
                        if len(pend) > 2:
                            flush_one()
                    while pend:
                        flush_one()
                    while tap_tasks:
                        tap_tasks.pop(0)()
                    head_te_end[hA] = stn.v
                    head_te_end[hB] = stn.v
                    if hA + 4 < NH:
                        emit_loads(hA + 4)
                        emit_loads(hB + 4)

                    for h2 in (hA, hB):
                        @block.sync
                        def _(q, h2=h2, ov=last_oevac[h2 % 2]):
                            q.wait_ge(sv.h, ov)
                            ins = q.dma_start(out=o_h[h2], in_=orow[h2 % 4][:])
                            ost[h2 % 4].tag(ins, 16)

                p2ost = [ost[i].v for i in range(4)]
                p2w0, p2w1 = wcs[0].v, wcs[1].v

                def wait_p2_done(eng):
                    for i in range(4):
                        eng.wait_ge(ost[i].h, p2ost[i])
                    eng.wait_ge(wcs[0].h, p2w0)
                    eng.wait_ge(wcs[1].h, p2w1)
        else:
            wait_p2_done = wait_p1_done

        if nphases >= 4:
            with ExitStack() as px:
                xe = px.enter_context
                wst2 = [xe(nc.sbuf_tensor(f"p5w{i}", [128, D], F32)) for i in range(4)]
                w_ob = xe(nc.sbuf_tensor("p5wo", [128, ND * D], BF16))
                bv_all = xe(nc.sbuf_tensor("p5bv", [128, ND], F32))
                bo_row = xe(nc.sbuf_tensor("p5bo", [1, D], F32))
                beff = xe(nc.sbuf_tensor("p5be", [1, D], F32))
                beffb = xe(nc.sbuf_tensor("p5bb", [128, D], F32))
                oti = [xe(nc.sbuf_tensor(f"p5o{i}", [128, D], BF16)) for i in range(2)]
                xt2 = [xe(nc.sbuf_tensor(f"p5x{i}", [128, D], F32)) for i in range(3)]
                x1t = [xe(nc.sbuf_tensor(f"p5y{i}", [128, D], F32)) for i in range(2)]

                wl5 = [SemC(nc, ctx, f"p5wl{i}") for i in range(4)]
                bvl = SemC(nc, ctx, "p5bvl")
                otl = [SemC(nc, ctx, f"p5ot{i}") for i in range(2)]
                xl5 = [SemC(nc, ctx, f"p5xl{i}") for i in range(3)]
                x1s = [SemC(nc, ctx, f"p5x1{i}") for i in range(2)]

                with ExitStack() as bectx:
                    ps_be = bectx.enter_context(nc.psum_tensor("p5pbe", [1, D], F32))

                    @block.sync
                    def _(q):
                        wait_p2_done(q)
                        for c in range(ND):
                            ins = q.dma_start(
                                out=bv_all[:, c:c + 1],
                                in_=bv_h[c * 128:(c + 1) * 128].rearrange("(p u) -> p u", u=1))
                            bvl.tag(ins, 16)
                        ins = q.dma_start(out=bo_row[:], in_=bo_h.rearrange("(u d) -> u d", u=1))
                        bvl.tag(ins, 16)

                    bv_ready = bvl.v
                    cast5 = {}

                    for c in range(ND):
                        b = c % 4
                        qeng = block.sync if c % 2 == 0 else block.gpsimd

                        @qeng
                        def _(q, c=c, b=b, prev=(cast5.get(c - 4))):
                            if c < 2:
                                wait_p2_done(q)
                            if prev is not None:
                                q.wait_ge(stn.h, prev[0])
                                q.wait_ge(sv.h, prev[1])
                            src = wo_h[c * 128:(c + 1) * 128, :]
                            ins = q.dma_start(out=wst2[b][:], in_=src)
                            wl5[b].tag(ins, 16)

                        @block.tensor
                        def _(t, c=c, b=b, bv_ready=bv_ready):
                            t.wait_ge(wl5[b].h, 16 * (c // 4 + 1))
                            if c == 0:
                                t.wait_ge(bvl.h, bv_ready)
                            for oc4 in range(4):
                                ins = t.matmul(ps_be[:, oc4 * 512:(oc4 + 1) * 512],
                                               lhsT=bv_all[:, c:c + 1],
                                               rhs=wst2[b][:, oc4 * 512:(oc4 + 1) * 512],
                                               start=(c == 0), stop=(c == ND - 1))
                            stn.tag(ins, 1)

                        bemm = stn.v

                        @block.vector
                        def _(v, c=c, b=b, bemm=bemm):
                            v.wait_ge(stn.h, bemm)
                            ins = v.tensor_copy(out=w_ob[:, c * D:(c + 1) * D], in_=wst2[b][:])
                            sv.tag(ins)

                        cast5[c] = (bemm, sv.v)

                    be_done = stn.v
                    wob_done = sv.v

                    @scalar_section
                    def _(s, be_done=be_done):
                        s.wait_ge(stn.h, be_done)
                        ins = s.copy(out=beff[:], in_=ps_be[:])
                        ss.tag(ins)

                    @block.vector
                    def _(v, e=ss.v, bv_ready=bv_ready):
                        v.wait_ge(ss.h, e)
                        v.wait_ge(bvl.h, bv_ready)
                        ins = v.tensor_tensor(out=beff[:], in0=beff[:], in1=bo_row[:],
                                              op=ALU.add)
                        sv.tag(ins)

                    beff_done = sv.v

                ps_op = [xe(nc.psum_tensor(f"p5pop{i}", [128, D], F32)) for i in range(2)]

                @block.tensor
                def _(t, beff_done=beff_done):
                    t.wait_ge(sv.h, beff_done)
                    for oc4 in range(4):
                        ins = t.matmul(ps_op[1][:, oc4 * 512:(oc4 + 1) * 512],
                                       lhsT=ones1[0:1, :],
                                       rhs=beff[:, oc4 * 512:(oc4 + 1) * 512],
                                       start=True, stop=True)
                    stn.tag(ins, 1)

                bcast_mm5 = stn.v

                @scalar_section
                def _(s, bcast_mm5=bcast_mm5):
                    s.wait_ge(stn.h, bcast_mm5)
                    ins = s.copy(out=beffb[:], in_=ps_op[1][:])
                    ss.tag(ins)

                beffb_done = ss.v

                osb2 = [xe(nc.sbuf_tensor(f"p5ob{i}", [128, D], F32)) for i in range(2)]
                oln2 = [xe(nc.sbuf_tensor(f"p5olx{i}", [128, D], F32)) for i in range(2)]
                st5b = [xe(nc.sbuf_tensor(f"p5stx{i}", [128, 24], F32)) for i in range(2)]
                mv5b = [xe(nc.sbuf_tensor(f"p5mvx{i}", [128, 2], F32)) for i in range(2)]
                rs5b = [xe(nc.sbuf_tensor(f"p5rsx{i}", [128, 1], F32)) for i in range(2)]
                nm5b = [xe(nc.sbuf_tensor(f"p5nmx{i}", [128, 1], F32)) for i in range(2)]
                lt5b = [xe(nc.sbuf_tensor(f"p5ltx{i}", [128, 1], F32)) for i in range(2)]

                evop_hist = {}
                mm_hist5 = {}
                x1rd_hist = {}
                mvv5 = {}

                def p5_load_o(i):
                    b = i % 2

                    @block.sync
                    def _(q, i=i, b=b, mmprev=mm_hist5.get(i - 2, 0)):
                        if i >= 2:
                            q.wait_ge(stn.h, mmprev)
                        src = o_h.rearrange("c p t -> p c t")[:, :, i * 128:(i + 1) * 128]
                        ins = q.dma_start(out=oti[b][:].rearrange("p (c t) -> p c t", t=128),
                                          in_=src)
                        otl[b].tag(ins, 16)

                def p5_load_x(i):
                    b3 = i % 3

                    @block.sync
                    def _(q, i=i, b3=b3, xprev=x1rd_hist.get(i - 3, 0)):
                        if i >= 3:
                            q.wait_ge(sv.h, xprev)
                        ins = q.dma_start(out=xt2[b3][:], in_=x_h[i * 128:(i + 1) * 128, :])
                        xl5[b3].tag(ins, 16)

                def p5_mm(i):
                    b = i % 2

                    @block.tensor
                    def _(t, i=i, b=b, wob_done=wob_done, beffb_done=beffb_done,
                          evprev=evop_hist.get(i - 2, 0)):
                        t.wait_ge(otl[b].h, 16 * (i // 2 + 1))
                        if i == 0:
                            t.wait_ge(sv.h, wob_done)
                            t.wait_ge(ss.h, beffb_done)
                        if i == 1:
                            t.wait_ge(ss.h, beffb_done)
                        if i >= 2:
                            t.wait_ge(sv.h, evprev)
                        for c in range(ND):
                            for oc4 in range(4):
                                ins = t.matmul(
                                    ps_op[b][:, oc4 * 512:(oc4 + 1) * 512],
                                    lhsT=oti[b][:, c * 128:(c + 1) * 128],
                                    rhs=w_ob[:, c * D + oc4 * 512:c * D + (oc4 + 1) * 512],
                                    start=(c == 0), stop=(c == ND - 1))
                        stn.tag(ins, 1)
                    mm_hist5[i] = stn.v

                def p5_front(i):
                    b = i % 2

                    @block.vector
                    def _(v, i=i, b=b, opmm=mm_hist5[i], beffb_done=beffb_done):
                        v.wait_ge(stn.h, opmm)
                        if i == 0:
                            v.wait_ge(ss.h, beffb_done)
                        ins = v.scalar_tensor_tensor(out=osb2[b][:], in0=ps_op[b][:],
                                                     scalar=1.0, in1=beffb[:],
                                                     op0=ALU.mult, op1=ALU.add)
                        sv.tag(ins)
                        v.wait_ge(sv.h, sv.v)
                        for c in range(4):
                            ins = v.bn_stats(out=st5b[b][:, c * 6:(c + 1) * 6],
                                             in_=osb2[b][:, c * 512:(c + 1) * 512])
                        sv.tag(ins)
                        v.wait_ge(sv.h, sv.v)
                        ins = v.bn_aggr(out=mv5b[b][:], in_=st5b[b][:])
                        sv.tag(ins)
                    evop_hist[i] = sv.v - 2
                    mvv5[i] = sv.v

                def p5_tail(i):
                    b = i % 2

                    @scalar_section
                    def _(s, b=b, mvr=mvv5[i]):
                        s.wait_ge(sv.h, mvr)
                        ins = s.activation(out=lt5b[b][:], in_=mv5b[b][:, 1:2], func=AF.Ln,
                                           bias=eps_t[:], scale=1.0)
                        ss.tag(ins)
                        s.wait_ge(ss.h, ss.v)
                        ins = s.activation(out=rs5b[b][:], in_=lt5b[b][:], func=AF.Exp,
                                           bias=zero_t[:], scale=-0.5)
                        ss.tag(ins)

                    rstd_ready5 = ss.v

                    @block.vector
                    def _(v, b=b, rstd_ready5=rstd_ready5):
                        v.wait_ge(ss.h, rstd_ready5)
                        ins = v.scalar_tensor_tensor(out=nm5b[b][:], in0=mv5b[b][:, 0:1],
                                                     scalar=-1.0, in1=rs5b[b][:],
                                                     op0=ALU.mult, op1=ALU.mult)
                        sv.tag(ins)

                    nmr_ready5 = sv.v

                    @scalar_section
                    def _(s, b=b, nmr_ready5=nmr_ready5):
                        s.wait_ge(sv.h, nmr_ready5)
                        ins = s.activation(out=oln2[b][:], in_=osb2[b][:], func=AF.Identity,
                                           bias=nm5b[b][:], scale=rs5b[b][:])
                        ss.tag(ins)

                    oln_ready = ss.v

                    @block.vector
                    def _(v, i=i, b=b, oln_ready=oln_ready, x1prev=16 * (i // 2)):
                        v.wait_ge(ss.h, oln_ready)
                        v.wait_ge(xl5[i % 3].h, 16 * (i // 3 + 1))
                        if i >= 2:
                            v.wait_ge(x1s[b].h, x1prev)
                        ins = v.tensor_tensor(out=x1t[b][:], in0=xt2[i % 3][:],
                                              in1=oln2[b][:], op=ALU.add)
                        sv.tag(ins)

                    x1_ready = sv.v
                    x1rd_hist[i] = x1_ready

                    @block.sync
                    def _(q, i=i, b=b, x1_ready=x1_ready):
                        q.wait_ge(sv.h, x1_ready)
                        ins = q.dma_start(out=x1_h[i], in_=x1t[b][:])
                        x1s[b].tag(ins, 16)

                p5_load_o(0)
                p5_load_x(0)
                p5_load_o(1)
                p5_load_x(1)
                p5_load_x(2)
                for i in range(NT):
                    p5_mm(i)
                    if i + 2 < NT:
                        p5_load_o(i + 2)
                    p5_front(i)
                    if i >= 1:
                        p5_tail(i - 1)
                        if i + 2 < NT:
                            p5_load_x(i + 2)
                p5_tail(NT - 1)

                p5a, p5b = x1s[0].v, x1s[1].v

                def wait_p5_done(eng):
                    eng.wait_ge(x1s[0].h, p5a)
                    eng.wait_ge(x1s[1].h, p5b)
        else:
            wait_p5_done = wait_p2_done

        if nphases >= 5:
            xfm2_cm = nc.sbuf_tensor("xfm2", [128, ND * T], BF16)
            xfm2 = xfm2_cm.__enter__()
            f0a, f0b, f0_ptfr, _f0m = emit_ln_transpose("f0", x1_h, True, xfm2,
                                                  gate=wait_p5_done)

            if dbg:
                @block.sync
                def _(q):
                    q.wait_ge(f0_ptfr[0].h, f0a)
                    q.wait_ge(f0_ptfr[1].h, f0b)
                    for dc in range(ND):
                        ins = q.dma_start(out=xfm_h[dc], in_=xfm2[:, dc * T:(dc + 1) * T])
                        sq.tag(ins, 16)
            dbg_f0_dump = sq.v

            def wait_f0_done(eng):
                eng.wait_ge(f0_ptfr[0].h, f0a)
                eng.wait_ge(f0_ptfr[1].h, f0b)
                if dbg and dbg_f0_dump:
                    eng.wait_ge(sq.h, dbg_f0_dump)
        else:
            wait_f0_done = wait_p5_done

        if nphases >= 6:
            with ExitStack() as px:
                xe = px.enter_context
                gst = [xe(nc.sbuf_tensor(f"f1g{i}", [128, D], I32)) for i in range(2)]
                ust = [xe(nc.sbuf_tensor(f"f1u{i}", [128, D], I32)) for i in range(2)]
                gwb = [xe(nc.sbuf_tensor(f"f1gb{i}", [128, D], BF16)) for i in range(2)]
                uwb = [xe(nc.sbuf_tensor(f"f1ub{i}", [128, D], BF16)) for i in range(2)]
                sgb = [xe(nc.sbuf_tensor(f"f1sg{i}", [128, D], BF16)) for i in range(2)]
                hbf = [xe(nc.sbuf_tensor(f"f1h{i}", [128, D], BF16)) for i in range(2)]
                gsc = [xe(nc.sbuf_tensor(f"f1gs{i}", [128, 1], F32)) for i in range(2)]
                usc = [xe(nc.sbuf_tensor(f"f1us{i}", [128, 1], F32)) for i in range(2)]
                ps_g = xe(nc.psum_tensor("f1pg", [128, D], F32))
                ps_u = xe(nc.psum_tensor("f1pu", [128, D], F32))

                gl = [SemC(nc, ctx, f"f1gl{i}") for i in range(2)]
                scl = [SemC(nc, ctx, f"f1sc{i}") for i in range(2)]
                hst = [SemC(nc, ctx, f"f1hs{i}") for i in range(2)]

                deq_hist = {}
                gmm_hist = {}
                umm_hist = {}
                silu_hist = {}
                hstt_hist = {}

                def f1_load(ic):
                    b = ic % 2

                    @block.sync
                    def _(q, ic=ic, b=b, dqprev=deq_hist.get(ic - 2, 0),
                          husers=hstt_hist.get((ic - 2, 1), 0)):
                        if ic == 0:
                            wait_f0_done(q)
                        if ic >= 2:
                            q.wait_ge(ss.h, dqprev)
                            q.wait_ge(sv.h, husers)
                        for w_hbm, dstt, sem in ((wgT_h, gst, gl), (wuT_h, ust, gl)):
                            src = w_hbm[:, ic * 128:(ic + 1) * 128].rearrange(
                                "(c p) o -> p c o", p=128)
                            ins = q.dma_start(out=dstt[b][:].rearrange(
                                "p (c o) -> p c o", o=128), in_=src)
                            sem[b].tag(ins, 16)
                        for v_hbm, dstt in ((gs_h, gsc), (us_h, usc)):
                            ins = q.dma_start(
                                out=dstt[b][:],
                                in_=v_hbm[ic * 128:(ic + 1) * 128].rearrange("(p u) -> p u", u=1))
                            scl[b].tag(ins, 16)

                def f1_deq(ic):
                    b = ic % 2

                    @scalar_section
                    def _(s, ic=ic, b=b, mmprev=(gmm_hist.get((ic - 2, 1), 0),
                                                 umm_hist.get((ic - 2, 1), 0))):
                        s.wait_ge(gl[b].h, 32 * (ic // 2 + 1))
                        if ic >= 2:
                            s.wait_ge(stn.h, max(mmprev))
                        ins = s.activation(out=gwb[b][:], in_=gst[b][:], func=AF.Copy,
                                           bias=-128.0, scale=1.0)
                        ss.tag(ins)
                        ins = s.activation(out=uwb[b][:], in_=ust[b][:], func=AF.Copy,
                                           bias=-128.0, scale=1.0)
                        ss.tag(ins)

                    deq_hist[ic] = ss.v

                f1_load(0)
                f1_load(1)
                f1_deq(0)
                for ic in range(NI):
                    b = ic % 2
                    for half in range(2):
                        t0h = half * 2048

                        @block.tensor
                        def _(t, ic=ic, b=b, half=half, t0h=t0h, dq=deq_hist[ic],
                              sl=silu_hist.get((ic - 1, 1) if half == 0 else (ic, 0), 0)):
                            if half == 0:
                                t.wait_ge(ss.h, dq)
                            prevg = silu_hist.get((ic, half - 1), silu_hist.get((ic - 1, 1), 0))
                            t.wait_ge(ss.h, prevg)
                            for d in range(ND):
                                for bk in range(4):
                                    ins = t.matmul(
                                        ps_g[:, bk * 512:(bk + 1) * 512],
                                        lhsT=gwb[b][:, d * 128:(d + 1) * 128],
                                        rhs=xfm2[:, d * T + t0h + bk * 512:
                                                 d * T + t0h + (bk + 1) * 512],
                                        start=(d == 0), stop=(d == ND - 1))
                            stn.tag(ins, 1)

                        gmm_hist[(ic, half)] = stn.v

                        @scalar_section
                        def _(s, ic=ic, b=b, half=half, gmm=gmm_hist[(ic, half)],
                              scv=16 * (ic // 2 + 1) * 2,
                              sgprev=hstt_hist.get((ic - 1, half), 0)):
                            s.wait_ge(stn.h, gmm)
                            if half == 0:
                                s.wait_ge(scl[b].h, scv)
                            if ic >= 1:
                                s.wait_ge(sv.h, sgprev)
                            ins = s.activation(out=sgb[half][:], in_=ps_g[:], func=AF.Silu,
                                               bias=zero_t[:], scale=gsc[b][:])
                            ss.tag(ins)

                        silu_hist[(ic, half)] = ss.v

                        @block.tensor
                        def _(t, ic=ic, b=b, half=half, t0h=t0h,
                              prevu=hstt_hist.get((ic, half - 1),
                                                  hstt_hist.get((ic - 1, 1), 0))):
                            t.wait_ge(sv.h, prevu)
                            for d in range(ND):
                                for bk in range(4):
                                    ins = t.matmul(
                                        ps_u[:, bk * 512:(bk + 1) * 512],
                                        lhsT=uwb[b][:, d * 128:(d + 1) * 128],
                                        rhs=xfm2[:, d * T + t0h + bk * 512:
                                                 d * T + t0h + (bk + 1) * 512],
                                        start=(d == 0), stop=(d == ND - 1))
                            stn.tag(ins, 1)

                        umm_hist[(ic, half)] = stn.v

                        @block.vector
                        def _(v, ic=ic, b=b, half=half, umm=umm_hist[(ic, half)],
                              silu_v=silu_hist[(ic, half)],
                              hprev=hst[half].v):
                            v.wait_ge(stn.h, umm)
                            v.wait_ge(ss.h, silu_v)
                            if ic >= 1:
                                v.wait_ge(hst[half].h, hprev)
                            ins = v.scalar_tensor_tensor(out=hbf[half][:], in0=ps_u[:],
                                                         scalar=usc[b][:], in1=sgb[half][:],
                                                         op0=ALU.mult, op1=ALU.mult)
                            sv.tag(ins)

                        hstt_hist[(ic, half)] = sv.v

                        @block.sync
                        def _(q, ic=ic, half=half, t0h=t0h, hv=hstt_hist[(ic, half)]):
                            q.wait_ge(sv.h, hv)
                            ins = q.dma_start(out=h_h[ic][:, t0h:t0h + 2048],
                                              in_=hbf[half][:])
                            hst[half].tag(ins, 16)

                        if half == 0 and ic + 1 < NI:
                            f1_deq(ic + 1)
                        if half == 1 and ic + 2 < NI:
                            f1_load(ic + 2)

                f1a, f1b = hst[0].v, hst[1].v

                def wait_f1_done(eng):
                    eng.wait_ge(hst[0].h, f1a)
                    eng.wait_ge(hst[1].h, f1b)
        else:
            wait_f1_done = wait_f0_done

        if nphases >= 5:
            xfm2_cm.__exit__(None, None, None)

        if nphases >= 7:
            HGRP = [(0, 6), (6, 12), (12, 16)]
            with ExitStack() as px:
                xe = px.enter_context
                hb = [xe(nc.sbuf_tensor(f"f2h{i}", [128, NI * 512], BF16)) for i in range(2)]
                wdt = [xe(nc.sbuf_tensor(f"f2w{i}", [128, 768], BF16)) for i in range(8)]
                dob = [xe(nc.sbuf_tensor(f"f2d{i}", [128, ND * 512], BF16)) for i in range(2)]
                dsc = xe(nc.sbuf_tensor("f2ds", [128, ND], F32))
                dtm = [xe(nc.sbuf_tensor(f"f2tm{i}", [128, D], BF16)) for i in range(2)]
                dnr = xe(nc.sbuf_tensor("f2dn", [128, D], F32))
                xt6 = [xe(nc.sbuf_tensor(f"f2x{i}", [128, D], F32)) for i in range(2)]
                st6 = xe(nc.sbuf_tensor("f2st", [128, 24], F32))
                mv6 = xe(nc.sbuf_tensor("f2mv", [128, 2], F32))
                rs6 = xe(nc.sbuf_tensor("f2rs", [128, 1], F32))
                nm6 = xe(nc.sbuf_tensor("f2nm", [128, 1], F32))
                lt6 = xe(nc.sbuf_tensor("f2lt", [128, 1], F32))
                ps_d = xe(nc.psum_tensor("f2pd", [128, 6 * 512], F32))
                ps_t = xe(nc.psum_tensor("f2pt", [128, 2048], BF16))

                hbl = [SemC(nc, ctx, f"f2hl{i}") for i in range(2)]
                wdl = [SemC(nc, ctx, f"f2wl{i}") for i in range(8)]
                dsl = SemC(nc, ctx, "f2dsl")
                xl6 = [SemC(nc, ctx, f"f2xl{i}") for i in range(2)]
                ots = [SemC(nc, ctx, f"f2os{i}") for i in range(2)]

                @block.sync
                def _(q):
                    wait_f1_done(q)
                    for c in range(ND):
                        ins = q.dma_start(
                            out=dsc[:, c:c + 1],
                            in_=ds_h[c * 128:(c + 1) * 128].rearrange("(p u) -> p u", u=1))
                        dsl.tag(ins, 16)

                ds_ready = dsl.v
                wd_hist = {}
                dev_hist = {}
                grp_end = {}
                trev_hist = {}
                dtmrd_hist = {}
                NB = 8
                tr_tasks = []

                def emit_group(tb, gi, hbslot):
                    h0, h1 = HGRP[gi]
                    nhc = h1 - h0
                    for ic in range(NI):
                        git = (tb * 3 + gi) * NI + ic
                        ws = git % 8
                        qeng = block.sync if ic % 2 == 0 else block.gpsimd

                        @qeng
                        def _(q, gi=gi, ic=ic, ws=ws, git=git, h0=h0, h1=h1):
                            if git >= 8:
                                q.wait_ge(stn.h, wd_hist[git - 8])
                            src = wdb_h[ic][:, h0 * 128:h1 * 128]
                            ins = q.dma_start(out=wdt[ws][:, 0:(h1 - h0) * 128],
                                              in_=src)
                            wdl[ws].tag(ins, 16)

                        @block.tensor
                        def _(t, tb=tb, gi=gi, ic=ic, ws=ws, git=git, nhc=nhc,
                              hbslot=hbslot,
                              evprev=dev_hist.get((tb, gi - 1) if gi else
                                                  (tb - 1, 2), 0)):
                            t.wait_ge(wdl[ws].h, 16 * (git // 8 + 1))
                            if ic == 0:
                                t.wait_ge(hbl[hbslot].h, 16 * (tb // 2 + 1))
                                if evprev:
                                    t.wait_ge(sv.h, evprev)
                            for u in range(nhc):
                                ins = t.matmul(
                                    ps_d[:, u * 512:(u + 1) * 512],
                                    lhsT=wdt[ws][:, u * 128:(u + 1) * 128],
                                    rhs=hb[hbslot][:, ic * 512:(ic + 1) * 512],
                                    start=(ic == 0), stop=(ic == NI - 1))
                            stn.tag(ins, 1)

                        wd_hist[git] = stn.v

                    mm_end = stn.v

                    @block.vector
                    def _(v, tb=tb, gi=gi, h0=h0, nhc=nhc, mm_end=mm_end,
                          ds_ready=ds_ready):
                        v.wait_ge(stn.h, mm_end)
                        if tb == 0 and gi == 0:
                            v.wait_ge(dsl.h, ds_ready)
                        for u in range(nhc):
                            ins = v.tensor_scalar(
                                out=dob[tb % 2][:, (h0 + u) * 512:(h0 + u + 1) * 512],
                                in0=ps_d[:, u * 512:(u + 1) * 512],
                                scalar1=dsc[:, h0 + u:h0 + u + 1], scalar2=None,
                                op0=ALU.mult, op1=ALU.bypass)
                        sv.tag(ins)

                    dev_hist[(tb, gi)] = sv.v

                def make_tr(tb, tc):
                    def run():
                        gt = tb * 4 + tc
                        ob = gt % 2

                        @block.sync
                        def _(q, gt=gt, ob=ob):
                            if gt >= 2:
                                q.wait_ge(ots[ob].h, 16 * (gt // 2))
                            ins = q.dma_start(out=xt6[ob][:], in_=x1_h[gt])
                            xl6[ob].tag(ins, 16)

                        @block.tensor
                        def _(t, tb=tb, tc=tc, devs=dev_hist[(tb, 2)],
                              trprev=trev_hist.get((tb, tc - 1) if tc >= 1 else
                                                   (tb - 1, 3), 0)):
                            t.wait_ge(sv.h, devs)
                            if trprev:
                                t.wait_ge(ss.h, trprev)
                            for hc in range(ND):
                                ins = t.transpose(
                                    ps_t[:, hc * 128:(hc + 1) * 128],
                                    dob[tb % 2][:, hc * 512 + tc * 128:
                                                hc * 512 + (tc + 1) * 128],
                                    ident_b[:])
                            stn.tag(ins, 1)

                        trmm = stn.v

                        @scalar_section
                        def _(s, tb=tb, tc=tc, trmm=trmm,
                              dtmprev=dtmrd_hist.get((tb, tc - 2) if tc >= 2 else
                                                     (tb - 1, tc + 2), 0)):
                            s.wait_ge(stn.h, trmm)
                            if dtmprev:
                                s.wait_ge(sv.h, dtmprev)
                            ins = s.copy(out=dtm[tc % 2][:], in_=ps_t[:])
                            ss.tag(ins)

                        trev_hist[(tb, tc)] = ss.v
                        dtm_ready = ss.v

                        @block.vector
                        def _(v, tc=tc, dtm_ready=dtm_ready):
                            v.wait_ge(ss.h, dtm_ready)
                            for c in range(4):
                                ins = v.bn_stats(out=st6[:, c * 6:(c + 1) * 6],
                                                 in_=dtm[tc % 2][:, c * 512:(c + 1) * 512])
                            sv.tag(ins)
                            v.wait_ge(sv.h, sv.v)
                            ins = v.bn_aggr(out=mv6[:], in_=st6[:])
                            sv.tag(ins)

                        mv_ready6 = sv.v
                        dtmrd_hist[(tb, tc)] = mv_ready6

                        @scalar_section
                        def _(s, mv_ready6=mv_ready6):
                            s.wait_ge(sv.h, mv_ready6)
                            ins = s.activation(out=lt6[:], in_=mv6[:, 1:2], func=AF.Ln,
                                               bias=eps_t[:], scale=1.0)
                            ss.tag(ins)
                            s.wait_ge(ss.h, ss.v)
                            ins = s.activation(out=rs6[:], in_=lt6[:], func=AF.Exp,
                                               bias=zero_t[:], scale=-0.5)
                            ss.tag(ins)

                        rstd_ready6 = ss.v

                        @block.vector
                        def _(v, rstd_ready6=rstd_ready6):
                            v.wait_ge(ss.h, rstd_ready6)
                            ins = v.scalar_tensor_tensor(out=nm6[:], in0=mv6[:, 0:1],
                                                         scalar=-1.0, in1=rs6[:],
                                                         op0=ALU.mult, op1=ALU.mult)
                            sv.tag(ins)

                        nmr_ready6 = sv.v

                        @scalar_section
                        def _(s, tc=tc, nmr_ready6=nmr_ready6):
                            s.wait_ge(sv.h, nmr_ready6)
                            ins = s.activation(out=dnr[:], in_=dtm[tc % 2][:],
                                               func=AF.Identity, bias=nm6[:], scale=rs6[:])
                            ss.tag(ins)

                        ln3_ready = ss.v

                        @block.vector
                        def _(v, gt=gt, ob=ob, ln3_ready=ln3_ready):
                            v.wait_ge(ss.h, ln3_ready)
                            v.wait_ge(xl6[ob].h, 16 * (gt // 2 + 1))
                            ins = v.tensor_tensor(out=xt6[ob][:], in0=xt6[ob][:],
                                                  in1=dnr[:], op=ALU.add)
                            sv.tag(ins)

                        out_ready = sv.v

                        @block.sync
                        def _(q, gt=gt, ob=ob, out_ready=out_ready):
                            q.wait_ge(sv.h, out_ready)
                            ins = q.dma_start(out=out_h[gt * 128:(gt + 1) * 128, :],
                                              in_=xt6[ob][:])
                            ots[ob].tag(ins, 16)
                    return run

                @block.gpsimd
                def _(g):
                    wait_f1_done(g)

                for tb in range(NB):
                    hbslot = tb % 2

                    @block.gpsimd
                    def _(q, tb=tb, hbslot=hbslot, prev=grp_end.get(tb - 2, 0)):
                        if prev:
                            q.wait_ge(stn.h, prev)
                        src = h_h.rearrange("c p t -> p c t")[:, :, tb * 512:(tb + 1) * 512]
                        ins = q.dma_start(out=hb[hbslot][:].rearrange(
                            "p (c t) -> p c t", t=512), in_=src)
                        hbl[hbslot].tag(ins, 16)

                    for gi in range(3):
                        emit_group(tb, gi, hbslot)
                        if tr_tasks:
                            tr_tasks.pop(0)()
                        if gi == 2 and tr_tasks:
                            tr_tasks.pop(0)()
                    grp_end[tb] = stn.v
                    for tc in range(4):
                        tr_tasks.append(make_tr(tb, tc))

                while tr_tasks:
                    tr_tasks.pop(0)()

                f2a, f2b = ots[0].v, ots[1].v

                def wait_f2_done(eng):
                    eng.wait_ge(ots[0].h, f2a)
                    eng.wait_ge(ots[1].h, f2b)
        else:
            wait_f2_done = wait_f1_done


        _scr.flush()

        @block.sync
        def _(q):
            wait_f2_done(q)
            q.wait_ge(sq.h, sq.v)

    return nc



def _prep_inputs(
    L0_attn_query, L0_attn_w_q, L0_attn_b_q, L0_attn_w_k, L0_attn_b_k,
    L0_attn_w_v, L0_attn_b_v, L0_attn_w_o, L0_attn_b_o,
    L0_ffn_gate_w, L0_ffn_gate_scale, L0_ffn_up_w, L0_ffn_up_scale,
    L0_ffn_down_w, L0_ffn_down_scale,
):
    base = {
        "wq": np.ascontiguousarray(np.asarray(L0_attn_w_q, dtype=np.float32)),
        "wk": np.ascontiguousarray(np.asarray(L0_attn_w_k, dtype=np.float32)),
        "wv": np.ascontiguousarray(np.asarray(L0_attn_w_v, dtype=np.float32)),
        "wo": np.ascontiguousarray(np.asarray(L0_attn_w_o, dtype=np.float32)),
        "bq": np.ascontiguousarray(np.asarray(L0_attn_b_q, dtype=np.float32)),
        "bv": np.ascontiguousarray(np.asarray(L0_attn_b_v, dtype=np.float32)),
        "bo": np.ascontiguousarray(np.asarray(L0_attn_b_o, dtype=np.float32)),
        "wgT": np.ascontiguousarray(np.asarray(L0_ffn_gate_w).T),
        "wuT": np.ascontiguousarray(np.asarray(L0_ffn_up_w).T),
        "wdT": np.ascontiguousarray(np.asarray(L0_ffn_down_w).T),
        "gs": np.ascontiguousarray(np.asarray(L0_ffn_gate_scale, dtype=np.float32)),
        "us": np.ascontiguousarray(np.asarray(L0_ffn_up_scale, dtype=np.float32)),
        "ds": np.ascontiguousarray(np.asarray(L0_ffn_down_scale, dtype=np.float32)),
    }
    xq = np.asarray(L0_attn_query)
    return [dict(base, x=np.ascontiguousarray(xq[b])) for b in range(8)]


def kernel(**inputs):
    in_maps = _prep_inputs(**inputs)
    nc = build()
    res = run_bass_kernel_spmd(nc, in_maps, core_ids=list(range(8)))
    return np.stack([np.asarray(r["out"]) for r in res.results], axis=0)
